# version 53
# baseline (speedup 1.0000x reference)
import numpy as np
import ml_dtypes
from contextlib import ExitStack

import concourse.bass as bass
import concourse.mybir as mybir
from concourse.bass_utils import run_bass_kernel_spmd

F32 = mybir.dt.float32
BF16 = mybir.dt.bfloat16
AF = mybir.ActivationFunctionType
ALU = mybir.AluOpType
AX = mybir.AxisListType

NCORES = 8
B, T, D = 16, 2048, 1024
SEQ_PER_CORE = B // NCORES
NTOK = SEQ_PER_CORE * T
NJ = D // 128
CHUNK = 512
NCH = NTOK // CHUNK
H = 64
E = 8
I2 = 2048
NI = I2 // 128
NBF = 266
NHEADS = 16
DH = 64
LN_EPS = 1e-5
RMS_EPS = float(np.finfo(np.float32).eps)
PERF_EPS = 1e-4
PERF_RATIO = float(NBF) ** -0.5
PERF_DN = float(DH) ** -0.25

BF = ml_dtypes.bfloat16



def _pcol(v):
    return np.ascontiguousarray(v.reshape(NJ, 128).T)


def stage_inputs(inputs):
    x = inputs["L0_attn_x"]
    g = {}
    g["m_w1t"] = np.ascontiguousarray(inputs["L0_attn_W1"].T)
    g["m_w2t"] = np.ascontiguousarray(inputs["L0_attn_W2"].T)
    g["m_wlt"] = np.ascontiguousarray(inputs["L0_attn_Wlast"].T)
    g["m_dmt"] = np.ascontiguousarray(inputs["L0_attn_Dw"].T)
    g["m_dwt"] = np.ascontiguousarray(inputs["L0_attn_dw"].T)
    g["m_bwt"] = np.ascontiguousarray(inputs["L0_attn_Bw"].T)
    g["m_cwt"] = np.ascontiguousarray(inputs["L0_attn_Cw"].T)
    g["m_normw"] = _pcol(inputs["L0_attn_norm_w"])
    g["m_convw"] = np.ascontiguousarray(
        inputs["L0_attn_conv_w"].reshape(NJ, 128, 3).transpose(1, 0, 2).reshape(128, NJ * 3))
    g["m_convb"] = _pcol(inputs["L0_attn_conv_b"])
    g["m_A"] = np.ascontiguousarray(inputs["L0_attn_A"].reshape(H, 1))
    g["m_db"] = np.ascontiguousarray(inputs["L0_attn_db"].reshape(H, 1))
    g["m_Bb"] = np.ascontiguousarray(inputs["L0_attn_Bb"].reshape(H, 1))
    g["m_Cb"] = _pcol(inputs["L0_attn_Cb"])
    g["m_Db"] = _pcol(inputs["L0_attn_Db"])
    g["f_gwt"] = np.ascontiguousarray(inputs["L0_ffn_gate_w"].T)
    g["f_w1t"] = np.ascontiguousarray(inputs["L0_ffn_w1"].transpose(0, 2, 1))
    g["f_w3t"] = np.ascontiguousarray(inputs["L0_ffn_w3"].transpose(0, 2, 1))
    g["f_w2t"] = np.ascontiguousarray(inputs["L0_ffn_w2"].transpose(0, 2, 1))
    g["f_b1"] = np.ascontiguousarray(inputs["L0_ffn_b1"].reshape(E, NJ, 128).transpose(2, 0, 1).reshape(128, E * NJ))
    g["f_b2"] = np.ascontiguousarray(inputs["L0_ffn_b2"].reshape(E, NJ, 128).transpose(2, 0, 1).reshape(128, E * NJ))
    g["f_b3"] = np.ascontiguousarray(inputs["L0_ffn_b3"].reshape(E, NJ, 128).transpose(2, 0, 1).reshape(128, E * NJ))
    g["f_sw1t"] = np.ascontiguousarray(inputs["L0_ffn_sw1"].T)
    g["f_sw2t"] = np.ascontiguousarray(inputs["L0_ffn_sw2"].T)
    g["f_sw3t"] = np.ascontiguousarray(inputs["L0_ffn_sw3"].T)
    g["a_wq"] = np.ascontiguousarray(inputs["L1_attn_Wq"])
    g["a_wk"] = np.ascontiguousarray(inputs["L1_attn_Wk"])
    g["a_wv"] = np.ascontiguousarray(inputs["L1_attn_Wv"])
    g["a_wo"] = np.ascontiguousarray(inputs["L1_attn_Wo"])
    g["a_bo"] = _pcol(inputs["L1_attn_bo"])
    g["a_projt"] = np.ascontiguousarray(inputs["L1_attn_proj"].T)
    g["r_wrt"] = np.ascontiguousarray(inputs["L1_ffn_W_r"].T)
    g["r_wkt"] = np.ascontiguousarray(inputs["L1_ffn_W_k"].T)
    g["r_wvt"] = np.ascontiguousarray(inputs["L1_ffn_W_v"].T)
    g["r_mur"] = _pcol(inputs["L1_ffn_mu_r"])
    g["r_muk"] = _pcol(inputs["L1_ffn_mu_k"])

    maps = []
    for c in range(NCORES):
        m = dict(g)
        shard = x[c * SEQ_PER_CORE:(c + 1) * SEQ_PER_CORE].reshape(NTOK, D)
        m["x0f"] = np.ascontiguousarray(shard.T)
        maps.append(m)
    return maps


def assemble_output(results):
    outs = [r["out_fm"] for r in results]
    full = np.concatenate([o.T.reshape(SEQ_PER_CORE, T, D) for o in outs], axis=0)
    return np.ascontiguousarray(full.astype(np.float32))



def _key(ap):
    try:
        return ap.tensor.name
    except AttributeError:
        return None


class Prog:
    K_FENCE = 4

    def __init__(self, nc, stack):
        self.nc = nc
        self.stack = stack
        self._sems = {}
        self.writer = {}
        self.readers = {}
        self.waited = {}
        self.engines = {}

    def s(self, name):
        h = self._sems.get(name)
        if h is None:
            assert len(self._sems) < 100, "semaphore budget exceeded"
            h = [self.stack.enter_context(self.nc.semaphore(name)), 0]
            self._sems[name] = h
        return h

    def ensure_tok(self, rec):
        if rec["tok"] is not None:
            return rec["sem"], rec["tok"]
        eng = self.engines.get(rec["eng"])
        if eng is not None and eng.use_milestones:
            import bisect
            seqs = [m[0] for m in eng.milestones]
            i = bisect.bisect_left(seqs, rec["seq"])
            if i < len(eng.milestones):
                rec["tok"] = eng.milestones[i][1]
                return rec["sem"], rec["tok"]
            h = self.s(rec["sem"])
            eng.last_instr.then_inc(h[0], rec["step"])
            h[1] += rec["step"]
            eng.milestones.append((eng.seq, h[1]))
            rec["tok"] = h[1]
            return rec["sem"], rec["tok"]
        h = self.s(rec["sem"])
        rec["instr"].then_inc(h[0], rec["step"])
        h[1] += rec["step"]
        rec["tok"] = h[1]
        return rec["sem"], rec["tok"]

    def raw_wait(self, tag, raw_eng, sem, tok):
        w = self.waited.setdefault(tag, {})
        if w.get(sem, 0) < tok:
            raw_eng.wait_ge(self.s(sem)[0], tok)
            w[sem] = tok

    def stage_barrier(self):
        nc = self.nc
        for name, h in sorted(self._sems.items()):
            if name.startswith("D:"):
                self.raw_wait("G", nc.gpsimd, name, h[1])
        nc.all_engine_barrier()
        self.writer.clear()
        self.readers.clear()
        for tag in list(self.engines):
            w = self.waited.setdefault(tag, {})
            for name, h in self._sems.items():
                w[name] = h[1]


class Eng:
    def __init__(self, P, raw, tag, sem, step=1, use_milestones=False):
        self.P = P
        self.raw = raw
        self.tag = tag
        self.sem = sem
        self.step = step
        self.seq = 0
        self.use_milestones = use_milestones
        self.milestones = []
        self.last_instr = None
        P.engines[tag] = self

    def _edge(self, rec):
        if rec.get("dma"):
            sem = rec["sem"]
            self.P.raw_wait(self.tag, self.raw, sem, self.P.s(sem)[1])
        else:
            sem, tok = self.P.ensure_tok(rec)
            self.P.raw_wait(self.tag, self.raw, sem, tok)

    def _need(self, rec):
        if rec["eng"] != self.tag:
            return True
        if rec.get("dma") or self.step == 16:
            return True
        if self.tag == "T":
            return False
        return self.seq - rec["seq"] <= self.P.K_FENCE

    def _pre(self, reads, writes):
        P = self.P
        for k in reads:
            if k is None:
                continue
            rec = P.writer.get(k)
            if rec is not None and self._need(rec):
                self._edge(rec)
        for k in writes:
            if k is None:
                continue
            rec = P.writer.get(k)
            if rec is not None and self._need(rec):
                self._edge(rec)
            for rdr in P.readers.get(k, ()):
                if self._need(rdr):
                    self._edge(rdr)

    def _post(self, ins, reads, writes, sem=None, dma=False):
        P = self.P
        self.seq += 1
        rec = {"instr": ins, "eng": self.tag, "sem": sem or self.sem, "step": self.step,
               "tok": None, "seq": self.seq, "dma": dma}
        self.last_instr = ins
        if not self.use_milestones:
            P.ensure_tok(rec)
        for k in writes:
            if k is None:
                continue
            P.writer[k] = rec
            P.readers[k] = []
        for k in reads:
            if k is None or k in writes:
                continue
            P.readers.setdefault(k, []).append(rec)
        return rec

    def emit(self, fn, reads, writes, sem=None, dma=False):
        rkeys = [_key(a) for a in reads]
        wkeys = [_key(a) for a in writes]
        self._pre(rkeys, wkeys)
        ins = fn()
        return self._post(ins, rkeys, wkeys, sem=sem, dma=dma)

    def tokenize(self, rec):
        return self.P.ensure_tok(rec)

    def tt(self, out, in0, in1, op):
        return self.emit(lambda: self.raw.tensor_tensor(out=out, in0=in0, in1=in1, op=op), [in0, in1], [out])

    def ts(self, out, in0, s1, s2=None, op0=ALU.mult, op1=ALU.bypass, accum=None):
        outs = [out] + ([accum] if accum is not None else [])
        return self.emit(lambda: self.raw.tensor_scalar(
            out=out, in0=in0, scalar1=s1, scalar2=s2, op0=op0, op1=op1,
            accum_out=accum), [in0, s1, s2], outs)

    def stt(self, out, in0, scalar, in1, op0, op1, accum=None):
        outs = [out] + ([accum] if accum is not None else [])
        return self.emit(lambda: self.raw.scalar_tensor_tensor(
            out=out, in0=in0, scalar=scalar, in1=in1, op0=op0, op1=op1, accum_out=accum),
            [in0, in1, scalar], outs)

    def copy(self, out, in_):
        return self.emit(lambda: self.raw.tensor_copy(out=out, in_=in_), [in_], [out])

    def memset(self, ap, v):
        return self.emit(lambda: self.raw.memset(ap, v), [], [ap])

    def scan(self, out, d0, d1, init, op0, op1):
        return self.emit(lambda: self.raw.tensor_tensor_scan(
            out=out, data0=d0, data1=d1, initial=init, op0=op0, op1=op1),
            [d0, d1, init if not isinstance(init, float) else None], [out])

    def reduce(self, out, in_, op, negate=False):
        return self.emit(lambda: self.raw.tensor_reduce(
            out=out, in_=in_, axis=AX.X, op=op, negate=negate), [in_], [out])

    def recip(self, out, in_):
        return self.emit(lambda: self.raw.reciprocal(out=out, in_=in_), [in_], [out])

    def select(self, out, mask, a, b):
        return self.emit(lambda: self.raw.select(out=out, mask=mask, on_true=a, on_false=b), [mask, a, b], [out])

    def act(self, out, in_, func, bias=0.0, scale=1.0, accum=None):
        outs = [out] + ([accum] if accum is not None else [])
        return self.emit(lambda: self.raw.activation(
            out=out, in_=in_, func=func, bias=bias, scale=scale, accum_out=accum),
            [in_, bias, scale], outs)

    def mm(self, out, lhsT, rhs, start, stop):
        return self.emit(lambda: self.raw.matmul(out, lhsT, rhs, start=start, stop=stop),
                         [lhsT, rhs], [out])

    def transpose(self, out, in_, ident):
        return self.emit(lambda: self.raw.transpose(out, in_, ident), [in_, ident], [out])

    def dma(self, out, in_, semname=None, accum=False, disjoint=False, unordered=False):
        sem = semname or ("D:" + _key(out))
        if accum:
            fn = lambda: self.raw.dma_start(out=out, in_=in_, accum_op=ALU.add)
        else:
            fn = lambda: self.raw.dma_start(out=out, in_=in_)
        reads = [in_] if (accum and unordered) else ([in_, out] if accum else [in_])
        if disjoint or unordered:
            rkeys = [_key(a) for a in reads]
            self._pre(rkeys, [])
            ins = fn()
            rec = self._post(ins, rkeys, [_key(out)], sem=sem, dma=True)
        else:
            rec = self.emit(fn, reads, [out], sem=sem, dma=True)
        self.P.ensure_tok(rec)
        return rec

    def nop_rec(self):
        return self.emit(lambda: self.raw.nop(), [], [])


def make_engs(P):
    nc = P.nc
    return {
        "V": Eng(P, nc.vector, "V", "V", use_milestones=True),
        "A": Eng(P, nc.scalar, "A", "A", use_milestones=True),
        "T": Eng(P, nc.tensor, "T", "P", use_milestones=True),
        "L": Eng(P, nc.sync, "L", "L", step=16),
        "G": Eng(P, nc.gpsimd, "G", "G", step=16),
        "Q": Eng(P, nc.gpsimd, "Q", "Q"),
    }


def dram_io(nc, name, shape, dtype, debug):
    kind = "ExternalOutput" if debug else "Internal"
    return nc.dram_tensor(name, shape, dtype, kind=kind).ap()



def load_lhsT_weight(P, G, st, name, hbm, KK, MM):
    nc = P.nc
    njk = KK // 128
    sb = st.enter_context(nc.sbuf_tensor(name, [128, njk * MM], BF16))
    for j in range(njk):
        G.dma(sb[:, j * MM:(j + 1) * MM], hbm[j * 128:(j + 1) * 128, :])

    def sl(j, k, w=128):
        return sb[:, j * MM + k * w: j * MM + k * w + w]
    return sb, sl


def load_f32_tile(P, L, st, name, hbm, shape):
    sb = st.enter_context(P.nc.sbuf_tensor(name, list(shape), F32))
    L.dma(sb[:, :], hbm[:, :])
    return sb



def emit_ln(P, engs, st, src_hbm, dst_hbm, dst_b=None, gw=None, sm_all=None):
    nc = P.nc
    V, A_, Te, L, G = engs["V"], engs["A"], engs["T"], engs["L"], engs["G"]
    if gw is not None:
        gw_f = st.enter_context(nc.sbuf_tensor("ln_gw_f", [128, NJ * E], F32))
        for j in range(NJ):
            L.dma(gw_f[:, j * E:(j + 1) * E], gw[j * 128:(j + 1) * 128, :])
        psG = st.enter_context(nc.psum_tensor("ln_psG", [8, CHUNK], F32))
        psT = st.enter_context(nc.psum_tensor("ln_psT", [128, 8], F32))
        psTb = st.enter_context(nc.psum_tensor("ln_psTb", [8, 128], F32))
        identg = st.enter_context(nc.sbuf_tensor("ln_identg", [128, 128], F32))
        L.dma(identg[:, :], nc.inline_tensor(np.eye(128, dtype=np.float32), "ln_identg_d").ap()[:, :])
        lg_loc = [st.enter_context(nc.sbuf_tensor(f"ln_lg{r}", [8, CHUNK], F32)) for r in range(2)]
        ge_f = st.enter_context(nc.sbuf_tensor("ln_ge", [128, 8], F32))
        gsc = st.enter_context(nc.sbuf_tensor("ln_gsc", [128, 8], F32))
        gscm = st.enter_context(nc.sbuf_tensor("ln_gscm", [128, 8], F32))
        gmask = st.enter_context(nc.sbuf_tensor("ln_gmask", [128, 8], F32))
        gsm = st.enter_context(nc.sbuf_tensor("ln_gsm", [128, 8], F32))
        gca = st.enter_context(nc.sbuf_tensor("ln_gca", [128, 1], F32))
        gcb = st.enter_context(nc.sbuf_tensor("ln_gcb", [128, 1], F32))
        gcc = st.enter_context(nc.sbuf_tensor("ln_gcc", [128, 1], F32))
    xin_w = [st.enter_context(nc.sbuf_tensor(f"ln_xin{r}", [128, NJ * CHUNK], F32)) for r in range(2)]
    xnf_w = [st.enter_context(nc.sbuf_tensor(f"ln_xnf{r}", [128, NJ * CHUNK], F32)) for r in range(2)]
    xin = [[xin_w[r][:, j * CHUNK:(j + 1) * CHUNK] for j in range(NJ)] for r in range(2)]
    xnf = [[xnf_w[r][:, j * CHUNK:(j + 1) * CHUNK] for j in range(NJ)] for r in range(2)]
    xb16 = [st.enter_context(nc.sbuf_tensor(f"ln_xb16_{j}", [128, CHUNK], BF16)) for j in range(NJ)]
    xsq = [st.enter_context(nc.sbuf_tensor(f"ln_xsq_{j}", [128, CHUNK], BF16)) for j in range(NJ)]
    ones_bf = st.enter_context(nc.sbuf_tensor("ln_ones", [128, 1], BF16))
    ones_row = st.enter_context(nc.sbuf_tensor("ln_ones_row", [1, 128], F32))
    ps_sum = st.enter_context(nc.psum_tensor("ln_ps_sum", [1, CHUNK], F32))
    ps_sq = st.enter_context(nc.psum_tensor("ln_ps_sq", [1, CHUNK], F32))
    ps_r = st.enter_context(nc.psum_tensor("ln_ps_r", [128, CHUNK], F32))
    ps_mr = st.enter_context(nc.psum_tensor("ln_ps_mr", [128, CHUNK], F32))
    rows = [st.enter_context(nc.sbuf_tensor(f"ln_row{i}", [1, CHUNK], F32)) for i in range(4)]
    m_row, v_row, r_row, mr_row = rows
    V.memset(ones_bf[:, :], 1.0)
    V.memset(ones_row[:, :], 1.0)
    for ch in range(NCH):
        sl = ch % 2
        c0 = ch * CHUNK
        for j in range(NJ):
            L.dma(xin[sl][j], src_hbm[j * 128:(j + 1) * 128, c0:c0 + CHUNK])
        for j in range(NJ):
            A_.act(xb16[j][:], xin[sl][j], AF.Copy)
            A_.act(xsq[j][:], xb16[j][:], AF.Square)
        for j in range(NJ):
            Te.mm(ps_sum[0:1, :], ones_bf[:, 0:1], xb16[j][:], start=(j == 0), stop=(j == NJ - 1))
        for j in range(NJ):
            Te.mm(ps_sq[0:1, :], ones_bf[:, 0:1], xsq[j][:], start=(j == 0), stop=(j == NJ - 1))
        V.ts(m_row[:], ps_sum[0:1, :], 1.0 / D)
        V.ts(v_row[:], ps_sq[0:1, :], 1.0 / D, LN_EPS, ALU.mult, ALU.add)
        V.tt(mr_row[:], m_row[:], m_row[:], ALU.mult)
        V.tt(v_row[:], v_row[:], mr_row[:], ALU.subtract)
        A_.act(r_row[:], v_row[:], AF.Ln)
        A_.act(r_row[:], r_row[:], AF.Exp, scale=-0.5)
        V.tt(mr_row[:], m_row[:], r_row[:], ALU.mult)
        Te.mm(ps_r[:, :], ones_row[:, :], r_row[:], start=True, stop=True)
        Te.mm(ps_mr[:, :], ones_row[:, :], mr_row[:], start=True, stop=True)
        for j in range(NJ):
            V.tt(xnf[sl][j], xin[sl][j], ps_r[:, :], ALU.mult)
            V.tt(xnf[sl][j], xnf[sl][j], ps_mr[:, :], ALU.subtract)
        for j in range(NJ):
            G.dma(dst_hbm[j * 128:(j + 1) * 128, c0:c0 + CHUNK], xnf[sl][j], disjoint=True)
            if dst_b is not None:
                G.dma(dst_b[j * 128:(j + 1) * 128, c0:c0 + CHUNK], xnf[sl][j], disjoint=True)
        if gw is not None:
            for j in range(NJ):
                Te.mm(psG[0:8, :], gw_f[:, j * E:(j + 1) * E], xnf[sl][j], start=(j == 0), stop=(j == NJ - 1))
            A_.act(lg_loc[sl][:, :], psG[0:8, :], AF.Copy)
            for i in range(CHUNK // 128):
                Te.transpose(psT[:, 0:8], lg_loc[sl][:, i * 128:(i + 1) * 128], identg[0:8, 0:8])
                V.reduce(gca[:, 0:1], psT[:, 0:8], ALU.max, negate=True)
                A_.act(ge_f[:, :], psT[:, 0:8], AF.Exp, bias=gca[:, 0:1], accum=gcb[:, 0:1])
                V.recip(gcc[:, 0:1], gcb[:, 0:1])
                V.ts(gsc[:, :], ge_f[:, :], gcc[:, 0:1], None, ALU.mult)
                V.reduce(gca[:, 0:1], gsc[:, :], ALU.max)
                V.ts(gmask[:, :], gsc[:, :], gca[:, 0:1], None, ALU.is_ge)
                V.stt(gscm[:, :], gmask[:, :], -1e30, gsc[:, :], ALU.mult, ALU.add)
                V.reduce(gcb[:, 0:1], gscm[:, :], ALU.max)
                V.ts(gmask[:, :], gsc[:, :], gcb[:, 0:1], None, ALU.is_ge)
                V.tt(gsm[:, :], gsc[:, :], gmask[:, :], ALU.mult)
                Te.transpose(psTb[0:8, :], gsm[:, :], identg[:, :])
                V.copy(sm_all[:, c0 + i * 128:c0 + (i + 1) * 128], psTb[0:8, :])



def emit_mamba(P, engs, st, hin, XN0, MAM, dbg=None):
    nc = P.nc
    V, A_, Te, L, G = engs["V"], engs["A"], engs["T"], engs["L"], engs["G"]
    C = CHUNK

    def dump(gc, tile, name, rows):
        if dbg is None or gc != 0:
            return
        if name not in dbg:
            dbg[name] = nc.dram_tensor("DBG_" + name, [rows, C], F32, kind="ExternalOutput").ap()
        G.dma(dbg[name][:, :], tile, semname="D:DBG")

    _, w1 = load_lhsT_weight(P, G, st, "m_w1", hin["m_w1t"], D, D)
    _, w2 = load_lhsT_weight(P, G, st, "m_w2", hin["m_w2t"], D, D)
    _, dm = load_lhsT_weight(P, G, st, "m_dm", hin["m_dmt"], D, D)
    _, wl = load_lhsT_weight(P, G, st, "m_wl", hin["m_wlt"], D, D)
    _, dw = load_lhsT_weight(P, G, st, "m_dw", hin["m_dwt"], D, H)
    _, bw = load_lhsT_weight(P, G, st, "m_bw", hin["m_bwt"], D, H)
    cw_sb = st.enter_context(nc.sbuf_tensor("m_cw", [H, D], BF16))
    G.dma(cw_sb[:, :], hin["m_cwt"][:, :])
    normw = load_f32_tile(P, L, st, "m_normw_sb", hin["m_normw"], (128, NJ))
    convw = load_f32_tile(P, L, st, "m_convw_sb", hin["m_convw"], (128, NJ * 3))
    convb = load_f32_tile(P, L, st, "m_convb_sb", hin["m_convb"], (128, NJ))
    A_sb = load_f32_tile(P, L, st, "m_A_sb", hin["m_A"], (H, 1))
    db_sb = load_f32_tile(P, L, st, "m_db_sb", hin["m_db"], (H, 1))
    Bb_sb = load_f32_tile(P, L, st, "m_Bb_sb", hin["m_Bb"], (H, 1))
    Cb_sb = load_f32_tile(P, L, st, "m_Cb_sb", hin["m_Cb"], (128, NJ))
    Db_sb = load_f32_tile(P, L, st, "m_Db_sb", hin["m_Db"], (128, NJ))
    ones_row = st.enter_context(nc.sbuf_tensor("m_ones_row", [1, 128], F32))
    ones_bf = st.enter_context(nc.sbuf_tensor("m_ones_bf", [128, 1], BF16))

    xn0b_w = [st.enter_context(nc.sbuf_tensor(f"m_xn0b{r}", [128, NJ * C], BF16)) for r in range(2)]
    xn0b = [[xn0b_w[r][:, j * C:(j + 1) * C] for j in range(NJ)] for r in range(2)]
    xn2b = [st.enter_context(nc.sbuf_tensor(f"m_xn2b_{j}", [128, C], BF16)) for j in range(NJ)]
    xbf = [st.enter_context(nc.sbuf_tensor(f"m_xbf_{j}", [128, C + 2], F32)) for j in range(NJ)]
    xcb = [st.enter_context(nc.sbuf_tensor(f"m_xcb_{j}", [128, C], BF16)) for j in range(NJ)]
    zsf = [st.enter_context(nc.sbuf_tensor(f"m_zsf_{j}", [128, C], F32)) for j in range(NJ)]
    gb = [st.enter_context(nc.sbuf_tensor(f"m_gb_{j}", [128, C], BF16)) for j in range(NJ)]
    yf = st.enter_context(nc.sbuf_tensor("m_yf", [128, C], F32))
    cs = [st.enter_context(nc.sbuf_tensor(f"m_cs_{r}", [128, C], F32)) for r in range(2)]
    mo = [st.enter_context(nc.sbuf_tensor(f"m_mo_{r}", [128, C], F32)) for r in range(2)]
    A_log = st.enter_context(nc.sbuf_tensor("m_A_log", [H, 1], F32))
    relu_A = st.enter_context(nc.sbuf_tensor("m_relu_A", [H, 1], F32))
    sp_t = st.enter_context(nc.sbuf_tensor("m_sp_t", [H, C], F32))
    u_d = st.enter_context(nc.sbuf_tensor("m_u_d", [H, C], F32))
    dlt = st.enter_context(nc.sbuf_tensor("m_dlt", [H, C], F32))
    abl = st.enter_context(nc.sbuf_tensor("m_abl", [H, C], F32))
    bbar = st.enter_context(nc.sbuf_tensor("m_bbar", [H, C], F32))
    zero_h = st.enter_context(nc.sbuf_tensor("m_zero_h", [H, C], F32))
    acum_raw = st.enter_context(nc.sbuf_tensor("m_acum_raw", [H, C], F32))
    acum_ext = st.enter_context(nc.sbuf_tensor("m_acum_ext", [H, C + 1], F32))
    d_t = st.enter_context(nc.sbuf_tensor("m_d_t", [H, C], F32))
    g_t = st.enter_context(nc.sbuf_tensor("m_g_t", [H, C], F32))
    h_t = st.enter_context(nc.sbuf_tensor("m_h_t", [H, C], F32))
    hb = st.enter_context(nc.sbuf_tensor("m_hb", [H, C], BF16))
    carryA = st.enter_context(nc.sbuf_tensor("m_carryA", [H, 1], F32))
    carryH = st.enter_context(nc.sbuf_tensor("m_carryH", [H, 1], F32))
    rms_row = st.enter_context(nc.sbuf_tensor("m_rms_row", [1, C], F32))
    xsqb = [st.enter_context(nc.sbuf_tensor(f"m_xsqb_{j}", [128, C], BF16)) for j in range(NJ)]

    psS = st.enter_context(nc.psum_tensor("m_psS", [1, C], F32))
    psB = st.enter_context(nc.psum_tensor("m_psB", [128, C], F32))
    ps = [st.enter_context(nc.psum_tensor(f"m_ps{i}", [128, C], F32)) for i in range(4)]

    V.memset(ones_row[:, :], 1.0)
    V.memset(ones_bf[:, :], 1.0)
    V.memset(zero_h[:, :], 0.0)
    A_.act(A_log[:, :], A_sb[:, :], AF.Abs)
    A_.act(A_log[:, :], A_log[:, :], AF.Exp, scale=-1.0)
    A_.act(A_log[:, :], A_log[:, :], AF.Ln, bias=1.0)
    V.ts(relu_A[:, :], A_sb[:, :], 0.0, None, ALU.max)
    V.tt(A_log[:, :], A_log[:, :], relu_A[:, :], ALU.add)
    V.ts(A_log[:, :], A_log[:, :], -1.0, None, ALU.mult)
    V.tt(Cb_sb[:, :], Cb_sb[:, :], Db_sb[:, :], ALU.add)

    for s in range(SEQ_PER_CORE):
        V.memset(carryA[:, :], 0.0)
        V.memset(carryH[:, :], 0.0)
        V.memset(acum_ext[:, 0:1], 0.0)
        for j in range(NJ):
            V.memset(xbf[j][:, 0:2], 0.0)
        for ch in range(T // C):
            gc = s * (T // C) + ch
            sl = gc % 2
            c0 = s * T + ch * C
            for j in range(NJ):
                L.dma(xn0b[sl][j], XN0[j * 128:(j + 1) * 128, c0:c0 + C])
            for j in range(NJ):
                A_.act(xsqb[j][:], xn0b[sl][j], AF.Square)
            for j in range(NJ):
                Te.mm(psS[0:1, :], ones_bf[:, 0:1], xsqb[j][:], start=(j == 0), stop=(j == NJ - 1))
            V.ts(rms_row[:], psS[0:1, :], 1.0 / D, RMS_EPS, ALU.mult, ALU.add)
            A_.act(rms_row[:], rms_row[:], AF.Ln)
            A_.act(rms_row[:], rms_row[:], AF.Exp, scale=-0.5)
            Te.mm(psB[:, :], ones_row[:, :], rms_row[:], start=True, stop=True)
            for j in range(NJ):
                V.stt(xn2b[j][:], xn0b[sl][j], normw[:, j:j + 1], psB[:, :], ALU.mult, ALU.mult)
            if dbg is not None and gc == 0:
                for j in range(NJ):
                    dump(gc, xn2b[j][:, :], f"xn2b_{j}", 128)
            for k in range(NJ):
                for j in range(NJ):
                    Te.mm(ps[k % 2][:, :], w1(j, k), xn2b[j][:], start=(j == 0), stop=(j == NJ - 1))
                if ch > 0:
                    V.copy(xbf[k][:, 0:2], xbf[k][:, C:C + 2])
                V.copy(xbf[k][:, 2:C + 2], ps[k % 2][:, :])
            for k in range(NJ):
                for j in range(NJ):
                    Te.mm(ps[2 + k % 2][:, :], w2(j, k), xn2b[j][:], start=(j == 0), stop=(j == NJ - 1))
                A_.act(zsf[k][:], ps[2 + k % 2][:, :], AF.Silu)
            for k in range(NJ):
                cslot = k % 2
                V.ts(cs[cslot][:, :], xbf[k][:, 0:C], convw[:, 3 * k:3 * k + 1])
                V.stt(cs[cslot][:, :], xbf[k][:, 1:C + 1], convw[:, 3 * k + 1:3 * k + 2], cs[cslot][:, :], ALU.mult, ALU.add)
                V.stt(cs[cslot][:, :], xbf[k][:, 2:C + 2], convw[:, 3 * k + 2:3 * k + 3], cs[cslot][:, :], ALU.mult, ALU.add)
                A_.act(xcb[k][:], cs[cslot][:, :], AF.Silu, bias=convb[:, k:k + 1])
                dump(gc, xcb[k][:, :], f"xcb_{k}", 128)
            for j in range(NJ):
                Te.mm(ps[0][0:H, :], dw(j, 0, H), xcb[j][:], start=(j == 0), stop=(j == NJ - 1))
            for j in range(NJ):
                Te.mm(ps[1][0:H, :], bw(j, 0, H), xcb[j][:], start=(j == 0), stop=(j == NJ - 1))
            V.ts(u_d[:, :], ps[0][0:H, :], db_sb[:, 0:1], None, ALU.add)
            A_.act(sp_t[:, :], u_d[:, :], AF.Abs)
            A_.act(sp_t[:, :], sp_t[:, :], AF.Exp, scale=-1.0)
            A_.act(sp_t[:, :], sp_t[:, :], AF.Ln, bias=1.0)
            V.ts(dlt[:, :], u_d[:, :], 0.0, None, ALU.max)
            V.tt(dlt[:, :], dlt[:, :], sp_t[:, :], ALU.add)
            V.ts(abl[:, :], dlt[:, :], A_log[:, 0:1])
            V.stt(bbar[:, :], ps[1][0:H, :], Bb_sb[:, 0:1], dlt[:, :], ALU.add, ALU.mult)
            dump(gc, dlt[:, :], "dlt", H)
            dump(gc, abl[:, :], "abl", H)
            dump(gc, bbar[:, :], "bbar", H)
            V.scan(acum_raw[:, :], abl[:, :], zero_h[:, :], carryA[:, 0:1], ALU.add, ALU.add)
            V.copy(carryA[:, :], acum_raw[:, C - 1:C])
            V.ts(acum_ext[:, 1:C + 1], acum_raw[:, :], -20.0, 20.0, ALU.max, ALU.min)
            V.tt(d_t[:, :], acum_ext[:, 1:C + 1], acum_ext[:, 0:C], ALU.subtract)
            V.copy(acum_ext[:, 0:1], acum_ext[:, C:C + 1])
            dump(gc, acum_raw[:, :], "acum_raw", H)
            dump(gc, d_t[:, :], "d_t", H)
            A_.act(g_t[:, :], d_t[:, :], AF.Exp)
            V.scan(h_t[:, :], bbar[:, :], g_t[:, :], carryH[:, 0:1], ALU.add, ALU.mult)
            V.copy(carryH[:, :], h_t[:, C - 1:C])
            dump(gc, g_t[:, :], "g_t", H)
            dump(gc, h_t[:, :], "h_t", H)
            A_.act(hb[:, :], h_t[:, :], AF.Copy)
            for k in range(NJ):
                Te.mm(ps[2 + k % 2][:, :], cw_sb[:, k * 128:(k + 1) * 128], hb[:, :], start=True, stop=False)
                for j in range(NJ):
                    Te.mm(ps[2 + k % 2][:, :], dm(j, k), xcb[j][:], start=False, stop=(j == NJ - 1))
                V.ts(yf[:, :], ps[2 + k % 2][:, :], Cb_sb[:, k:k + 1], None, ALU.add)
                V.tt(gb[k][:, :], yf[:, :], zsf[k][:, :], ALU.mult)
            for k in range(NJ):
                for j in range(NJ):
                    Te.mm(ps[k % 2][:, :], wl(j, k), gb[j][:, :], start=(j == 0), stop=(j == NJ - 1))
                mslot = k % 2
                A_.act(mo[mslot][:, :], ps[k % 2][:, :], AF.Copy)
                G.dma(MAM[k * 128:(k + 1) * 128, c0:c0 + C], mo[mslot][:, :], disjoint=True)



def emit_moe(P, engs, st, hin, sm_all, XN0B, MOE, dbg=None):
    nc = P.nc
    V, A_, Te, L, G = engs["V"], engs["A"], engs["T"], engs["L"], engs["G"]
    C = CHUNK

    def dump(cond, tile, name, rows, cols=None):
        if dbg is None or not cond:
            return
        if name not in dbg:
            dbg[name] = nc.dram_tensor("DBG_" + name, [rows, cols or C], F32, kind="ExternalOutput").ap()
        G.dma(dbg[name][:, :], tile, semname="D:DBG")

    b1 = load_f32_tile(P, L, st, "f_b1_sb", hin["f_b1"], (128, E * NJ))
    b2 = load_f32_tile(P, L, st, "f_b2_sb", hin["f_b2"], (128, E * NJ))
    b3 = load_f32_tile(P, L, st, "f_b3_sb", hin["f_b3"], (128, E * NJ))
    sel_np = np.zeros((8, E * 128), dtype=np.float32)
    for e in range(E):
        sel_np[e, e * 128:(e + 1) * 128] = 1.0
    sel_dram = nc.inline_tensor(sel_np, "f_sel").ap()
    sel = st.enter_context(nc.sbuf_tensor("f_sel_sb", [8, E * 128], F32))
    L.dma(sel[:, :], sel_dram[:, :])

    xn0b_w = [st.enter_context(nc.sbuf_tensor(f"f_xn0b{r}", [128, NJ * C], BF16)) for r in range(2)]
    NWB = 3
    wbuf = [st.enter_context(nc.sbuf_tensor(f"f_wbuf{i}", [128, NJ * D], BF16)) for i in range(NWB)]
    xs_w = [st.enter_context(nc.sbuf_tensor(f"f_xs{r}", [128, NJ * C], BF16)) for r in range(2)]
    hb_w = st.enter_context(nc.sbuf_tensor("f_hb", [128, NJ * C], BF16))
    x3_w = st.enter_context(nc.sbuf_tensor("f_x3", [128, NJ * C], BF16))
    acc_w = st.enter_context(nc.sbuf_tensor("f_acc", [128, NJ * NTOK], BF16))
    x2t = [st.enter_context(nc.sbuf_tensor(f"f_x2t{r}", [128, C], F32)) for r in range(2)]
    qt = [st.enter_context(nc.sbuf_tensor(f"f_qt{r}", [128, C], F32)) for r in range(2)]
    mo = [st.enter_context(nc.sbuf_tensor(f"f_mo{r}", [128, C], F32)) for r in range(4)]
    bc_s = [st.enter_context(nc.sbuf_tensor(f"f_bc_s{r}", [128, C], F32)) for r in range(2)]
    bc_m = [st.enter_context(nc.sbuf_tensor(f"f_bc_m{r}", [128, C], F32)) for r in range(2)]

    psS = st.enter_context(nc.psum_tensor("f_psS", [128, C], F32))
    psU = [st.enter_context(nc.psum_tensor(f"f_psU{i}", [128, C], F32)) for i in range(2)]

    wsl = {}

    def load_cls(cls, src3d, e, q=None):
        wb = wbuf[{"w1": 0, "w3": 1, "w2": 2}[cls]]
        for j in range(NJ):
            G.dma(wb[:, j * D:(j + 1) * D], src3d[e, j * 128:(j + 1) * 128, :])

        def sl(j, k):
            return wb[:, j * D + k * 128: j * D + (k + 1) * 128]
        wsl[(cls, e)] = sl

    def emit_bc_xs(e, ch):
        sl_ = ch % 2
        c0 = ch * C
        Te.mm(psS[:, :], sel[:, e * 128:(e + 1) * 128], sm_all[:, c0:c0 + C], start=True, stop=True)
        V.copy(bc_s[sl_][:, :], psS[:, :])
        V.ts(bc_m[sl_][:, :], bc_s[sl_][:, :], 0.0, None, ALU.is_gt)
        for j in range(NJ):
            V.tt(xs_w[sl_][:, j * C:(j + 1) * C], xn0b_w[sl_][:, j * C:(j + 1) * C], bc_s[sl_][:, :], ALU.mult)

    def load_xn0b(ch):
        sl_ = ch % 2
        c0 = ch * C
        for j in range(NJ):
            L.dma(xn0b_w[sl_][:, j * C:(j + 1) * C], XN0B[j * 128:(j + 1) * 128, c0:c0 + C])

    load_cls("w1", hin["f_w1t"], 0, G)
    load_cls("w3", hin["f_w3t"], 0, L)
    load_cls("w2", hin["f_w2t"], 0, G)
    load_xn0b(0)
    load_xn0b(1)
    dump(True, sm_all[:, 0:C], "sm_fm", 8)
    emit_bc_xs(0, 0)
    for e in range(E):
        w1s = wsl.pop(("w1", e))
        w3s = wsl.pop(("w3", e))
        w2s = wsl.pop(("w2", e))
        for ch in range(NCH):
            sl_ = ch % 2
            c0 = ch * C
            last = ch == NCH - 1
            xs = xs_w[sl_]
            for k in range(NJ):
                for j in range(NJ):
                    Te.mm(psU[k % 2][:, :], w1s(j, k), xs[:, j * C:(j + 1) * C], start=(j == 0), stop=(j == NJ - 1))
                A_.act(hb_w[:, k * C:(k + 1) * C], psU[k % 2][:, :], AF.Silu, bias=b1[:, e * NJ + k:e * NJ + k + 1])
            if last and e + 1 < E:
                load_cls("w1", hin["f_w1t"], e + 1)
            for k in range(NJ):
                for j in range(NJ):
                    Te.mm(psU[k % 2][:, :], w3s(j, k), xs[:, j * C:(j + 1) * C], start=(j == 0), stop=(j == NJ - 1))
                V.copy(x3_w[:, k * C:(k + 1) * C], psU[k % 2][:, :])
            if last and e + 1 < E:
                load_cls("w3", hin["f_w3t"], e + 1)
            if ch + 1 < NCH:
                emit_bc_xs(e, ch + 1)
                if ch + 2 < NCH:
                    load_xn0b(ch + 2)
            elif e + 1 < E:
                load_xn0b(0)
                emit_bc_xs(e + 1, 0)
                load_xn0b(1)
            for k in range(NJ):
                r = k % 2
                rq = k % 2
                for j in range(NJ):
                    Te.mm(psU[r][:, :], w2s(j, k), hb_w[:, j * C:(j + 1) * C], start=(j == 0), stop=(j == NJ - 1))
                A_.act(x2t[r][:, :], psU[r][:, :], AF.Identity, bias=b2[:, e * NJ + k:e * NJ + k + 1])
                V.stt(qt[rq][:, :], x3_w[:, k * C:(k + 1) * C], b3[:, e * NJ + k:e * NJ + k + 1], x2t[r][:, :], ALU.add, ALU.mult)
                V.tt(qt[rq][:, :], qt[rq][:, :], bc_m[sl_][:, :], ALU.mult)
                asl = acc_w[:, k * NTOK + c0: k * NTOK + c0 + C]
                if e == 0:
                    V.copy(asl, qt[rq][:, :])
                elif e < E - 1:
                    V.tt(asl, asl, qt[rq][:, :], ALU.add)
                else:
                    V.tt(mo[k % 4][:, :], asl, qt[rq][:, :], ALU.add)
                    G.dma(MOE[k * 128:(k + 1) * 128, c0:c0 + C], mo[k % 4][:, :], accum=True, unordered=True)
        if e + 1 < E:
            load_cls("w2", hin["f_w2t"], e + 1)


def emit_shared(P, engs, st, hin, XN0, SHR):
    nc = P.nc
    V, A_, Te, L, G = engs["V"], engs["A"], engs["T"], engs["L"], engs["G"]
    C = CHUNK
    _, s1 = load_lhsT_weight(P, G, st, "s_sw1", hin["f_sw1t"], D, I2)
    _, s2 = load_lhsT_weight(P, G, st, "s_sw2", hin["f_sw2t"], D, I2)
    _, s3 = load_lhsT_weight(P, G, st, "s_sw3", hin["f_sw3t"], I2, D)
    xn0b_w = [st.enter_context(nc.sbuf_tensor(f"s_xn0b{r}", [128, NJ * C], BF16)) for r in range(2)]
    g_w = st.enter_context(nc.sbuf_tensor("s_g", [128, NI * C], BF16))
    tmp = st.enter_context(nc.sbuf_tensor("s_tmp", [128, C], F32))
    zo = [st.enter_context(nc.sbuf_tensor(f"s_zo{r}", [128, C], F32)) for r in range(2)]
    psA = [st.enter_context(nc.psum_tensor(f"s_psA{i}", [128, C], F32)) for i in range(2)]
    psB = [st.enter_context(nc.psum_tensor(f"s_psB{i}", [128, C], F32)) for i in range(2)]

    for ch in range(NCH):
        sl_ = ch % 2
        c0 = ch * C
        xb = xn0b_w[sl_]
        for j in range(NJ):
            L.dma(xb[:, j * C:(j + 1) * C], XN0[j * 128:(j + 1) * 128, c0:c0 + C])
        for i in range(NI):
            for j in range(NJ):
                Te.mm(psA[i % 2][:, :], s1(j, i), xb[:, j * C:(j + 1) * C], start=(j == 0), stop=(j == NJ - 1))
            for j in range(NJ):
                Te.mm(psB[i % 2][:, :], s2(j, i), xb[:, j * C:(j + 1) * C], start=(j == 0), stop=(j == NJ - 1))
            A_.act(tmp[:, :], psA[i % 2][:, :], AF.Silu)
            V.tt(g_w[:, i * C:(i + 1) * C], tmp[:, :], psB[i % 2][:, :], ALU.mult)
        for k in range(NJ):
            for i in range(NI):
                Te.mm(psA[k % 2][:, :], s3(i, k), g_w[:, i * C:(i + 1) * C], start=(i == 0), stop=(i == NI - 1))
            V.copy(zo[k % 2][:, :], psA[k % 2][:, :])
            G.dma(SHR[k * 128:(k + 1) * 128, c0:c0 + C], zo[k % 2][:, :], accum=True)



def emit_shared_ln(P, engs, st, hin, XN0B, terms, OUT, XN1, XN1B):
    nc = P.nc
    V, A_, Te, L, G = engs["V"], engs["A"], engs["T"], engs["L"], engs["G"]
    Q = engs["Q"]
    C = CHUNK
    _, s1 = load_lhsT_weight(P, G, st, "s_sw1", hin["f_sw1t"], D, I2)
    _, s2 = load_lhsT_weight(P, G, st, "s_sw2", hin["f_sw2t"], D, I2)
    _, s3 = load_lhsT_weight(P, G, st, "s_sw3", hin["f_sw3t"], I2, D)
    xn0b_w = [st.enter_context(nc.sbuf_tensor(f"s_xn0b{r}", [128, NJ * C], BF16)) for r in range(2)]
    g_w = st.enter_context(nc.sbuf_tensor("s_g", [128, NI * C], BF16))
    tmp = st.enter_context(nc.sbuf_tensor("s_tmp", [128, C], F32))
    nterm = len(terms)
    tin_w = [st.enter_context(nc.sbuf_tensor(f"c_tin{r}", [128, nterm * C], F32)) for r in range(2)]
    tin = [[tin_w[r][:, i * C:(i + 1) * C] for i in range(nterm)] for r in range(2)]
    x1_w2 = [st.enter_context(nc.sbuf_tensor(f"c_x1{r}", [128, NJ * C], F32)) for r in range(2)]
    nk_t = [st.enter_context(nc.sbuf_tensor(f"c_nk{r}", [128, C], F32)) for r in range(2)]
    xb16 = st.enter_context(nc.sbuf_tensor("c_xb16", [128, NJ * C], BF16))
    xsq = st.enter_context(nc.sbuf_tensor("c_xsq", [128, NJ * C], BF16))
    ones_bf = st.enter_context(nc.sbuf_tensor("c_ones", [128, 1], BF16))
    ones_row = st.enter_context(nc.sbuf_tensor("c_ones_row", [1, 128], F32))
    rows = [st.enter_context(nc.sbuf_tensor(f"c_row{i}", [1, C], F32)) for i in range(4)]
    m_row, v_row, r_row, mr_row = rows
    V.memset(ones_bf[:, :], 1.0)
    V.memset(ones_row[:, :], 1.0)
    psA = [st.enter_context(nc.psum_tensor(f"s_psA{i}", [128, C], F32)) for i in range(2)]
    psB = [st.enter_context(nc.psum_tensor(f"s_psB{i}", [128, C], F32)) for i in range(2)]
    ps_sum = st.enter_context(nc.psum_tensor("c_ps_sum", [1, C], F32))
    ps_sq = st.enter_context(nc.psum_tensor("c_ps_sq", [1, C], F32))
    ps_r = st.enter_context(nc.psum_tensor("c_ps_r", [128, C], F32))
    ps_mr = st.enter_context(nc.psum_tensor("c_ps_mr", [128, C], F32))

    def compute_block(ch):
        sl_ = ch % 2
        c0 = ch * C
        xb = xn0b_w[sl_]
        x1_w = x1_w2[sl_]
        for j in range(NJ):
            L.dma(xb[:, j * C:(j + 1) * C], XN0B[j * 128:(j + 1) * 128, c0:c0 + C])
        for i in range(NI):
            for j in range(NJ):
                Te.mm(psA[i % 2][:, :], s1(j, i), xb[:, j * C:(j + 1) * C], start=(j == 0), stop=(j == NJ - 1))
            for j in range(NJ):
                Te.mm(psB[i % 2][:, :], s2(j, i), xb[:, j * C:(j + 1) * C], start=(j == 0), stop=(j == NJ - 1))
            A_.act(tmp[:, :], psA[i % 2][:, :], AF.Silu)
            V.tt(g_w[:, i * C:(i + 1) * C], tmp[:, :], psB[i % 2][:, :], ALU.mult)
        for k in range(2):
            for i, t in enumerate(terms):
                L.dma(tin[k][i], t[k * 128:(k + 1) * 128, c0:c0 + C])
        for k in range(NJ):
            r = k % 2
            for i in range(NI):
                Te.mm(psA[k % 2][:, :], s3(i, k), g_w[:, i * C:(i + 1) * C], start=(i == 0), stop=(i == NI - 1))
            xk = x1_w[:, k * C:(k + 1) * C]
            V.tt(xk, tin[r][0], tin[r][1], ALU.add)
            for i in range(2, nterm):
                V.tt(xk, xk, tin[r][i], ALU.add)
            V.tt(xk, xk, psA[k % 2][:, :], ALU.add)
            if k + 2 < NJ:
                for i, t in enumerate(terms):
                    L.dma(tin[(k + 2) % 2][i], t[(k + 2) * 128:(k + 3) * 128, c0:c0 + C])

    def ln_block(ch):
        sl_ = ch % 2
        c0 = ch * C
        x1_w = x1_w2[sl_]
        for k in range(NJ):
            xk = x1_w[:, k * C:(k + 1) * C]
            A_.act(xb16[:, k * C:(k + 1) * C], xk, AF.Copy)
            Q.tt(xsq[:, k * C:(k + 1) * C], xb16[:, k * C:(k + 1) * C], xb16[:, k * C:(k + 1) * C], ALU.mult)
        for k in range(NJ):
            Te.mm(ps_sum[0:1, :], ones_bf[:, 0:1], xb16[:, k * C:(k + 1) * C], start=(k == 0), stop=(k == NJ - 1))
        for k in range(NJ):
            Te.mm(ps_sq[0:1, :], ones_bf[:, 0:1], xsq[:, k * C:(k + 1) * C], start=(k == 0), stop=(k == NJ - 1))
        V.ts(m_row[:], ps_sum[0:1, :], 1.0 / D)
        V.ts(v_row[:], ps_sq[0:1, :], 1.0 / D, LN_EPS, ALU.mult, ALU.add)
        V.tt(mr_row[:], m_row[:], m_row[:], ALU.mult)
        V.tt(v_row[:], v_row[:], mr_row[:], ALU.subtract)
        A_.act(r_row[:], v_row[:], AF.Ln)
        A_.act(r_row[:], r_row[:], AF.Exp, scale=-0.5)
        V.tt(mr_row[:], m_row[:], r_row[:], ALU.mult)
        Te.mm(ps_r[:, :], ones_row[:, :], r_row[:], start=True, stop=True)
        Te.mm(ps_mr[:, :], ones_row[:, :], mr_row[:], start=True, stop=True)
        for k in range(NJ):
            xk = x1_w[:, k * C:(k + 1) * C]
            nk = nk_t[k % 2][:, :]
            V.tt(nk, xk, ps_r[:, :], ALU.mult)
            V.tt(nk, nk, ps_mr[:, :], ALU.subtract)
            G.dma(OUT[k * 128:(k + 1) * 128, c0:c0 + C], xk)
            L.dma(XN1[k * 128:(k + 1) * 128, c0:c0 + C], nk, disjoint=True)
            G.dma(XN1B[k * 128:(k + 1) * 128, c0:c0 + C], nk, disjoint=True)

    for ch in range(NCH + 1):
        if ch < NCH:
            compute_block(ch)
        if ch >= 1:
            ln_block(ch - 1)



def emit_combine_ln(P, engs, st, terms, X1, XN1, XN1B=None):
    nc = P.nc
    V, A_, Te, L, G = engs["V"], engs["A"], engs["T"], engs["L"], engs["G"]
    C = CHUNK
    nterm = len(terms)
    Q = engs["Q"]
    tin_w = [st.enter_context(nc.sbuf_tensor(f"c_tin{r}", [128, nterm * C], F32)) for r in range(2)]
    tin = [[tin_w[r][:, i * C:(i + 1) * C] for i in range(nterm)] for r in range(2)]
    x1_w2 = [st.enter_context(nc.sbuf_tensor(f"c_x1{r}", [128, NJ * C], F32)) for r in range(2)]
    nk_t = [st.enter_context(nc.sbuf_tensor(f"c_nk{r}", [128, C], F32)) for r in range(2)]
    xb16 = st.enter_context(nc.sbuf_tensor("c_xb16", [128, NJ * C], BF16))
    xsq = st.enter_context(nc.sbuf_tensor("c_xsq", [128, NJ * C], BF16))
    ones_bf = st.enter_context(nc.sbuf_tensor("c_ones", [128, 1], BF16))
    ones_row = st.enter_context(nc.sbuf_tensor("c_ones_row", [1, 128], F32))
    ps_sum = st.enter_context(nc.psum_tensor("c_ps_sum", [1, C], F32))
    ps_sq = st.enter_context(nc.psum_tensor("c_ps_sq", [1, C], F32))
    ps_r = st.enter_context(nc.psum_tensor("c_ps_r", [128, C], F32))
    ps_mr = st.enter_context(nc.psum_tensor("c_ps_mr", [128, C], F32))
    rows = [st.enter_context(nc.sbuf_tensor(f"c_row{i}", [1, C], F32)) for i in range(4)]
    m_row, v_row, r_row, mr_row = rows
    V.memset(ones_bf[:, :], 1.0)
    V.memset(ones_row[:, :], 1.0)
    for ch in range(NCH):
        c0 = ch * C
        x1_w, xn1_w = x1_w2[ch % 2], xn1_w2[ch % 2]
        xb16, xsq = xb16_2[ch % 2], xsq_2[ch % 2]
        for j in range(NJ):
            r = j % 4
            for i, t in enumerate(terms):
                L.dma(tin[r][i], t[j * 128:(j + 1) * 128, c0:c0 + C])
            xj = x1_w[:, j * C:(j + 1) * C]
            V.tt(xj, tin[r][0], tin[r][1], ALU.add)
            for i in range(2, nterm):
                V.tt(xj, xj, tin[r][i], ALU.add)
            A_.act(xb16[:, j * C:(j + 1) * C], xj, AF.Copy)
            Q.tt(xsq[:, j * C:(j + 1) * C], xb16[:, j * C:(j + 1) * C], xb16[:, j * C:(j + 1) * C], ALU.mult)
        for j in range(NJ):
            Te.mm(ps_sum[0:1, :], ones_bf[:, 0:1], xb16[:, j * C:(j + 1) * C], start=(j == 0), stop=(j == NJ - 1))
        for j in range(NJ):
            Te.mm(ps_sq[0:1, :], ones_bf[:, 0:1], xsq[:, j * C:(j + 1) * C], start=(j == 0), stop=(j == NJ - 1))
        V.ts(m_row[:], ps_sum[0:1, :], 1.0 / D)
        V.ts(v_row[:], ps_sq[0:1, :], 1.0 / D, LN_EPS, ALU.mult, ALU.add)
        V.tt(mr_row[:], m_row[:], m_row[:], ALU.mult)
        V.tt(v_row[:], v_row[:], mr_row[:], ALU.subtract)
        A_.act(r_row[:], v_row[:], AF.Ln)
        A_.act(r_row[:], r_row[:], AF.Exp, scale=-0.5)
        V.tt(mr_row[:], m_row[:], r_row[:], ALU.mult)
        Te.mm(ps_r[:, :], ones_row[:, :], r_row[:], start=True, stop=True)
        Te.mm(ps_mr[:, :], ones_row[:, :], mr_row[:], start=True, stop=True)
        for j in range(NJ):
            xj = x1_w[:, j * C:(j + 1) * C]
            nj = xn1_w[:, j * C:(j + 1) * C]
            V.tt(nj, xj, ps_r[:, :], ALU.mult)
            V.tt(nj, nj, ps_mr[:, :], ALU.subtract)
            G.dma(X1[j * 128:(j + 1) * 128, c0:c0 + C], xj, disjoint=True)
            L.dma(XN1[j * 128:(j + 1) * 128, c0:c0 + C], nj, disjoint=True)
            if XN1B is not None:
                G.dma(XN1B[j * 128:(j + 1) * 128, c0:c0 + C], nj, disjoint=True)



def emit_rwkv(P, engs, st, hin, XN1, FFN1):
    nc = P.nc
    V, A_, Te, L, G = engs["V"], engs["A"], engs["T"], engs["L"], engs["G"]
    C = CHUNK
    _, wr = load_lhsT_weight(P, G, st, "r_wr", hin["r_wrt"], D, D)
    _, wk = load_lhsT_weight(P, G, st, "r_wk", hin["r_wkt"], D, D)
    _, wv = load_lhsT_weight(P, G, st, "r_wv", hin["r_wvt"], D, D)
    mur = load_f32_tile(P, L, st, "r_mur_sb", hin["r_mur"], (128, NJ))
    muk = load_f32_tile(P, L, st, "r_muk_sb", hin["r_muk"], (128, NJ))
    omr = st.enter_context(nc.sbuf_tensor("r_omr", [128, NJ], F32))
    omk = st.enter_context(nc.sbuf_tensor("r_omk", [128, NJ], F32))
    V.ts(omr[:, :], mur[:, :], -1.0, 1.0, ALU.mult, ALU.add)
    V.ts(omk[:, :], muk[:, :], -1.0, 1.0, ALU.mult, ALU.add)
    CE = C + 1
    Q = engs["Q"]
    xe_w = [st.enter_context(nc.sbuf_tensor(f"r_xe{r}", [128, NJ * CE], F32)) for r in range(2)]
    xr_w2 = [st.enter_context(nc.sbuf_tensor(f"r_xr{r}", [128, NJ * C], BF16)) for r in range(2)]
    xk_w2 = [st.enter_context(nc.sbuf_tensor(f"r_xk{r}", [128, NJ * C], BF16)) for r in range(2)]
    ka_w = st.enter_context(nc.sbuf_tensor("r_ka", [128, NJ * C], BF16))
    r_w = st.enter_context(nc.sbuf_tensor("r_r", [128, NJ * C], F32))
    tmp = st.enter_context(nc.sbuf_tensor("r_tmp", [128, C], F32))
    tmq = st.enter_context(nc.sbuf_tensor("r_tmq", [128, C], F32))
    tmv = st.enter_context(nc.sbuf_tensor("r_tmv", [128, C], F32))
    fo = [st.enter_context(nc.sbuf_tensor(f"r_fo{i}", [128, C], F32)) for i in range(2)]
    psR = [st.enter_context(nc.psum_tensor(f"r_psR{i}", [128, C], F32)) for i in range(2)]
    psK = [st.enter_context(nc.psum_tensor(f"r_psK{i}", [128, C], F32)) for i in range(2)]

    def load_blend(s, ch):
        sl_ = ch % 2
        c0 = s * T + ch * C
        xe = xe_w[sl_]
        for j in range(NJ):
            xj = xe[:, j * CE:(j + 1) * CE]
            if ch == 0:
                V.memset(xj[:, 0:1], 0.0)
                L.dma(xj[:, 1:CE], XN1[j * 128:(j + 1) * 128, c0:c0 + C])
            else:
                L.dma(xj, XN1[j * 128:(j + 1) * 128, c0 - 1:c0 + C])
        for j in range(NJ):
            xj = xe[:, j * CE:(j + 1) * CE]
            V.ts(tmp[:, :], xj[:, 0:C], omr[:, j:j + 1])
            V.stt(xr_w2[sl_][:, j * C:(j + 1) * C], xj[:, 1:CE], mur[:, j:j + 1], tmp[:, :], ALU.mult, ALU.add)
            V.ts(tmq[:, :], xj[:, 0:C], omk[:, j:j + 1])
            V.stt(xk_w2[sl_][:, j * C:(j + 1) * C], xj[:, 1:CE], muk[:, j:j + 1], tmq[:, :], ALU.mult, ALU.add)

    load_blend(0, 0)
    for s in range(SEQ_PER_CORE):
        for ch in range(T // C):
            sl_ = ch % 2
            c0 = s * T + ch * C
            xr_w, xk_w = xr_w2[sl_], xk_w2[sl_]
            for k in range(NJ):
                for j in range(NJ):
                    Te.mm(psR[k % 2][:, :], wr(j, k), xr_w[:, j * C:(j + 1) * C], start=(j == 0), stop=(j == NJ - 1))
                A_.act(r_w[:, k * C:(k + 1) * C], psR[k % 2][:, :], AF.Sigmoid)
                for j in range(NJ):
                    Te.mm(psK[k % 2][:, :], wk(j, k), xk_w[:, j * C:(j + 1) * C], start=(j == 0), stop=(j == NJ - 1))
                V.ts(tmv[:, :], psK[k % 2][:, :], 0.0, None, ALU.max)
                A_.act(ka_w[:, k * C:(k + 1) * C], tmv[:, :], AF.Square)
            if ch + 1 < T // C:
                load_blend(s, ch + 1)
            elif s + 1 < SEQ_PER_CORE:
                load_blend(s + 1, 0)
            for k in range(NJ):
                for j in range(NJ):
                    Te.mm(psR[k % 2][:, :], wv(j, k), ka_w[:, j * C:(j + 1) * C], start=(j == 0), stop=(j == NJ - 1))
                V.tt(fo[k % 2][:, :], psR[k % 2][:, :], r_w[:, k * C:(k + 1) * C], ALU.mult)
                G.dma(FFN1[k * 128:(k + 1) * 128, c0:c0 + C], fo[k % 2][:, :], accum=True)



def emit_performer(P, engs, st, hin, XN1, ATT):
    nc = P.nc
    V, A_, Te, L, G = engs["V"], engs["A"], engs["T"], engs["L"], engs["G"]
    Q = engs["Q"]
    NT = T // 128
    _, wq = load_lhsT_weight(P, G, st, "a_wq_sb", hin["a_wq"], D, D)
    _, wk = load_lhsT_weight(P, G, st, "a_wk_sb", hin["a_wk"], D, D)
    _, wv = load_lhsT_weight(P, G, st, "a_wv_sb", hin["a_wv"], D, D)
    _, wo = load_lhsT_weight(P, G, st, "a_wo_sb", hin["a_wo"], D, D)
    bo = load_f32_tile(P, L, st, "a_bo_sb", hin["a_bo"], (128, NJ))
    projf = load_f32_tile(P, L, st, "a_projf", hin["a_projt"], (DH, NBF))
    projb = st.enter_context(nc.sbuf_tensor("a_projb", [DH, NBF], BF16))
    V.ts(projb[:, :], projf[:, :], PERF_DN)
    identb_np = np.eye(128, dtype=ml_dtypes.bfloat16)
    identb = st.enter_context(nc.sbuf_tensor("a_identb_sb", [128, 128], BF16))
    L.dma(identb[:, :], nc.inline_tensor(identb_np, "a_identb").ap()[:, :])
    identf_np = np.eye(128, dtype=np.float32)
    identf = st.enter_context(nc.sbuf_tensor("a_identf_sb", [128, 128], F32))
    L.dma(identf[:, :], nc.inline_tensor(identf_np, "a_identf").ap()[:, :])
    ones_row = st.enter_context(nc.sbuf_tensor("a_ones_row", [1, 128], F32))
    V.memset(ones_row[:, :], 1.0)
    ones_colb = st.enter_context(nc.sbuf_tensor("a_ones_colb", [128, 1], BF16))
    V.memset(ones_colb[:, :], 1.0)
    ones_rowb = st.enter_context(nc.sbuf_tensor("a_ones_rowb", [1, 128], BF16))
    V.memset(ones_rowb[:, :], 1.0)
    vsum_sb = [st.enter_context(nc.sbuf_tensor(f"a_vsum{r}", [1, DH + 1], BF16)) for r in range(2)]
    csum_sb = st.enter_context(nc.sbuf_tensor("a_csum", [1, DH + 1], BF16))
    colk_all = [st.enter_context(nc.sbuf_tensor(f"a_colk{r}", [128, NT], F32)) for r in range(2)]

    xn1b = st.enter_context(nc.sbuf_tensor("a_xn1b", [128, NJ * T], BF16))
    attn_fm = st.enter_context(nc.sbuf_tensor("a_attn_fm", [128, NJ * T], BF16))
    q_b = [st.enter_context(nc.sbuf_tensor(f"a_q_b{r}", [DH, T], BF16)) for r in range(2)]
    k_b = [st.enter_context(nc.sbuf_tensor(f"a_k_b{r}", [DH, T], BF16)) for r in range(2)]
    v_f = [st.enter_context(nc.sbuf_tensor(f"a_v_f{r}", [DH, T], BF16)) for r in range(2)]
    qsq = [st.enter_context(nc.sbuf_tensor(f"a_qsq{r}", [DH, T], BF16)) for r in range(2)]
    ksq = [st.enter_context(nc.sbuf_tensor(f"a_ksq{r}", [DH, T], BF16)) for r in range(2)]
    v_w = [st.enter_context(nc.sbuf_tensor(f"a_v_w{r}", [128, NT * (DH + 1)], BF16)) for r in range(2)]
    V.memset(v_w[0][:, :], 1.0)
    V.memset(v_w[1][:, :], 1.0)
    qp_fm = st.enter_context(nc.sbuf_tensor("a_qp_fm", [128, 3 * T], BF16))
    diag_q = [st.enter_context(nc.sbuf_tensor(f"a_diag_q{r}", [128, NT], F32)) for r in range(2)]
    diag_k = [st.enter_context(nc.sbuf_tensor(f"a_diag_k{r}", [128, NT], F32)) for r in range(2)]
    kmax_c = st.enter_context(nc.sbuf_tensor("a_kmax", [128, NT], F32))
    col_a = st.enter_context(nc.sbuf_tensor("a_col_a", [128, 1], F32))
    col_b = [st.enter_context(nc.sbuf_tensor(f"a_col_b{r}", [128, 1], F32)) for r in range(2)]
    stabk = st.enter_context(nc.sbuf_tensor("a_stabk", [128, 1], F32))
    row1 = st.enter_context(nc.sbuf_tensor("a_row1", [1, 128], F32))
    sc11 = st.enter_context(nc.sbuf_tensor("a_sc11", [1, 1], F32))
    e_t = [st.enter_context(nc.sbuf_tensor(f"a_e_t{r}", [128, NBF], F32)) for r in range(2)]
    qkp_b = [st.enter_context(nc.sbuf_tensor(f"a_qkp{r}", [128, NBF], BF16)) for r in range(2)]
    ctx_b = st.enter_context(nc.sbuf_tensor("a_ctx", [128, 3 * (DH + 1)], BF16))
    kp_all = st.enter_context(nc.sbuf_tensor("a_kp_all", [128, NT * NBF], BF16))
    ho_t = st.enter_context(nc.sbuf_tensor("a_ho", [128, DH], BF16))
    ao = [st.enter_context(nc.sbuf_tensor(f"a_ao{i}", [128, CHUNK], F32)) for i in range(2)]

    psA = st.enter_context(nc.psum_tensor("a_psA", [128, CHUNK], F32))
    psB = st.enter_context(nc.psum_tensor("a_psB", [128, CHUNK], F32))
    psT_bf = st.enter_context(nc.psum_tensor("a_psT_bf", [128, 3 * 128], BF16))
    psD = [st.enter_context(nc.psum_tensor(f"a_psD{i}", [128, NBF], F32)) for i in range(2)]
    psCtx = st.enter_context(nc.psum_tensor("a_psCtx", [128, 3 * (DH + 1)], F32))
    psO = [st.enter_context(nc.psum_tensor(f"a_psO{i}", [128, DH + 1], F32)) for i in range(2)]
    MBW = [128, 128, NBF - 256]

    def emit_qkv(s, h):
        r = h % 2
        for cc in range(T // CHUNK):
            ps = psA if cc % 2 == 0 else psB
            csl = slice(cc * CHUNK, (cc + 1) * CHUNK)
            for j in range(NJ):
                Te.mm(ps[0:DH, :], wq(j, h, DH), xn1b[:, j * T + cc * CHUNK: j * T + (cc + 1) * CHUNK], start=(j == 0), stop=(j == NJ - 1))
            A_.act(q_b[r][:, csl], ps[0:DH, :], AF.Copy)
            Q.tt(qsq[r][:, csl], q_b[r][:, csl], q_b[r][:, csl], ALU.mult)
            for j in range(NJ):
                Te.mm(ps[0:DH, :], wk(j, h, DH), xn1b[:, j * T + cc * CHUNK: j * T + (cc + 1) * CHUNK], start=(j == 0), stop=(j == NJ - 1))
            A_.act(k_b[r][:, csl], ps[0:DH, :], AF.Copy)
            Q.tt(ksq[r][:, csl], k_b[r][:, csl], k_b[r][:, csl], ALU.mult)
            for j in range(NJ):
                Te.mm(ps[0:DH, :], wv(j, h, DH), xn1b[:, j * T + cc * CHUNK: j * T + (cc + 1) * CHUNK], start=(j == 0), stop=(j == NJ - 1))
            A_.act(v_f[r][:, csl], ps[0:DH, :], AF.Copy)
            yield
        for ti in range(NT):
            reg = (ti % 3) * 128
            Te.transpose(psT_bf[:, reg:reg + DH], v_f[r][:, ti * 128:(ti + 1) * 128], identb[0:DH, 0:DH])
            V.copy(v_w[r][:, ti * (DH + 1): ti * (DH + 1) + DH], psT_bf[:, reg:reg + DH])
            if ti % 4 == 3:
                yield
        for ti in range(NT):
            Te.mm(psA[:, ti:ti + 1], qsq[r][:, ti * 128:(ti + 1) * 128], ones_colb[0:DH, 0:1], start=True, stop=True)
        V.ts(diag_q[r][:, :], psA[:, 0:NT], 0.5 * PERF_DN * PERF_DN)
        yield
        for ti in range(NT):
            Te.mm(psB[:, ti:ti + 1], ksq[r][:, ti * 128:(ti + 1) * 128], ones_colb[0:DH, 0:1], start=True, stop=True)
        V.ts(diag_k[r][:, :], psB[:, 0:NT], 0.5 * PERF_DN * PERF_DN)
        yield
        for ti in range(NT):
            Te.mm(psO[0][0:1, :], ones_colb[:, 0:1], v_w[r][:, ti * (DH + 1):(ti + 1) * (DH + 1)], start=(ti == 0), stop=(ti == NT - 1))
        V.ts(vsum_sb[r][:, :], psO[0][0:1, :], PERF_EPS)
        yield

    def emit_post(s, h):
        r = h % 2
        for ti in range(NT):
            Te.mm(psD[ti % 2][:, :], k_b[r][:, ti * 128:(ti + 1) * 128], projb[:, :], start=True, stop=True)
            V.reduce(kmax_c[:, ti:ti + 1], psD[ti % 2][:, :], ALU.max)
            if ti % 4 == 3:
                yield
        V.reduce(col_a[:, 0:1], kmax_c[:, :], ALU.max)
        Te.transpose(psD[0][0:1, 0:128], col_a[:, 0:1], identf[:, :])
        V.copy(row1[:, :], psD[0][0:1, 0:128])
        V.reduce(sc11[0:1, 0:1], row1[:, :], ALU.max)
        Te.mm(psD[0][:, 0:1], ones_row[:, :], sc11[0:1, 0:1], start=True, stop=True)
        V.ts(stabk[:, 0:1], psD[0][:, 0:1], -1.0)
        V.ts(colk_all[r][:, :], diag_k[r][:, :], stabk[:, 0:1], -1.0, ALU.subtract, ALU.mult)
        yield
        for ti in range(NT):
            rr = ti % 2
            Te.mm(psD[rr][:, :], k_b[r][:, ti * 128:(ti + 1) * 128], projb[:, :], start=True, stop=True)
            A_.act(kp_all[:, ti * NBF:(ti + 1) * NBF], psD[rr][:, :], AF.Exp, bias=colk_all[r][:, ti:ti + 1])
            if ti % 4 == 3:
                yield
        for mb in range(3):
            for ti in range(NT):
                Te.mm(psCtx[0:MBW[mb], mb * (DH + 1):(mb + 1) * (DH + 1)],
                      kp_all[:, ti * NBF + mb * 128: ti * NBF + mb * 128 + MBW[mb]],
                      v_w[r][:, ti * (DH + 1): (ti + 1) * (DH + 1)],
                      start=(ti == 0), stop=False)
            Te.mm(psCtx[0:MBW[mb], mb * (DH + 1):(mb + 1) * (DH + 1)],
                  ones_rowb[0:1, 0:MBW[mb]], vsum_sb[r][:, :], start=False, stop=True)
        for mb in range(3):
            A_.act(ctx_b[0:MBW[mb], mb * (DH + 1):(mb + 1) * (DH + 1)], psCtx[0:MBW[mb], mb * (DH + 1):(mb + 1) * (DH + 1)], AF.Copy)
        for mb in range(3):
            Te.mm(psD[0][0:1, 0:DH + 1], ones_colb[0:MBW[mb], 0:1], ctx_b[0:MBW[mb], mb * (DH + 1):(mb + 1) * (DH + 1)], start=(mb == 0), stop=(mb == 2))
        V.ts(csum_sb[:, :], psD[0][0:1, 0:DH + 1], PERF_EPS)
        yield
        Te.mm(psD[0][:, :], q_b[r][:, 0:128], projb[:, :], start=True, stop=True)
        for ti in range(NT):
            rr = ti % 2
            V.reduce(col_a[:, 0:1], psD[rr][:, :], ALU.max, negate=True)
            V.tt(col_b[rr][:, 0:1], col_a[:, 0:1], diag_q[r][:, ti:ti + 1], ALU.subtract)
            A_.act(qkp_b[rr][:, :], psD[rr][:, :], AF.Exp, bias=col_b[rr][:, 0:1])
            if ti + 1 < NT:
                Te.mm(psD[(ti + 1) % 2][:, :], q_b[r][:, (ti + 1) * 128:(ti + 2) * 128], projb[:, :], start=True, stop=True)
            yield
            for mb in range(3):
                reg = mb * 128
                Te.transpose(psT_bf[0:MBW[mb], reg:reg + 128], qkp_b[rr][:, mb * 128: mb * 128 + MBW[mb]], identb[:, :])
                A_.act(qp_fm[0:MBW[mb], mb * T + ti * 128: mb * T + (ti + 1) * 128], psT_bf[0:MBW[mb], reg:reg + 128], AF.Copy)
        def psO_group(ti):
            rr = ti % 2
            for mb in range(3):
                Te.mm(psO[rr][:, :], qp_fm[0:MBW[mb], mb * T + ti * 128: mb * T + (ti + 1) * 128],
                      ctx_b[0:MBW[mb], mb * (DH + 1):(mb + 1) * (DH + 1)], start=(mb == 0), stop=False)
            Te.mm(psO[rr][:, :], ones_rowb[:, :], csum_sb[:, :], start=False, stop=True)
        psO_group(0)
        ho2 = [ho_t[:, :], ho_t[:, :]]
        for ti in range(NT):
            rr = ti % 2
            if ti + 1 < NT:
                psO_group(ti + 1)
            V.recip(col_a[:, 0:1], psO[rr][:, DH:DH + 1])
            V.ts(ho_t[:, :], psO[rr][:, 0:DH], col_a[:, 0:1])
            reg = (ti % 3) * 128
            Te.transpose(psT_bf[0:DH, reg:reg + 128], ho_t[:, :], identb[:, :])
            V.copy(attn_fm[(h % 2) * DH:(h % 2) * DH + DH, (h // 2) * T + ti * 128:(h // 2) * T + (ti + 1) * 128], psT_bf[0:DH, reg:reg + 128])
            if ti % 4 == 3:
                yield

    for s in range(SEQ_PER_CORE):
        c0s = s * T
        for j in range(NJ):
            L.dma(xn1b[:, j * T:(j + 1) * T], XN1[j * 128:(j + 1) * 128, c0s:c0s + T])
        for _ in emit_qkv(s, 0):
            pass
        for h in range(NHEADS):
            gens = []
            if h + 1 < NHEADS:
                gens.append(emit_qkv(s, h + 1))
            gens.append(emit_post(s, h))
            live = list(gens)
            while live:
                for g in list(live):
                    try:
                        next(g)
                    except StopIteration:
                        live.remove(g)
        for cc in range(T // CHUNK):
            for k in range(NJ):
                ps = psA if k % 2 == 0 else psB
                for j in range(NJ):
                    Te.mm(ps[:, :], wo(j, k), attn_fm[:, j * T + cc * CHUNK: j * T + (cc + 1) * CHUNK], start=(j == 0), stop=(j == NJ - 1))
                A_.act(ao[k % 2][:, :], ps[:, :], AF.Identity, bias=bo[:, k:k + 1])
                G.dma(ATT[k * 128:(k + 1) * 128, c0s + cc * CHUNK: c0s + (cc + 1) * CHUNK], ao[k % 2][:, :], accum=True)


def emit_final(P, engs, st, terms, OUT):
    nc = P.nc
    V, A_, Te, L, G = engs["V"], engs["A"], engs["T"], engs["L"], engs["G"]
    C = CHUNK
    nterm = len(terms)
    tin_w = [st.enter_context(nc.sbuf_tensor(f"o_tin{r}", [128, 3 * C], F32)) for r in range(2)]
    tin = [[tin_w[r][:, i * C:(i + 1) * C] for i in range(nterm)] for r in range(2)]
    po = [st.enter_context(nc.sbuf_tensor(f"o_po{r}", [128, C], F32)) for r in range(2)]
    for ch in range(NCH):
        c0 = ch * C
        for j in range(NJ):
            r = j % 2
            for i, t in enumerate(terms):
                L.dma(tin[r][i], t[j * 128:(j + 1) * 128, c0:c0 + C])
            V.tt(po[r][:], tin[r][0], tin[r][1], ALU.add)
            for i in range(2, nterm):
                V.tt(po[r][:], po[r][:], tin[r][i], ALU.add)
            G.dma(OUT[j * 128:(j + 1) * 128, c0:c0 + C], po[r][:])



def build_program(debug=False):
    nc = bass.Bass("TRN2", target_bir_lowering=False)
    stack = ExitStack()
    P = Prog(nc, stack)
    engs = make_engs(P)

    x0f = nc.dram_tensor("x0f", [D, NTOK], F32, kind="ExternalInput").ap()
    hin = {}
    for name, shape in [
        ("m_w1t", [D, D]), ("m_w2t", [D, D]), ("m_wlt", [D, D]), ("m_dmt", [D, D]),
        ("m_dwt", [D, H]), ("m_bwt", [D, H]), ("m_cwt", [H, D]),
        ("m_normw", [128, NJ]), ("m_convw", [128, NJ * 3]), ("m_convb", [128, NJ]),
        ("m_A", [H, 1]), ("m_db", [H, 1]), ("m_Bb", [H, 1]),
        ("m_Cb", [128, NJ]), ("m_Db", [128, NJ]),
    ]:
        hin[name] = nc.dram_tensor(name, shape, F32, kind="ExternalInput").ap()
    for name, shape in [
        ("f_gwt", [D, E]), ("f_w1t", [E, D, D]), ("f_w3t", [E, D, D]), ("f_w2t", [E, D, D]),
        ("f_b1", [128, E * NJ]), ("f_b2", [128, E * NJ]), ("f_b3", [128, E * NJ]),
        ("f_sw1t", [D, I2]), ("f_sw2t", [D, I2]), ("f_sw3t", [I2, D]),
        ("a_wq", [D, D]), ("a_wk", [D, D]), ("a_wv", [D, D]), ("a_wo", [D, D]),
        ("a_bo", [128, NJ]), ("a_projt", [DH, NBF]),
        ("r_wrt", [D, D]), ("r_wkt", [D, D]), ("r_wvt", [D, D]),
        ("r_mur", [128, NJ]), ("r_muk", [128, NJ]),
    ]:
        hin[name] = nc.dram_tensor(name, shape, F32, kind="ExternalInput").ap()
    out_fm = nc.dram_tensor("out_fm", [D, NTOK], F32, kind="ExternalOutput").ap()
    XN0 = dram_io(nc, "XN0", [D, NTOK], F32, debug)
    XN0B = nc.dram_tensor("XN0B", [D, NTOK], BF16, kind="Internal").ap()
    ACC0 = dram_io(nc, "ACC0", [D, NTOK], F32, debug)
    XN1 = dram_io(nc, "XN1", [D, NTOK], F32, debug)
    XN1B = nc.dram_tensor("XN1B", [D, NTOK], BF16, kind="Internal").ap()

    with stack:
        with ExitStack() as st_g:
            sm_all = st_g.enter_context(nc.sbuf_tensor("sm_all", [8, NTOK], F32))
            with ExitStack() as st:
                emit_ln(P, engs, st, x0f, XN0, dst_b=XN0B, gw=hin["f_gwt"], sm_all=sm_all)
            P.stage_barrier()
            with ExitStack() as st:
                emit_mamba(P, engs, st, hin, XN0B, ACC0, dbg={} if debug else None)
            P.stage_barrier()
            with ExitStack() as st:
                emit_moe(P, engs, st, hin, sm_all, XN0B, ACC0, dbg={} if debug else None)
        P.stage_barrier()
        with ExitStack() as st:
            emit_shared_ln(P, engs, st, hin, XN0B, [x0f, XN0, ACC0], out_fm, XN1, XN1B)
        P.stage_barrier()
        with ExitStack() as st:
            emit_rwkv(P, engs, st, hin, XN1, out_fm)
        P.stage_barrier()
        with ExitStack() as st:
            emit_performer(P, engs, st, hin, XN1B, out_fm)
        P.stage_barrier()

        P.stage_barrier()
    return nc


def kernel(**inputs):
    maps = stage_inputs(inputs)
    nc = build_program()
    res = run_bass_kernel_spmd(nc, maps, core_ids=list(range(NCORES)))
    return assemble_output(res.results)


# version 54 (final)
# speedup vs baseline: 1.0161x; 1.0161x over previous
import numpy as np
import ml_dtypes
from contextlib import ExitStack

import concourse.bass as bass
import concourse.mybir as mybir
from concourse.bass_utils import run_bass_kernel_spmd

F32 = mybir.dt.float32
BF16 = mybir.dt.bfloat16
AF = mybir.ActivationFunctionType
ALU = mybir.AluOpType
AX = mybir.AxisListType

NCORES = 8
B, T, D = 16, 2048, 1024
SEQ_PER_CORE = B // NCORES
NTOK = SEQ_PER_CORE * T
NJ = D // 128
CHUNK = 512
NCH = NTOK // CHUNK
H = 64
E = 8
I2 = 2048
NI = I2 // 128
NBF = 266
NHEADS = 16
DH = 64
LN_EPS = 1e-5
RMS_EPS = float(np.finfo(np.float32).eps)
PERF_EPS = 1e-4
PERF_RATIO = float(NBF) ** -0.5
PERF_DN = float(DH) ** -0.25

BF = ml_dtypes.bfloat16



def _pcol(v):
    return np.ascontiguousarray(v.reshape(NJ, 128).T)


def stage_inputs(inputs):
    x = inputs["L0_attn_x"]
    g = {}
    g["m_w1t"] = np.ascontiguousarray(inputs["L0_attn_W1"].T)
    g["m_w2t"] = np.ascontiguousarray(inputs["L0_attn_W2"].T)
    g["m_wlt"] = np.ascontiguousarray(inputs["L0_attn_Wlast"].T)
    g["m_dmt"] = np.ascontiguousarray(inputs["L0_attn_Dw"].T)
    g["m_dwt"] = np.ascontiguousarray(inputs["L0_attn_dw"].T)
    g["m_bwt"] = np.ascontiguousarray(inputs["L0_attn_Bw"].T)
    g["m_cwt"] = np.ascontiguousarray(inputs["L0_attn_Cw"].T)
    g["m_normw"] = _pcol(inputs["L0_attn_norm_w"])
    g["m_convw"] = np.ascontiguousarray(
        inputs["L0_attn_conv_w"].reshape(NJ, 128, 3).transpose(1, 0, 2).reshape(128, NJ * 3))
    g["m_convb"] = _pcol(inputs["L0_attn_conv_b"])
    g["m_A"] = np.ascontiguousarray(inputs["L0_attn_A"].reshape(H, 1))
    g["m_db"] = np.ascontiguousarray(inputs["L0_attn_db"].reshape(H, 1))
    g["m_Bb"] = np.ascontiguousarray(inputs["L0_attn_Bb"].reshape(H, 1))
    g["m_Cb"] = _pcol(inputs["L0_attn_Cb"])
    g["m_Db"] = _pcol(inputs["L0_attn_Db"])
    g["f_gwt"] = np.ascontiguousarray(inputs["L0_ffn_gate_w"].T)
    g["f_w1t"] = np.ascontiguousarray(inputs["L0_ffn_w1"].transpose(0, 2, 1))
    g["f_w3t"] = np.ascontiguousarray(inputs["L0_ffn_w3"].transpose(0, 2, 1))
    g["f_w2t"] = np.ascontiguousarray(inputs["L0_ffn_w2"].transpose(0, 2, 1))
    g["f_b1"] = np.ascontiguousarray(inputs["L0_ffn_b1"].reshape(E, NJ, 128).transpose(2, 0, 1).reshape(128, E * NJ))
    g["f_b2"] = np.ascontiguousarray(inputs["L0_ffn_b2"].reshape(E, NJ, 128).transpose(2, 0, 1).reshape(128, E * NJ))
    g["f_b3"] = np.ascontiguousarray(inputs["L0_ffn_b3"].reshape(E, NJ, 128).transpose(2, 0, 1).reshape(128, E * NJ))
    g["f_sw1t"] = np.ascontiguousarray(inputs["L0_ffn_sw1"].T)
    g["f_sw2t"] = np.ascontiguousarray(inputs["L0_ffn_sw2"].T)
    g["f_sw3t"] = np.ascontiguousarray(inputs["L0_ffn_sw3"].T)
    g["a_wq"] = np.ascontiguousarray(inputs["L1_attn_Wq"])
    g["a_wk"] = np.ascontiguousarray(inputs["L1_attn_Wk"])
    g["a_wv"] = np.ascontiguousarray(inputs["L1_attn_Wv"])
    g["a_wo"] = np.ascontiguousarray(inputs["L1_attn_Wo"])
    g["a_bo"] = _pcol(inputs["L1_attn_bo"])
    g["a_projt"] = np.ascontiguousarray(inputs["L1_attn_proj"].T)
    g["r_wrt"] = np.ascontiguousarray(inputs["L1_ffn_W_r"].T)
    g["r_wkt"] = np.ascontiguousarray(inputs["L1_ffn_W_k"].T)
    g["r_wvt"] = np.ascontiguousarray(inputs["L1_ffn_W_v"].T)
    g["r_mur"] = _pcol(inputs["L1_ffn_mu_r"])
    g["r_muk"] = _pcol(inputs["L1_ffn_mu_k"])

    maps = []
    for c in range(NCORES):
        m = dict(g)
        shard = x[c * SEQ_PER_CORE:(c + 1) * SEQ_PER_CORE].reshape(NTOK, D)
        m["x0f"] = np.ascontiguousarray(shard.T)
        maps.append(m)
    return maps


def assemble_output(results):
    outs = [r["out_fm"] for r in results]
    full = np.concatenate([o.T.reshape(SEQ_PER_CORE, T, D) for o in outs], axis=0)
    return np.ascontiguousarray(full.astype(np.float32))



def _key(ap):
    try:
        return ap.tensor.name
    except AttributeError:
        return None


class Prog:
    K_FENCE = 4

    def __init__(self, nc, stack):
        self.nc = nc
        self.stack = stack
        self._sems = {}
        self.writer = {}
        self.readers = {}
        self.waited = {}
        self.engines = {}

    def s(self, name):
        h = self._sems.get(name)
        if h is None:
            assert len(self._sems) < 100, "semaphore budget exceeded"
            h = [self.stack.enter_context(self.nc.semaphore(name)), 0]
            self._sems[name] = h
        return h

    def ensure_tok(self, rec):
        if rec["tok"] is not None:
            return rec["sem"], rec["tok"]
        eng = self.engines.get(rec["eng"])
        if eng is not None and eng.use_milestones:
            import bisect
            seqs = [m[0] for m in eng.milestones]
            i = bisect.bisect_left(seqs, rec["seq"])
            if i < len(eng.milestones):
                rec["tok"] = eng.milestones[i][1]
                return rec["sem"], rec["tok"]
            h = self.s(rec["sem"])
            eng.last_instr.then_inc(h[0], rec["step"])
            h[1] += rec["step"]
            eng.milestones.append((eng.seq, h[1]))
            rec["tok"] = h[1]
            return rec["sem"], rec["tok"]
        h = self.s(rec["sem"])
        rec["instr"].then_inc(h[0], rec["step"])
        h[1] += rec["step"]
        rec["tok"] = h[1]
        return rec["sem"], rec["tok"]

    def raw_wait(self, tag, raw_eng, sem, tok):
        w = self.waited.setdefault(tag, {})
        if w.get(sem, 0) < tok:
            raw_eng.wait_ge(self.s(sem)[0], tok)
            w[sem] = tok

    def stage_barrier(self):
        nc = self.nc
        for name, h in sorted(self._sems.items()):
            if name.startswith("D:"):
                self.raw_wait("G", nc.gpsimd, name, h[1])
        nc.all_engine_barrier()
        self.writer.clear()
        self.readers.clear()
        for tag in list(self.engines):
            w = self.waited.setdefault(tag, {})
            for name, h in self._sems.items():
                w[name] = h[1]


class Eng:
    def __init__(self, P, raw, tag, sem, step=1, use_milestones=False):
        self.P = P
        self.raw = raw
        self.tag = tag
        self.sem = sem
        self.step = step
        self.seq = 0
        self.use_milestones = use_milestones
        self.milestones = []
        self.last_instr = None
        P.engines[tag] = self

    def _edge(self, rec):
        if rec.get("dma"):
            sem = rec["sem"]
            self.P.raw_wait(self.tag, self.raw, sem, self.P.s(sem)[1])
        else:
            sem, tok = self.P.ensure_tok(rec)
            self.P.raw_wait(self.tag, self.raw, sem, tok)

    def _need(self, rec):
        if rec["eng"] != self.tag:
            return True
        if rec.get("dma") or self.step == 16:
            return True
        if self.tag == "T":
            return False
        return self.seq - rec["seq"] <= self.P.K_FENCE

    def _pre(self, reads, writes):
        P = self.P
        for k in reads:
            if k is None:
                continue
            rec = P.writer.get(k)
            if rec is not None and self._need(rec):
                self._edge(rec)
        for k in writes:
            if k is None:
                continue
            rec = P.writer.get(k)
            if rec is not None and self._need(rec):
                self._edge(rec)
            for rdr in P.readers.get(k, ()):
                if self._need(rdr):
                    self._edge(rdr)

    def _post(self, ins, reads, writes, sem=None, dma=False):
        P = self.P
        self.seq += 1
        rec = {"instr": ins, "eng": self.tag, "sem": sem or self.sem, "step": self.step,
               "tok": None, "seq": self.seq, "dma": dma}
        self.last_instr = ins
        if not self.use_milestones:
            P.ensure_tok(rec)
        for k in writes:
            if k is None:
                continue
            P.writer[k] = rec
            P.readers[k] = []
        for k in reads:
            if k is None or k in writes:
                continue
            P.readers.setdefault(k, []).append(rec)
        return rec

    def emit(self, fn, reads, writes, sem=None, dma=False):
        rkeys = [_key(a) for a in reads]
        wkeys = [_key(a) for a in writes]
        self._pre(rkeys, wkeys)
        ins = fn()
        return self._post(ins, rkeys, wkeys, sem=sem, dma=dma)

    def tokenize(self, rec):
        return self.P.ensure_tok(rec)

    def tt(self, out, in0, in1, op):
        return self.emit(lambda: self.raw.tensor_tensor(out=out, in0=in0, in1=in1, op=op), [in0, in1], [out])

    def ts(self, out, in0, s1, s2=None, op0=ALU.mult, op1=ALU.bypass, accum=None):
        outs = [out] + ([accum] if accum is not None else [])
        return self.emit(lambda: self.raw.tensor_scalar(
            out=out, in0=in0, scalar1=s1, scalar2=s2, op0=op0, op1=op1,
            accum_out=accum), [in0, s1, s2], outs)

    def stt(self, out, in0, scalar, in1, op0, op1, accum=None):
        outs = [out] + ([accum] if accum is not None else [])
        return self.emit(lambda: self.raw.scalar_tensor_tensor(
            out=out, in0=in0, scalar=scalar, in1=in1, op0=op0, op1=op1, accum_out=accum),
            [in0, in1, scalar], outs)

    def copy(self, out, in_):
        return self.emit(lambda: self.raw.tensor_copy(out=out, in_=in_), [in_], [out])

    def memset(self, ap, v):
        return self.emit(lambda: self.raw.memset(ap, v), [], [ap])

    def scan(self, out, d0, d1, init, op0, op1):
        return self.emit(lambda: self.raw.tensor_tensor_scan(
            out=out, data0=d0, data1=d1, initial=init, op0=op0, op1=op1),
            [d0, d1, init if not isinstance(init, float) else None], [out])

    def reduce(self, out, in_, op, negate=False):
        return self.emit(lambda: self.raw.tensor_reduce(
            out=out, in_=in_, axis=AX.X, op=op, negate=negate), [in_], [out])

    def recip(self, out, in_):
        return self.emit(lambda: self.raw.reciprocal(out=out, in_=in_), [in_], [out])

    def select(self, out, mask, a, b):
        return self.emit(lambda: self.raw.select(out=out, mask=mask, on_true=a, on_false=b), [mask, a, b], [out])

    def act(self, out, in_, func, bias=0.0, scale=1.0, accum=None):
        outs = [out] + ([accum] if accum is not None else [])
        return self.emit(lambda: self.raw.activation(
            out=out, in_=in_, func=func, bias=bias, scale=scale, accum_out=accum),
            [in_, bias, scale], outs)

    def mm(self, out, lhsT, rhs, start, stop):
        return self.emit(lambda: self.raw.matmul(out, lhsT, rhs, start=start, stop=stop),
                         [lhsT, rhs], [out])

    def transpose(self, out, in_, ident):
        return self.emit(lambda: self.raw.transpose(out, in_, ident), [in_, ident], [out])

    def dma(self, out, in_, semname=None, accum=False, disjoint=False, unordered=False):
        sem = semname or ("D:" + _key(out))
        if accum:
            fn = lambda: self.raw.dma_start(out=out, in_=in_, accum_op=ALU.add)
        else:
            fn = lambda: self.raw.dma_start(out=out, in_=in_)
        reads = [in_] if (accum and unordered) else ([in_, out] if accum else [in_])
        if disjoint or unordered:
            rkeys = [_key(a) for a in reads]
            self._pre(rkeys, [])
            ins = fn()
            rec = self._post(ins, rkeys, [_key(out)], sem=sem, dma=True)
        else:
            rec = self.emit(fn, reads, [out], sem=sem, dma=True)
        self.P.ensure_tok(rec)
        return rec

    def nop_rec(self):
        return self.emit(lambda: self.raw.nop(), [], [])


def make_engs(P):
    nc = P.nc
    return {
        "V": Eng(P, nc.vector, "V", "V"),
        "A": Eng(P, nc.scalar, "A", "A"),
        "T": Eng(P, nc.tensor, "T", "P", use_milestones=True),
        "L": Eng(P, nc.sync, "L", "L", step=16),
        "G": Eng(P, nc.gpsimd, "G", "G", step=16),
        "Q": Eng(P, nc.gpsimd, "Q", "Q"),
    }


def dram_io(nc, name, shape, dtype, debug):
    kind = "ExternalOutput" if debug else "Internal"
    return nc.dram_tensor(name, shape, dtype, kind=kind).ap()



def load_lhsT_weight(P, G, st, name, hbm, KK, MM):
    nc = P.nc
    njk = KK // 128
    sb = st.enter_context(nc.sbuf_tensor(name, [128, njk * MM], BF16))
    for j in range(njk):
        G.dma(sb[:, j * MM:(j + 1) * MM], hbm[j * 128:(j + 1) * 128, :])

    def sl(j, k, w=128):
        return sb[:, j * MM + k * w: j * MM + k * w + w]
    return sb, sl


def load_f32_tile(P, L, st, name, hbm, shape):
    sb = st.enter_context(P.nc.sbuf_tensor(name, list(shape), F32))
    L.dma(sb[:, :], hbm[:, :])
    return sb



def emit_ln(P, engs, st, src_hbm, dst_hbm, dst_b=None, gw=None, sm_all=None):
    nc = P.nc
    V, A_, Te, L, G = engs["V"], engs["A"], engs["T"], engs["L"], engs["G"]
    if gw is not None:
        gw_f = st.enter_context(nc.sbuf_tensor("ln_gw_f", [128, NJ * E], F32))
        for j in range(NJ):
            L.dma(gw_f[:, j * E:(j + 1) * E], gw[j * 128:(j + 1) * 128, :])
        psG = st.enter_context(nc.psum_tensor("ln_psG", [8, CHUNK], F32))
        psT = st.enter_context(nc.psum_tensor("ln_psT", [128, 8], F32))
        psTb = st.enter_context(nc.psum_tensor("ln_psTb", [8, 128], F32))
        identg = st.enter_context(nc.sbuf_tensor("ln_identg", [128, 128], F32))
        L.dma(identg[:, :], nc.inline_tensor(np.eye(128, dtype=np.float32), "ln_identg_d").ap()[:, :])
        lg_loc = [st.enter_context(nc.sbuf_tensor(f"ln_lg{r}", [8, CHUNK], F32)) for r in range(2)]
        ge_f = st.enter_context(nc.sbuf_tensor("ln_ge", [128, 8], F32))
        gsc = st.enter_context(nc.sbuf_tensor("ln_gsc", [128, 8], F32))
        gscm = st.enter_context(nc.sbuf_tensor("ln_gscm", [128, 8], F32))
        gmask = st.enter_context(nc.sbuf_tensor("ln_gmask", [128, 8], F32))
        gsm = st.enter_context(nc.sbuf_tensor("ln_gsm", [128, 8], F32))
        gca = st.enter_context(nc.sbuf_tensor("ln_gca", [128, 1], F32))
        gcb = st.enter_context(nc.sbuf_tensor("ln_gcb", [128, 1], F32))
        gcc = st.enter_context(nc.sbuf_tensor("ln_gcc", [128, 1], F32))
    xin_w = [st.enter_context(nc.sbuf_tensor(f"ln_xin{r}", [128, NJ * CHUNK], F32)) for r in range(2)]
    xnf_w = [st.enter_context(nc.sbuf_tensor(f"ln_xnf{r}", [128, NJ * CHUNK], F32)) for r in range(2)]
    xin = [[xin_w[r][:, j * CHUNK:(j + 1) * CHUNK] for j in range(NJ)] for r in range(2)]
    xnf = [[xnf_w[r][:, j * CHUNK:(j + 1) * CHUNK] for j in range(NJ)] for r in range(2)]
    xb16 = [st.enter_context(nc.sbuf_tensor(f"ln_xb16_{j}", [128, CHUNK], BF16)) for j in range(NJ)]
    xsq = [st.enter_context(nc.sbuf_tensor(f"ln_xsq_{j}", [128, CHUNK], BF16)) for j in range(NJ)]
    ones_bf = st.enter_context(nc.sbuf_tensor("ln_ones", [128, 1], BF16))
    ones_row = st.enter_context(nc.sbuf_tensor("ln_ones_row", [1, 128], F32))
    ps_sum = st.enter_context(nc.psum_tensor("ln_ps_sum", [1, CHUNK], F32))
    ps_sq = st.enter_context(nc.psum_tensor("ln_ps_sq", [1, CHUNK], F32))
    ps_r = st.enter_context(nc.psum_tensor("ln_ps_r", [128, CHUNK], F32))
    ps_mr = st.enter_context(nc.psum_tensor("ln_ps_mr", [128, CHUNK], F32))
    rows = [st.enter_context(nc.sbuf_tensor(f"ln_row{i}", [1, CHUNK], F32)) for i in range(4)]
    m_row, v_row, r_row, mr_row = rows
    V.memset(ones_bf[:, :], 1.0)
    V.memset(ones_row[:, :], 1.0)
    for ch in range(NCH):
        sl = ch % 2
        c0 = ch * CHUNK
        for j in range(NJ):
            L.dma(xin[sl][j], src_hbm[j * 128:(j + 1) * 128, c0:c0 + CHUNK])
        for j in range(NJ):
            A_.act(xb16[j][:], xin[sl][j], AF.Copy)
            A_.act(xsq[j][:], xb16[j][:], AF.Square)
        for j in range(NJ):
            Te.mm(ps_sum[0:1, :], ones_bf[:, 0:1], xb16[j][:], start=(j == 0), stop=(j == NJ - 1))
        for j in range(NJ):
            Te.mm(ps_sq[0:1, :], ones_bf[:, 0:1], xsq[j][:], start=(j == 0), stop=(j == NJ - 1))
        V.ts(m_row[:], ps_sum[0:1, :], 1.0 / D)
        V.ts(v_row[:], ps_sq[0:1, :], 1.0 / D, LN_EPS, ALU.mult, ALU.add)
        V.tt(mr_row[:], m_row[:], m_row[:], ALU.mult)
        V.tt(v_row[:], v_row[:], mr_row[:], ALU.subtract)
        A_.act(r_row[:], v_row[:], AF.Ln)
        A_.act(r_row[:], r_row[:], AF.Exp, scale=-0.5)
        V.tt(mr_row[:], m_row[:], r_row[:], ALU.mult)
        Te.mm(ps_r[:, :], ones_row[:, :], r_row[:], start=True, stop=True)
        Te.mm(ps_mr[:, :], ones_row[:, :], mr_row[:], start=True, stop=True)
        for j in range(NJ):
            V.tt(xnf[sl][j], xin[sl][j], ps_r[:, :], ALU.mult)
            V.tt(xnf[sl][j], xnf[sl][j], ps_mr[:, :], ALU.subtract)
        for j in range(NJ):
            G.dma(dst_hbm[j * 128:(j + 1) * 128, c0:c0 + CHUNK], xnf[sl][j], disjoint=True)
            if dst_b is not None:
                G.dma(dst_b[j * 128:(j + 1) * 128, c0:c0 + CHUNK], xnf[sl][j], disjoint=True)
        if gw is not None:
            for j in range(NJ):
                Te.mm(psG[0:8, :], gw_f[:, j * E:(j + 1) * E], xnf[sl][j], start=(j == 0), stop=(j == NJ - 1))
            A_.act(lg_loc[sl][:, :], psG[0:8, :], AF.Copy)
            for i in range(CHUNK // 128):
                Te.transpose(psT[:, 0:8], lg_loc[sl][:, i * 128:(i + 1) * 128], identg[0:8, 0:8])
                V.reduce(gca[:, 0:1], psT[:, 0:8], ALU.max, negate=True)
                A_.act(ge_f[:, :], psT[:, 0:8], AF.Exp, bias=gca[:, 0:1], accum=gcb[:, 0:1])
                V.recip(gcc[:, 0:1], gcb[:, 0:1])
                V.ts(gsc[:, :], ge_f[:, :], gcc[:, 0:1], None, ALU.mult)
                V.reduce(gca[:, 0:1], gsc[:, :], ALU.max)
                V.ts(gmask[:, :], gsc[:, :], gca[:, 0:1], None, ALU.is_ge)
                V.stt(gscm[:, :], gmask[:, :], -1e30, gsc[:, :], ALU.mult, ALU.add)
                V.reduce(gcb[:, 0:1], gscm[:, :], ALU.max)
                V.ts(gmask[:, :], gsc[:, :], gcb[:, 0:1], None, ALU.is_ge)
                V.tt(gsm[:, :], gsc[:, :], gmask[:, :], ALU.mult)
                Te.transpose(psTb[0:8, :], gsm[:, :], identg[:, :])
                V.copy(sm_all[:, c0 + i * 128:c0 + (i + 1) * 128], psTb[0:8, :])



def emit_mamba(P, engs, st, hin, XN0, MAM, dbg=None):
    nc = P.nc
    V, A_, Te, L, G = engs["V"], engs["A"], engs["T"], engs["L"], engs["G"]
    C = CHUNK

    def dump(gc, tile, name, rows):
        if dbg is None or gc != 0:
            return
        if name not in dbg:
            dbg[name] = nc.dram_tensor("DBG_" + name, [rows, C], F32, kind="ExternalOutput").ap()
        G.dma(dbg[name][:, :], tile, semname="D:DBG")

    _, w1 = load_lhsT_weight(P, G, st, "m_w1", hin["m_w1t"], D, D)
    _, w2 = load_lhsT_weight(P, G, st, "m_w2", hin["m_w2t"], D, D)
    _, dm = load_lhsT_weight(P, G, st, "m_dm", hin["m_dmt"], D, D)
    _, wl = load_lhsT_weight(P, G, st, "m_wl", hin["m_wlt"], D, D)
    _, dw = load_lhsT_weight(P, G, st, "m_dw", hin["m_dwt"], D, H)
    _, bw = load_lhsT_weight(P, G, st, "m_bw", hin["m_bwt"], D, H)
    cw_sb = st.enter_context(nc.sbuf_tensor("m_cw", [H, D], BF16))
    G.dma(cw_sb[:, :], hin["m_cwt"][:, :])
    normw = load_f32_tile(P, L, st, "m_normw_sb", hin["m_normw"], (128, NJ))
    convw = load_f32_tile(P, L, st, "m_convw_sb", hin["m_convw"], (128, NJ * 3))
    convb = load_f32_tile(P, L, st, "m_convb_sb", hin["m_convb"], (128, NJ))
    A_sb = load_f32_tile(P, L, st, "m_A_sb", hin["m_A"], (H, 1))
    db_sb = load_f32_tile(P, L, st, "m_db_sb", hin["m_db"], (H, 1))
    Bb_sb = load_f32_tile(P, L, st, "m_Bb_sb", hin["m_Bb"], (H, 1))
    Cb_sb = load_f32_tile(P, L, st, "m_Cb_sb", hin["m_Cb"], (128, NJ))
    Db_sb = load_f32_tile(P, L, st, "m_Db_sb", hin["m_Db"], (128, NJ))
    ones_row = st.enter_context(nc.sbuf_tensor("m_ones_row", [1, 128], F32))
    ones_bf = st.enter_context(nc.sbuf_tensor("m_ones_bf", [128, 1], BF16))

    xn0b_w = [st.enter_context(nc.sbuf_tensor(f"m_xn0b{r}", [128, NJ * C], BF16)) for r in range(2)]
    xn0b = [[xn0b_w[r][:, j * C:(j + 1) * C] for j in range(NJ)] for r in range(2)]
    xn2b = [st.enter_context(nc.sbuf_tensor(f"m_xn2b_{j}", [128, C], BF16)) for j in range(NJ)]
    xbf = [st.enter_context(nc.sbuf_tensor(f"m_xbf_{j}", [128, C + 2], F32)) for j in range(NJ)]
    xcb = [st.enter_context(nc.sbuf_tensor(f"m_xcb_{j}", [128, C], BF16)) for j in range(NJ)]
    zsf = [st.enter_context(nc.sbuf_tensor(f"m_zsf_{j}", [128, C], F32)) for j in range(NJ)]
    gb = [st.enter_context(nc.sbuf_tensor(f"m_gb_{j}", [128, C], BF16)) for j in range(NJ)]
    yf = st.enter_context(nc.sbuf_tensor("m_yf", [128, C], F32))
    cs = [st.enter_context(nc.sbuf_tensor(f"m_cs_{r}", [128, C], F32)) for r in range(2)]
    mo = [st.enter_context(nc.sbuf_tensor(f"m_mo_{r}", [128, C], F32)) for r in range(2)]
    A_log = st.enter_context(nc.sbuf_tensor("m_A_log", [H, 1], F32))
    relu_A = st.enter_context(nc.sbuf_tensor("m_relu_A", [H, 1], F32))
    sp_t = st.enter_context(nc.sbuf_tensor("m_sp_t", [H, C], F32))
    u_d = st.enter_context(nc.sbuf_tensor("m_u_d", [H, C], F32))
    dlt = st.enter_context(nc.sbuf_tensor("m_dlt", [H, C], F32))
    abl = st.enter_context(nc.sbuf_tensor("m_abl", [H, C], F32))
    bbar = st.enter_context(nc.sbuf_tensor("m_bbar", [H, C], F32))
    zero_h = st.enter_context(nc.sbuf_tensor("m_zero_h", [H, C], F32))
    acum_raw = st.enter_context(nc.sbuf_tensor("m_acum_raw", [H, C], F32))
    acum_ext = st.enter_context(nc.sbuf_tensor("m_acum_ext", [H, C + 1], F32))
    d_t = st.enter_context(nc.sbuf_tensor("m_d_t", [H, C], F32))
    g_t = st.enter_context(nc.sbuf_tensor("m_g_t", [H, C], F32))
    h_t = st.enter_context(nc.sbuf_tensor("m_h_t", [H, C], F32))
    hb = st.enter_context(nc.sbuf_tensor("m_hb", [H, C], BF16))
    carryA = st.enter_context(nc.sbuf_tensor("m_carryA", [H, 1], F32))
    carryH = st.enter_context(nc.sbuf_tensor("m_carryH", [H, 1], F32))
    rms_row = st.enter_context(nc.sbuf_tensor("m_rms_row", [1, C], F32))
    xsqb = [st.enter_context(nc.sbuf_tensor(f"m_xsqb_{j}", [128, C], BF16)) for j in range(NJ)]

    psS = st.enter_context(nc.psum_tensor("m_psS", [1, C], F32))
    psB = st.enter_context(nc.psum_tensor("m_psB", [128, C], F32))
    ps = [st.enter_context(nc.psum_tensor(f"m_ps{i}", [128, C], F32)) for i in range(4)]

    V.memset(ones_row[:, :], 1.0)
    V.memset(ones_bf[:, :], 1.0)
    V.memset(zero_h[:, :], 0.0)
    A_.act(A_log[:, :], A_sb[:, :], AF.Abs)
    A_.act(A_log[:, :], A_log[:, :], AF.Exp, scale=-1.0)
    A_.act(A_log[:, :], A_log[:, :], AF.Ln, bias=1.0)
    V.ts(relu_A[:, :], A_sb[:, :], 0.0, None, ALU.max)
    V.tt(A_log[:, :], A_log[:, :], relu_A[:, :], ALU.add)
    V.ts(A_log[:, :], A_log[:, :], -1.0, None, ALU.mult)
    V.tt(Cb_sb[:, :], Cb_sb[:, :], Db_sb[:, :], ALU.add)

    for s in range(SEQ_PER_CORE):
        V.memset(carryA[:, :], 0.0)
        V.memset(carryH[:, :], 0.0)
        V.memset(acum_ext[:, 0:1], 0.0)
        for j in range(NJ):
            V.memset(xbf[j][:, 0:2], 0.0)
        for ch in range(T // C):
            gc = s * (T // C) + ch
            sl = gc % 2
            c0 = s * T + ch * C
            for j in range(NJ):
                L.dma(xn0b[sl][j], XN0[j * 128:(j + 1) * 128, c0:c0 + C])
            for j in range(NJ):
                A_.act(xsqb[j][:], xn0b[sl][j], AF.Square)
            for j in range(NJ):
                Te.mm(psS[0:1, :], ones_bf[:, 0:1], xsqb[j][:], start=(j == 0), stop=(j == NJ - 1))
            V.ts(rms_row[:], psS[0:1, :], 1.0 / D, RMS_EPS, ALU.mult, ALU.add)
            A_.act(rms_row[:], rms_row[:], AF.Ln)
            A_.act(rms_row[:], rms_row[:], AF.Exp, scale=-0.5)
            Te.mm(psB[:, :], ones_row[:, :], rms_row[:], start=True, stop=True)
            for j in range(NJ):
                V.stt(xn2b[j][:], xn0b[sl][j], normw[:, j:j + 1], psB[:, :], ALU.mult, ALU.mult)
            if dbg is not None and gc == 0:
                for j in range(NJ):
                    dump(gc, xn2b[j][:, :], f"xn2b_{j}", 128)
            for k in range(NJ):
                for j in range(NJ):
                    Te.mm(ps[k % 2][:, :], w1(j, k), xn2b[j][:], start=(j == 0), stop=(j == NJ - 1))
                if ch > 0:
                    V.copy(xbf[k][:, 0:2], xbf[k][:, C:C + 2])
                V.copy(xbf[k][:, 2:C + 2], ps[k % 2][:, :])
            for k in range(NJ):
                for j in range(NJ):
                    Te.mm(ps[2 + k % 2][:, :], w2(j, k), xn2b[j][:], start=(j == 0), stop=(j == NJ - 1))
                A_.act(zsf[k][:], ps[2 + k % 2][:, :], AF.Silu)
            for k in range(NJ):
                cslot = k % 2
                V.ts(cs[cslot][:, :], xbf[k][:, 0:C], convw[:, 3 * k:3 * k + 1])
                V.stt(cs[cslot][:, :], xbf[k][:, 1:C + 1], convw[:, 3 * k + 1:3 * k + 2], cs[cslot][:, :], ALU.mult, ALU.add)
                V.stt(cs[cslot][:, :], xbf[k][:, 2:C + 2], convw[:, 3 * k + 2:3 * k + 3], cs[cslot][:, :], ALU.mult, ALU.add)
                A_.act(xcb[k][:], cs[cslot][:, :], AF.Silu, bias=convb[:, k:k + 1])
                dump(gc, xcb[k][:, :], f"xcb_{k}", 128)
            for j in range(NJ):
                Te.mm(ps[0][0:H, :], dw(j, 0, H), xcb[j][:], start=(j == 0), stop=(j == NJ - 1))
            for j in range(NJ):
                Te.mm(ps[1][0:H, :], bw(j, 0, H), xcb[j][:], start=(j == 0), stop=(j == NJ - 1))
            V.ts(u_d[:, :], ps[0][0:H, :], db_sb[:, 0:1], None, ALU.add)
            A_.act(sp_t[:, :], u_d[:, :], AF.Abs)
            A_.act(sp_t[:, :], sp_t[:, :], AF.Exp, scale=-1.0)
            A_.act(sp_t[:, :], sp_t[:, :], AF.Ln, bias=1.0)
            V.ts(dlt[:, :], u_d[:, :], 0.0, None, ALU.max)
            V.tt(dlt[:, :], dlt[:, :], sp_t[:, :], ALU.add)
            V.ts(abl[:, :], dlt[:, :], A_log[:, 0:1])
            V.stt(bbar[:, :], ps[1][0:H, :], Bb_sb[:, 0:1], dlt[:, :], ALU.add, ALU.mult)
            dump(gc, dlt[:, :], "dlt", H)
            dump(gc, abl[:, :], "abl", H)
            dump(gc, bbar[:, :], "bbar", H)
            V.scan(acum_raw[:, :], abl[:, :], zero_h[:, :], carryA[:, 0:1], ALU.add, ALU.add)
            V.copy(carryA[:, :], acum_raw[:, C - 1:C])
            V.ts(acum_ext[:, 1:C + 1], acum_raw[:, :], -20.0, 20.0, ALU.max, ALU.min)
            V.tt(d_t[:, :], acum_ext[:, 1:C + 1], acum_ext[:, 0:C], ALU.subtract)
            V.copy(acum_ext[:, 0:1], acum_ext[:, C:C + 1])
            dump(gc, acum_raw[:, :], "acum_raw", H)
            dump(gc, d_t[:, :], "d_t", H)
            A_.act(g_t[:, :], d_t[:, :], AF.Exp)
            V.scan(h_t[:, :], bbar[:, :], g_t[:, :], carryH[:, 0:1], ALU.add, ALU.mult)
            V.copy(carryH[:, :], h_t[:, C - 1:C])
            dump(gc, g_t[:, :], "g_t", H)
            dump(gc, h_t[:, :], "h_t", H)
            A_.act(hb[:, :], h_t[:, :], AF.Copy)
            for k in range(NJ):
                Te.mm(ps[2 + k % 2][:, :], cw_sb[:, k * 128:(k + 1) * 128], hb[:, :], start=True, stop=False)
                for j in range(NJ):
                    Te.mm(ps[2 + k % 2][:, :], dm(j, k), xcb[j][:], start=False, stop=(j == NJ - 1))
                V.ts(yf[:, :], ps[2 + k % 2][:, :], Cb_sb[:, k:k + 1], None, ALU.add)
                V.tt(gb[k][:, :], yf[:, :], zsf[k][:, :], ALU.mult)
            for k in range(NJ):
                for j in range(NJ):
                    Te.mm(ps[k % 2][:, :], wl(j, k), gb[j][:, :], start=(j == 0), stop=(j == NJ - 1))
                mslot = k % 2
                A_.act(mo[mslot][:, :], ps[k % 2][:, :], AF.Copy)
                G.dma(MAM[k * 128:(k + 1) * 128, c0:c0 + C], mo[mslot][:, :], disjoint=True)



def emit_moe(P, engs, st, hin, sm_all, XN0B, MOE, dbg=None):
    nc = P.nc
    V, A_, Te, L, G = engs["V"], engs["A"], engs["T"], engs["L"], engs["G"]
    C = CHUNK

    def dump(cond, tile, name, rows, cols=None):
        if dbg is None or not cond:
            return
        if name not in dbg:
            dbg[name] = nc.dram_tensor("DBG_" + name, [rows, cols or C], F32, kind="ExternalOutput").ap()
        G.dma(dbg[name][:, :], tile, semname="D:DBG")

    b1 = load_f32_tile(P, L, st, "f_b1_sb", hin["f_b1"], (128, E * NJ))
    b2 = load_f32_tile(P, L, st, "f_b2_sb", hin["f_b2"], (128, E * NJ))
    b3 = load_f32_tile(P, L, st, "f_b3_sb", hin["f_b3"], (128, E * NJ))
    sel_np = np.zeros((8, E * 128), dtype=np.float32)
    for e in range(E):
        sel_np[e, e * 128:(e + 1) * 128] = 1.0
    sel_dram = nc.inline_tensor(sel_np, "f_sel").ap()
    sel = st.enter_context(nc.sbuf_tensor("f_sel_sb", [8, E * 128], F32))
    L.dma(sel[:, :], sel_dram[:, :])

    xn0b_w = [st.enter_context(nc.sbuf_tensor(f"f_xn0b{r}", [128, NJ * C], BF16)) for r in range(2)]
    NWB = 3
    wbuf = [st.enter_context(nc.sbuf_tensor(f"f_wbuf{i}", [128, NJ * D], BF16)) for i in range(NWB)]
    xs_w = [st.enter_context(nc.sbuf_tensor(f"f_xs{r}", [128, NJ * C], BF16)) for r in range(2)]
    hb_w = st.enter_context(nc.sbuf_tensor("f_hb", [128, NJ * C], BF16))
    x3_w = st.enter_context(nc.sbuf_tensor("f_x3", [128, NJ * C], BF16))
    acc_w = st.enter_context(nc.sbuf_tensor("f_acc", [128, NJ * NTOK], BF16))
    x2t = [st.enter_context(nc.sbuf_tensor(f"f_x2t{r}", [128, C], F32)) for r in range(2)]
    qt = [st.enter_context(nc.sbuf_tensor(f"f_qt{r}", [128, C], F32)) for r in range(2)]
    mo = [st.enter_context(nc.sbuf_tensor(f"f_mo{r}", [128, C], F32)) for r in range(4)]
    bc_s = [st.enter_context(nc.sbuf_tensor(f"f_bc_s{r}", [128, C], F32)) for r in range(2)]
    bc_m = [st.enter_context(nc.sbuf_tensor(f"f_bc_m{r}", [128, C], F32)) for r in range(2)]

    psS = st.enter_context(nc.psum_tensor("f_psS", [128, C], F32))
    psU = [st.enter_context(nc.psum_tensor(f"f_psU{i}", [128, C], F32)) for i in range(4)]

    wsl = {}

    def load_cls(cls, src3d, e, q=None):
        wb = wbuf[{"w1": 0, "w3": 1, "w2": 2}[cls]]
        for j in range(NJ):
            G.dma(wb[:, j * D:(j + 1) * D], src3d[e, j * 128:(j + 1) * 128, :])

        def sl(j, k):
            return wb[:, j * D + k * 128: j * D + (k + 1) * 128]
        wsl[(cls, e)] = sl

    def emit_bc_xs(e, ch):
        sl_ = ch % 2
        c0 = ch * C
        Te.mm(psS[:, :], sel[:, e * 128:(e + 1) * 128], sm_all[:, c0:c0 + C], start=True, stop=True)
        V.copy(bc_s[sl_][:, :], psS[:, :])
        V.ts(bc_m[sl_][:, :], bc_s[sl_][:, :], 0.0, None, ALU.is_gt)
        for j in range(NJ):
            V.tt(xs_w[sl_][:, j * C:(j + 1) * C], xn0b_w[sl_][:, j * C:(j + 1) * C], bc_s[sl_][:, :], ALU.mult)

    def load_xn0b(ch):
        sl_ = ch % 2
        c0 = ch * C
        for j in range(NJ):
            L.dma(xn0b_w[sl_][:, j * C:(j + 1) * C], XN0B[j * 128:(j + 1) * 128, c0:c0 + C])

    load_cls("w1", hin["f_w1t"], 0, G)
    load_cls("w3", hin["f_w3t"], 0, L)
    load_cls("w2", hin["f_w2t"], 0, G)
    load_xn0b(0)
    load_xn0b(1)
    dump(True, sm_all[:, 0:C], "sm_fm", 8)
    emit_bc_xs(0, 0)
    for e in range(E):
        w1s = wsl.pop(("w1", e))
        w3s = wsl.pop(("w3", e))
        w2s = wsl.pop(("w2", e))
        for ch in range(NCH):
            sl_ = ch % 2
            c0 = ch * C
            last = ch == NCH - 1
            xs = xs_w[sl_]
            for k in range(NJ):
                for j in range(NJ):
                    Te.mm(psU[k % 4][:, :], w1s(j, k), xs[:, j * C:(j + 1) * C], start=(j == 0), stop=(j == NJ - 1))
                A_.act(hb_w[:, k * C:(k + 1) * C], psU[k % 4][:, :], AF.Silu, bias=b1[:, e * NJ + k:e * NJ + k + 1])
            if last and e + 1 < E:
                load_cls("w1", hin["f_w1t"], e + 1)
            for k in range(NJ):
                for j in range(NJ):
                    Te.mm(psU[k % 4][:, :], w3s(j, k), xs[:, j * C:(j + 1) * C], start=(j == 0), stop=(j == NJ - 1))
                V.copy(x3_w[:, k * C:(k + 1) * C], psU[k % 4][:, :])
            if last and e + 1 < E:
                load_cls("w3", hin["f_w3t"], e + 1)
            if ch + 1 < NCH:
                emit_bc_xs(e, ch + 1)
                if ch + 2 < NCH:
                    load_xn0b(ch + 2)
            elif e + 1 < E:
                load_xn0b(0)
                emit_bc_xs(e + 1, 0)
                load_xn0b(1)
            for k in range(NJ):
                r = k % 2
                rq = k % 2
                for j in range(NJ):
                    Te.mm(psU[k % 4][:, :], w2s(j, k), hb_w[:, j * C:(j + 1) * C], start=(j == 0), stop=(j == NJ - 1))
                A_.act(x2t[r][:, :], psU[k % 4][:, :], AF.Identity, bias=b2[:, e * NJ + k:e * NJ + k + 1])
                V.stt(qt[rq][:, :], x3_w[:, k * C:(k + 1) * C], b3[:, e * NJ + k:e * NJ + k + 1], x2t[r][:, :], ALU.add, ALU.mult)
                V.tt(qt[rq][:, :], qt[rq][:, :], bc_m[sl_][:, :], ALU.mult)
                asl = acc_w[:, k * NTOK + c0: k * NTOK + c0 + C]
                if e == 0:
                    V.copy(asl, qt[rq][:, :])
                elif e < E - 1:
                    V.tt(asl, asl, qt[rq][:, :], ALU.add)
                else:
                    V.tt(mo[k % 4][:, :], asl, qt[rq][:, :], ALU.add)
                    G.dma(MOE[k * 128:(k + 1) * 128, c0:c0 + C], mo[k % 4][:, :], accum=True, unordered=True)
        if e + 1 < E:
            load_cls("w2", hin["f_w2t"], e + 1)


def emit_shared(P, engs, st, hin, XN0, SHR):
    nc = P.nc
    V, A_, Te, L, G = engs["V"], engs["A"], engs["T"], engs["L"], engs["G"]
    C = CHUNK
    _, s1 = load_lhsT_weight(P, G, st, "s_sw1", hin["f_sw1t"], D, I2)
    _, s2 = load_lhsT_weight(P, G, st, "s_sw2", hin["f_sw2t"], D, I2)
    _, s3 = load_lhsT_weight(P, G, st, "s_sw3", hin["f_sw3t"], I2, D)
    xn0b_w = [st.enter_context(nc.sbuf_tensor(f"s_xn0b{r}", [128, NJ * C], BF16)) for r in range(2)]
    g_w = st.enter_context(nc.sbuf_tensor("s_g", [128, NI * C], BF16))
    tmp = st.enter_context(nc.sbuf_tensor("s_tmp", [128, C], F32))
    zo = [st.enter_context(nc.sbuf_tensor(f"s_zo{r}", [128, C], F32)) for r in range(2)]
    psA = [st.enter_context(nc.psum_tensor(f"s_psA{i}", [128, C], F32)) for i in range(2)]
    psB = [st.enter_context(nc.psum_tensor(f"s_psB{i}", [128, C], F32)) for i in range(2)]

    for ch in range(NCH):
        sl_ = ch % 2
        c0 = ch * C
        xb = xn0b_w[sl_]
        for j in range(NJ):
            L.dma(xb[:, j * C:(j + 1) * C], XN0[j * 128:(j + 1) * 128, c0:c0 + C])
        for i in range(NI):
            for j in range(NJ):
                Te.mm(psA[i % 2][:, :], s1(j, i), xb[:, j * C:(j + 1) * C], start=(j == 0), stop=(j == NJ - 1))
            for j in range(NJ):
                Te.mm(psB[i % 2][:, :], s2(j, i), xb[:, j * C:(j + 1) * C], start=(j == 0), stop=(j == NJ - 1))
            A_.act(tmp[:, :], psA[i % 2][:, :], AF.Silu)
            V.tt(g_w[:, i * C:(i + 1) * C], tmp[:, :], psB[i % 2][:, :], ALU.mult)
        for k in range(NJ):
            for i in range(NI):
                Te.mm(psA[k % 2][:, :], s3(i, k), g_w[:, i * C:(i + 1) * C], start=(i == 0), stop=(i == NI - 1))
            V.copy(zo[k % 2][:, :], psA[k % 2][:, :])
            G.dma(SHR[k * 128:(k + 1) * 128, c0:c0 + C], zo[k % 2][:, :], accum=True)



def emit_shared_ln(P, engs, st, hin, XN0B, terms, OUT, XN1, XN1B):
    nc = P.nc
    V, A_, Te, L, G = engs["V"], engs["A"], engs["T"], engs["L"], engs["G"]
    Q = engs["Q"]
    C = CHUNK
    _, s1 = load_lhsT_weight(P, G, st, "s_sw1", hin["f_sw1t"], D, I2)
    _, s2 = load_lhsT_weight(P, G, st, "s_sw2", hin["f_sw2t"], D, I2)
    _, s3 = load_lhsT_weight(P, G, st, "s_sw3", hin["f_sw3t"], I2, D)
    xn0b_w = [st.enter_context(nc.sbuf_tensor(f"s_xn0b{r}", [128, NJ * C], BF16)) for r in range(2)]
    g_w = st.enter_context(nc.sbuf_tensor("s_g", [128, NI * C], BF16))
    tmp = st.enter_context(nc.sbuf_tensor("s_tmp", [128, C], F32))
    nterm = len(terms)
    tin_w = [st.enter_context(nc.sbuf_tensor(f"c_tin{r}", [128, nterm * C], F32)) for r in range(2)]
    tin = [[tin_w[r][:, i * C:(i + 1) * C] for i in range(nterm)] for r in range(2)]
    x1_w2 = [st.enter_context(nc.sbuf_tensor(f"c_x1{r}", [128, NJ * C], F32)) for r in range(2)]
    nk_t = [st.enter_context(nc.sbuf_tensor(f"c_nk{r}", [128, C], F32)) for r in range(2)]
    xb16 = st.enter_context(nc.sbuf_tensor("c_xb16", [128, NJ * C], BF16))
    xsq = st.enter_context(nc.sbuf_tensor("c_xsq", [128, NJ * C], BF16))
    ones_bf = st.enter_context(nc.sbuf_tensor("c_ones", [128, 1], BF16))
    ones_row = st.enter_context(nc.sbuf_tensor("c_ones_row", [1, 128], F32))
    rows = [st.enter_context(nc.sbuf_tensor(f"c_row{i}", [1, C], F32)) for i in range(4)]
    m_row, v_row, r_row, mr_row = rows
    V.memset(ones_bf[:, :], 1.0)
    V.memset(ones_row[:, :], 1.0)
    psA = [st.enter_context(nc.psum_tensor(f"s_psA{i}", [128, C], F32)) for i in range(2)]
    psB = [st.enter_context(nc.psum_tensor(f"s_psB{i}", [128, C], F32)) for i in range(2)]
    ps_sum = st.enter_context(nc.psum_tensor("c_ps_sum", [1, C], F32))
    ps_sq = st.enter_context(nc.psum_tensor("c_ps_sq", [1, C], F32))
    ps_r = st.enter_context(nc.psum_tensor("c_ps_r", [128, C], F32))
    ps_mr = st.enter_context(nc.psum_tensor("c_ps_mr", [128, C], F32))

    def compute_block(ch):
        sl_ = ch % 2
        c0 = ch * C
        xb = xn0b_w[sl_]
        x1_w = x1_w2[sl_]
        for j in range(NJ):
            L.dma(xb[:, j * C:(j + 1) * C], XN0B[j * 128:(j + 1) * 128, c0:c0 + C])
        for i in range(NI):
            for j in range(NJ):
                Te.mm(psA[i % 2][:, :], s1(j, i), xb[:, j * C:(j + 1) * C], start=(j == 0), stop=(j == NJ - 1))
            for j in range(NJ):
                Te.mm(psB[i % 2][:, :], s2(j, i), xb[:, j * C:(j + 1) * C], start=(j == 0), stop=(j == NJ - 1))
            A_.act(tmp[:, :], psA[i % 2][:, :], AF.Silu)
            V.tt(g_w[:, i * C:(i + 1) * C], tmp[:, :], psB[i % 2][:, :], ALU.mult)
        for k in range(2):
            for i, t in enumerate(terms):
                L.dma(tin[k][i], t[k * 128:(k + 1) * 128, c0:c0 + C])
        for k in range(NJ):
            r = k % 2
            for i in range(NI):
                Te.mm(psA[k % 2][:, :], s3(i, k), g_w[:, i * C:(i + 1) * C], start=(i == 0), stop=(i == NI - 1))
            xk = x1_w[:, k * C:(k + 1) * C]
            V.tt(xk, tin[r][0], tin[r][1], ALU.add)
            for i in range(2, nterm):
                V.tt(xk, xk, tin[r][i], ALU.add)
            V.tt(xk, xk, psA[k % 2][:, :], ALU.add)
            if k + 2 < NJ:
                for i, t in enumerate(terms):
                    L.dma(tin[(k + 2) % 2][i], t[(k + 2) * 128:(k + 3) * 128, c0:c0 + C])

    def ln_block(ch):
        sl_ = ch % 2
        c0 = ch * C
        x1_w = x1_w2[sl_]
        for k in range(NJ):
            xk = x1_w[:, k * C:(k + 1) * C]
            A_.act(xb16[:, k * C:(k + 1) * C], xk, AF.Copy)
            Q.tt(xsq[:, k * C:(k + 1) * C], xb16[:, k * C:(k + 1) * C], xb16[:, k * C:(k + 1) * C], ALU.mult)
        for k in range(NJ):
            Te.mm(ps_sum[0:1, :], ones_bf[:, 0:1], xb16[:, k * C:(k + 1) * C], start=(k == 0), stop=(k == NJ - 1))
        for k in range(NJ):
            Te.mm(ps_sq[0:1, :], ones_bf[:, 0:1], xsq[:, k * C:(k + 1) * C], start=(k == 0), stop=(k == NJ - 1))
        V.ts(m_row[:], ps_sum[0:1, :], 1.0 / D)
        V.ts(v_row[:], ps_sq[0:1, :], 1.0 / D, LN_EPS, ALU.mult, ALU.add)
        V.tt(mr_row[:], m_row[:], m_row[:], ALU.mult)
        V.tt(v_row[:], v_row[:], mr_row[:], ALU.subtract)
        A_.act(r_row[:], v_row[:], AF.Ln)
        A_.act(r_row[:], r_row[:], AF.Exp, scale=-0.5)
        V.tt(mr_row[:], m_row[:], r_row[:], ALU.mult)
        Te.mm(ps_r[:, :], ones_row[:, :], r_row[:], start=True, stop=True)
        Te.mm(ps_mr[:, :], ones_row[:, :], mr_row[:], start=True, stop=True)
        for k in range(NJ):
            xk = x1_w[:, k * C:(k + 1) * C]
            nk = nk_t[k % 2][:, :]
            V.tt(nk, xk, ps_r[:, :], ALU.mult)
            V.tt(nk, nk, ps_mr[:, :], ALU.subtract)
            G.dma(OUT[k * 128:(k + 1) * 128, c0:c0 + C], xk)
            L.dma(XN1[k * 128:(k + 1) * 128, c0:c0 + C], nk, disjoint=True)
            G.dma(XN1B[k * 128:(k + 1) * 128, c0:c0 + C], nk, disjoint=True)

    for ch in range(NCH + 1):
        if ch < NCH:
            compute_block(ch)
        if ch >= 1:
            ln_block(ch - 1)



def emit_combine_ln(P, engs, st, terms, X1, XN1, XN1B=None):
    nc = P.nc
    V, A_, Te, L, G = engs["V"], engs["A"], engs["T"], engs["L"], engs["G"]
    C = CHUNK
    nterm = len(terms)
    Q = engs["Q"]
    tin_w = [st.enter_context(nc.sbuf_tensor(f"c_tin{r}", [128, nterm * C], F32)) for r in range(2)]
    tin = [[tin_w[r][:, i * C:(i + 1) * C] for i in range(nterm)] for r in range(2)]
    x1_w2 = [st.enter_context(nc.sbuf_tensor(f"c_x1{r}", [128, NJ * C], F32)) for r in range(2)]
    nk_t = [st.enter_context(nc.sbuf_tensor(f"c_nk{r}", [128, C], F32)) for r in range(2)]
    xb16 = st.enter_context(nc.sbuf_tensor("c_xb16", [128, NJ * C], BF16))
    xsq = st.enter_context(nc.sbuf_tensor("c_xsq", [128, NJ * C], BF16))
    ones_bf = st.enter_context(nc.sbuf_tensor("c_ones", [128, 1], BF16))
    ones_row = st.enter_context(nc.sbuf_tensor("c_ones_row", [1, 128], F32))
    ps_sum = st.enter_context(nc.psum_tensor("c_ps_sum", [1, C], F32))
    ps_sq = st.enter_context(nc.psum_tensor("c_ps_sq", [1, C], F32))
    ps_r = st.enter_context(nc.psum_tensor("c_ps_r", [128, C], F32))
    ps_mr = st.enter_context(nc.psum_tensor("c_ps_mr", [128, C], F32))
    rows = [st.enter_context(nc.sbuf_tensor(f"c_row{i}", [1, C], F32)) for i in range(4)]
    m_row, v_row, r_row, mr_row = rows
    V.memset(ones_bf[:, :], 1.0)
    V.memset(ones_row[:, :], 1.0)
    for ch in range(NCH):
        c0 = ch * C
        x1_w, xn1_w = x1_w2[ch % 2], xn1_w2[ch % 2]
        xb16, xsq = xb16_2[ch % 2], xsq_2[ch % 2]
        for j in range(NJ):
            r = j % 4
            for i, t in enumerate(terms):
                L.dma(tin[r][i], t[j * 128:(j + 1) * 128, c0:c0 + C])
            xj = x1_w[:, j * C:(j + 1) * C]
            V.tt(xj, tin[r][0], tin[r][1], ALU.add)
            for i in range(2, nterm):
                V.tt(xj, xj, tin[r][i], ALU.add)
            A_.act(xb16[:, j * C:(j + 1) * C], xj, AF.Copy)
            Q.tt(xsq[:, j * C:(j + 1) * C], xb16[:, j * C:(j + 1) * C], xb16[:, j * C:(j + 1) * C], ALU.mult)
        for j in range(NJ):
            Te.mm(ps_sum[0:1, :], ones_bf[:, 0:1], xb16[:, j * C:(j + 1) * C], start=(j == 0), stop=(j == NJ - 1))
        for j in range(NJ):
            Te.mm(ps_sq[0:1, :], ones_bf[:, 0:1], xsq[:, j * C:(j + 1) * C], start=(j == 0), stop=(j == NJ - 1))
        V.ts(m_row[:], ps_sum[0:1, :], 1.0 / D)
        V.ts(v_row[:], ps_sq[0:1, :], 1.0 / D, LN_EPS, ALU.mult, ALU.add)
        V.tt(mr_row[:], m_row[:], m_row[:], ALU.mult)
        V.tt(v_row[:], v_row[:], mr_row[:], ALU.subtract)
        A_.act(r_row[:], v_row[:], AF.Ln)
        A_.act(r_row[:], r_row[:], AF.Exp, scale=-0.5)
        V.tt(mr_row[:], m_row[:], r_row[:], ALU.mult)
        Te.mm(ps_r[:, :], ones_row[:, :], r_row[:], start=True, stop=True)
        Te.mm(ps_mr[:, :], ones_row[:, :], mr_row[:], start=True, stop=True)
        for j in range(NJ):
            xj = x1_w[:, j * C:(j + 1) * C]
            nj = xn1_w[:, j * C:(j + 1) * C]
            V.tt(nj, xj, ps_r[:, :], ALU.mult)
            V.tt(nj, nj, ps_mr[:, :], ALU.subtract)
            G.dma(X1[j * 128:(j + 1) * 128, c0:c0 + C], xj, disjoint=True)
            L.dma(XN1[j * 128:(j + 1) * 128, c0:c0 + C], nj, disjoint=True)
            if XN1B is not None:
                G.dma(XN1B[j * 128:(j + 1) * 128, c0:c0 + C], nj, disjoint=True)



def emit_rwkv(P, engs, st, hin, XN1, FFN1):
    nc = P.nc
    V, A_, Te, L, G = engs["V"], engs["A"], engs["T"], engs["L"], engs["G"]
    C = CHUNK
    _, wr = load_lhsT_weight(P, G, st, "r_wr", hin["r_wrt"], D, D)
    _, wk = load_lhsT_weight(P, G, st, "r_wk", hin["r_wkt"], D, D)
    _, wv = load_lhsT_weight(P, G, st, "r_wv", hin["r_wvt"], D, D)
    mur = load_f32_tile(P, L, st, "r_mur_sb", hin["r_mur"], (128, NJ))
    muk = load_f32_tile(P, L, st, "r_muk_sb", hin["r_muk"], (128, NJ))
    omr = st.enter_context(nc.sbuf_tensor("r_omr", [128, NJ], F32))
    omk = st.enter_context(nc.sbuf_tensor("r_omk", [128, NJ], F32))
    V.ts(omr[:, :], mur[:, :], -1.0, 1.0, ALU.mult, ALU.add)
    V.ts(omk[:, :], muk[:, :], -1.0, 1.0, ALU.mult, ALU.add)
    CE = C + 1
    Q = engs["Q"]
    xe_w = [st.enter_context(nc.sbuf_tensor(f"r_xe{r}", [128, NJ * CE], F32)) for r in range(2)]
    xr_w2 = [st.enter_context(nc.sbuf_tensor(f"r_xr{r}", [128, NJ * C], BF16)) for r in range(2)]
    xk_w2 = [st.enter_context(nc.sbuf_tensor(f"r_xk{r}", [128, NJ * C], BF16)) for r in range(2)]
    ka_w = st.enter_context(nc.sbuf_tensor("r_ka", [128, NJ * C], BF16))
    r_w = st.enter_context(nc.sbuf_tensor("r_r", [128, NJ * C], F32))
    tmp = st.enter_context(nc.sbuf_tensor("r_tmp", [128, C], F32))
    tmq = st.enter_context(nc.sbuf_tensor("r_tmq", [128, C], F32))
    tmv = st.enter_context(nc.sbuf_tensor("r_tmv", [128, C], F32))
    fo = [st.enter_context(nc.sbuf_tensor(f"r_fo{i}", [128, C], F32)) for i in range(2)]
    psR = [st.enter_context(nc.psum_tensor(f"r_psR{i}", [128, C], F32)) for i in range(2)]
    psK = [st.enter_context(nc.psum_tensor(f"r_psK{i}", [128, C], F32)) for i in range(2)]

    def load_blend(s, ch):
        sl_ = ch % 2
        c0 = s * T + ch * C
        xe = xe_w[sl_]
        for j in range(NJ):
            xj = xe[:, j * CE:(j + 1) * CE]
            if ch == 0:
                V.memset(xj[:, 0:1], 0.0)
                L.dma(xj[:, 1:CE], XN1[j * 128:(j + 1) * 128, c0:c0 + C])
            else:
                L.dma(xj, XN1[j * 128:(j + 1) * 128, c0 - 1:c0 + C])
        for j in range(NJ):
            xj = xe[:, j * CE:(j + 1) * CE]
            V.ts(tmp[:, :], xj[:, 0:C], omr[:, j:j + 1])
            V.stt(xr_w2[sl_][:, j * C:(j + 1) * C], xj[:, 1:CE], mur[:, j:j + 1], tmp[:, :], ALU.mult, ALU.add)
            V.ts(tmq[:, :], xj[:, 0:C], omk[:, j:j + 1])
            V.stt(xk_w2[sl_][:, j * C:(j + 1) * C], xj[:, 1:CE], muk[:, j:j + 1], tmq[:, :], ALU.mult, ALU.add)

    load_blend(0, 0)
    for s in range(SEQ_PER_CORE):
        for ch in range(T // C):
            sl_ = ch % 2
            c0 = s * T + ch * C
            xr_w, xk_w = xr_w2[sl_], xk_w2[sl_]
            for k in range(NJ):
                for j in range(NJ):
                    Te.mm(psR[k % 2][:, :], wr(j, k), xr_w[:, j * C:(j + 1) * C], start=(j == 0), stop=(j == NJ - 1))
                A_.act(r_w[:, k * C:(k + 1) * C], psR[k % 2][:, :], AF.Sigmoid)
                for j in range(NJ):
                    Te.mm(psK[k % 2][:, :], wk(j, k), xk_w[:, j * C:(j + 1) * C], start=(j == 0), stop=(j == NJ - 1))
                V.ts(tmv[:, :], psK[k % 2][:, :], 0.0, None, ALU.max)
                A_.act(ka_w[:, k * C:(k + 1) * C], tmv[:, :], AF.Square)
            if ch + 1 < T // C:
                load_blend(s, ch + 1)
            elif s + 1 < SEQ_PER_CORE:
                load_blend(s + 1, 0)
            for k in range(NJ):
                for j in range(NJ):
                    Te.mm(psR[k % 2][:, :], wv(j, k), ka_w[:, j * C:(j + 1) * C], start=(j == 0), stop=(j == NJ - 1))
                V.tt(fo[k % 2][:, :], psR[k % 2][:, :], r_w[:, k * C:(k + 1) * C], ALU.mult)
                G.dma(FFN1[k * 128:(k + 1) * 128, c0:c0 + C], fo[k % 2][:, :], accum=True)



def emit_performer(P, engs, st, hin, XN1, ATT):
    nc = P.nc
    V, A_, Te, L, G = engs["V"], engs["A"], engs["T"], engs["L"], engs["G"]
    Q = engs["Q"]
    NT = T // 128
    _, wq = load_lhsT_weight(P, G, st, "a_wq_sb", hin["a_wq"], D, D)
    _, wk = load_lhsT_weight(P, G, st, "a_wk_sb", hin["a_wk"], D, D)
    _, wv = load_lhsT_weight(P, G, st, "a_wv_sb", hin["a_wv"], D, D)
    _, wo = load_lhsT_weight(P, G, st, "a_wo_sb", hin["a_wo"], D, D)
    bo = load_f32_tile(P, L, st, "a_bo_sb", hin["a_bo"], (128, NJ))
    projf = load_f32_tile(P, L, st, "a_projf", hin["a_projt"], (DH, NBF))
    projb = st.enter_context(nc.sbuf_tensor("a_projb", [DH, NBF], BF16))
    V.ts(projb[:, :], projf[:, :], PERF_DN)
    identb_np = np.eye(128, dtype=ml_dtypes.bfloat16)
    identb = st.enter_context(nc.sbuf_tensor("a_identb_sb", [128, 128], BF16))
    L.dma(identb[:, :], nc.inline_tensor(identb_np, "a_identb").ap()[:, :])
    identf_np = np.eye(128, dtype=np.float32)
    identf = st.enter_context(nc.sbuf_tensor("a_identf_sb", [128, 128], F32))
    L.dma(identf[:, :], nc.inline_tensor(identf_np, "a_identf").ap()[:, :])
    ones_row = st.enter_context(nc.sbuf_tensor("a_ones_row", [1, 128], F32))
    V.memset(ones_row[:, :], 1.0)
    ones_colb = st.enter_context(nc.sbuf_tensor("a_ones_colb", [128, 1], BF16))
    V.memset(ones_colb[:, :], 1.0)
    ones_rowb = st.enter_context(nc.sbuf_tensor("a_ones_rowb", [1, 128], BF16))
    V.memset(ones_rowb[:, :], 1.0)
    vsum_sb = [st.enter_context(nc.sbuf_tensor(f"a_vsum{r}", [1, DH + 1], BF16)) for r in range(2)]
    csum_sb = st.enter_context(nc.sbuf_tensor("a_csum", [1, DH + 1], BF16))
    colk_all = [st.enter_context(nc.sbuf_tensor(f"a_colk{r}", [128, NT], F32)) for r in range(2)]

    xn1b = st.enter_context(nc.sbuf_tensor("a_xn1b", [128, NJ * T], BF16))
    attn_fm = st.enter_context(nc.sbuf_tensor("a_attn_fm", [128, NJ * T], BF16))
    q_b = [st.enter_context(nc.sbuf_tensor(f"a_q_b{r}", [DH, T], BF16)) for r in range(2)]
    k_b = [st.enter_context(nc.sbuf_tensor(f"a_k_b{r}", [DH, T], BF16)) for r in range(2)]
    v_f = [st.enter_context(nc.sbuf_tensor(f"a_v_f{r}", [DH, T], BF16)) for r in range(2)]
    qsq = [st.enter_context(nc.sbuf_tensor(f"a_qsq{r}", [DH, T], BF16)) for r in range(2)]
    ksq = [st.enter_context(nc.sbuf_tensor(f"a_ksq{r}", [DH, T], BF16)) for r in range(2)]
    v_w = [st.enter_context(nc.sbuf_tensor(f"a_v_w{r}", [128, NT * (DH + 1)], BF16)) for r in range(2)]
    V.memset(v_w[0][:, :], 1.0)
    V.memset(v_w[1][:, :], 1.0)
    qp_fm = st.enter_context(nc.sbuf_tensor("a_qp_fm", [128, 3 * T], BF16))
    diag_q = [st.enter_context(nc.sbuf_tensor(f"a_diag_q{r}", [128, NT], F32)) for r in range(2)]
    diag_k = [st.enter_context(nc.sbuf_tensor(f"a_diag_k{r}", [128, NT], F32)) for r in range(2)]
    kmax_c = st.enter_context(nc.sbuf_tensor("a_kmax", [128, NT], F32))
    col_a = st.enter_context(nc.sbuf_tensor("a_col_a", [128, 1], F32))
    col_b = [st.enter_context(nc.sbuf_tensor(f"a_col_b{r}", [128, 1], F32)) for r in range(2)]
    stabk = st.enter_context(nc.sbuf_tensor("a_stabk", [128, 1], F32))
    row1 = st.enter_context(nc.sbuf_tensor("a_row1", [1, 128], F32))
    sc11 = st.enter_context(nc.sbuf_tensor("a_sc11", [1, 1], F32))
    e_t = [st.enter_context(nc.sbuf_tensor(f"a_e_t{r}", [128, NBF], F32)) for r in range(2)]
    qkp_b = [st.enter_context(nc.sbuf_tensor(f"a_qkp{r}", [128, NBF], BF16)) for r in range(2)]
    ctx_b = st.enter_context(nc.sbuf_tensor("a_ctx", [128, 3 * (DH + 1)], BF16))
    kp_all = st.enter_context(nc.sbuf_tensor("a_kp_all", [128, NT * NBF], BF16))
    ho_t = st.enter_context(nc.sbuf_tensor("a_ho", [128, DH], BF16))
    ao = [st.enter_context(nc.sbuf_tensor(f"a_ao{i}", [128, CHUNK], F32)) for i in range(2)]

    psA = st.enter_context(nc.psum_tensor("a_psA", [128, CHUNK], F32))
    psB = st.enter_context(nc.psum_tensor("a_psB", [128, CHUNK], F32))
    psT_bf = st.enter_context(nc.psum_tensor("a_psT_bf", [128, 3 * 128], BF16))
    psD = [st.enter_context(nc.psum_tensor(f"a_psD{i}", [128, NBF], F32)) for i in range(2)]
    psCtx = st.enter_context(nc.psum_tensor("a_psCtx", [128, 3 * (DH + 1)], F32))
    psO = [st.enter_context(nc.psum_tensor(f"a_psO{i}", [128, DH + 1], F32)) for i in range(2)]
    MBW = [128, 128, NBF - 256]

    def emit_qkv(s, h):
        r = h % 2
        for cc in range(T // CHUNK):
            ps = psA if cc % 2 == 0 else psB
            csl = slice(cc * CHUNK, (cc + 1) * CHUNK)
            for j in range(NJ):
                Te.mm(ps[0:DH, :], wq(j, h, DH), xn1b[:, j * T + cc * CHUNK: j * T + (cc + 1) * CHUNK], start=(j == 0), stop=(j == NJ - 1))
            A_.act(q_b[r][:, csl], ps[0:DH, :], AF.Copy)
            Q.tt(qsq[r][:, csl], q_b[r][:, csl], q_b[r][:, csl], ALU.mult)
            for j in range(NJ):
                Te.mm(ps[0:DH, :], wk(j, h, DH), xn1b[:, j * T + cc * CHUNK: j * T + (cc + 1) * CHUNK], start=(j == 0), stop=(j == NJ - 1))
            A_.act(k_b[r][:, csl], ps[0:DH, :], AF.Copy)
            Q.tt(ksq[r][:, csl], k_b[r][:, csl], k_b[r][:, csl], ALU.mult)
            for j in range(NJ):
                Te.mm(ps[0:DH, :], wv(j, h, DH), xn1b[:, j * T + cc * CHUNK: j * T + (cc + 1) * CHUNK], start=(j == 0), stop=(j == NJ - 1))
            A_.act(v_f[r][:, csl], ps[0:DH, :], AF.Copy)
            yield
        for ti in range(NT):
            reg = (ti % 3) * 128
            Te.transpose(psT_bf[:, reg:reg + DH], v_f[r][:, ti * 128:(ti + 1) * 128], identb[0:DH, 0:DH])
            V.copy(v_w[r][:, ti * (DH + 1): ti * (DH + 1) + DH], psT_bf[:, reg:reg + DH])
            if ti % 4 == 3:
                yield
        for ti in range(NT):
            Te.mm(psA[:, ti:ti + 1], qsq[r][:, ti * 128:(ti + 1) * 128], ones_colb[0:DH, 0:1], start=True, stop=True)
        V.ts(diag_q[r][:, :], psA[:, 0:NT], 0.5 * PERF_DN * PERF_DN)
        yield
        for ti in range(NT):
            Te.mm(psB[:, ti:ti + 1], ksq[r][:, ti * 128:(ti + 1) * 128], ones_colb[0:DH, 0:1], start=True, stop=True)
        V.ts(diag_k[r][:, :], psB[:, 0:NT], 0.5 * PERF_DN * PERF_DN)
        yield
        for ti in range(NT):
            Te.mm(psO[0][0:1, :], ones_colb[:, 0:1], v_w[r][:, ti * (DH + 1):(ti + 1) * (DH + 1)], start=(ti == 0), stop=(ti == NT - 1))
        V.ts(vsum_sb[r][:, :], psO[0][0:1, :], PERF_EPS)
        yield

    def emit_post(s, h):
        r = h % 2
        for ti in range(NT):
            Te.mm(psD[ti % 2][:, :], k_b[r][:, ti * 128:(ti + 1) * 128], projb[:, :], start=True, stop=True)
            V.reduce(kmax_c[:, ti:ti + 1], psD[ti % 2][:, :], ALU.max)
            if ti % 4 == 3:
                yield
        V.reduce(col_a[:, 0:1], kmax_c[:, :], ALU.max)
        Te.transpose(psD[0][0:1, 0:128], col_a[:, 0:1], identf[:, :])
        V.copy(row1[:, :], psD[0][0:1, 0:128])
        V.reduce(sc11[0:1, 0:1], row1[:, :], ALU.max)
        Te.mm(psD[0][:, 0:1], ones_row[:, :], sc11[0:1, 0:1], start=True, stop=True)
        V.ts(stabk[:, 0:1], psD[0][:, 0:1], -1.0)
        V.ts(colk_all[r][:, :], diag_k[r][:, :], stabk[:, 0:1], -1.0, ALU.subtract, ALU.mult)
        yield
        for ti in range(NT):
            rr = ti % 2
            Te.mm(psD[rr][:, :], k_b[r][:, ti * 128:(ti + 1) * 128], projb[:, :], start=True, stop=True)
            A_.act(kp_all[:, ti * NBF:(ti + 1) * NBF], psD[rr][:, :], AF.Exp, bias=colk_all[r][:, ti:ti + 1])
            if ti % 4 == 3:
                yield
        for mb in range(3):
            for ti in range(NT):
                Te.mm(psCtx[0:MBW[mb], mb * (DH + 1):(mb + 1) * (DH + 1)],
                      kp_all[:, ti * NBF + mb * 128: ti * NBF + mb * 128 + MBW[mb]],
                      v_w[r][:, ti * (DH + 1): (ti + 1) * (DH + 1)],
                      start=(ti == 0), stop=False)
            Te.mm(psCtx[0:MBW[mb], mb * (DH + 1):(mb + 1) * (DH + 1)],
                  ones_rowb[0:1, 0:MBW[mb]], vsum_sb[r][:, :], start=False, stop=True)
        for mb in range(3):
            A_.act(ctx_b[0:MBW[mb], mb * (DH + 1):(mb + 1) * (DH + 1)], psCtx[0:MBW[mb], mb * (DH + 1):(mb + 1) * (DH + 1)], AF.Copy)
        for mb in range(3):
            Te.mm(psD[0][0:1, 0:DH + 1], ones_colb[0:MBW[mb], 0:1], ctx_b[0:MBW[mb], mb * (DH + 1):(mb + 1) * (DH + 1)], start=(mb == 0), stop=(mb == 2))
        V.ts(csum_sb[:, :], psD[0][0:1, 0:DH + 1], PERF_EPS)
        yield
        Te.mm(psD[0][:, :], q_b[r][:, 0:128], projb[:, :], start=True, stop=True)
        for ti in range(NT):
            rr = ti % 2
            V.reduce(col_a[:, 0:1], psD[rr][:, :], ALU.max, negate=True)
            V.tt(col_b[rr][:, 0:1], col_a[:, 0:1], diag_q[r][:, ti:ti + 1], ALU.subtract)
            A_.act(qkp_b[rr][:, :], psD[rr][:, :], AF.Exp, bias=col_b[rr][:, 0:1])
            if ti + 1 < NT:
                Te.mm(psD[(ti + 1) % 2][:, :], q_b[r][:, (ti + 1) * 128:(ti + 2) * 128], projb[:, :], start=True, stop=True)
            yield
            for mb in range(3):
                reg = mb * 128
                Te.transpose(psT_bf[0:MBW[mb], reg:reg + 128], qkp_b[rr][:, mb * 128: mb * 128 + MBW[mb]], identb[:, :])
                A_.act(qp_fm[0:MBW[mb], mb * T + ti * 128: mb * T + (ti + 1) * 128], psT_bf[0:MBW[mb], reg:reg + 128], AF.Copy)
        def psO_group(ti):
            rr = ti % 2
            for mb in range(3):
                Te.mm(psO[rr][:, :], qp_fm[0:MBW[mb], mb * T + ti * 128: mb * T + (ti + 1) * 128],
                      ctx_b[0:MBW[mb], mb * (DH + 1):(mb + 1) * (DH + 1)], start=(mb == 0), stop=False)
            Te.mm(psO[rr][:, :], ones_rowb[:, :], csum_sb[:, :], start=False, stop=True)
        psO_group(0)
        ho2 = [ho_t[:, :], ho_t[:, :]]
        for ti in range(NT):
            rr = ti % 2
            if ti + 1 < NT:
                psO_group(ti + 1)
            V.recip(col_a[:, 0:1], psO[rr][:, DH:DH + 1])
            V.ts(ho_t[:, :], psO[rr][:, 0:DH], col_a[:, 0:1])
            reg = (ti % 3) * 128
            Te.transpose(psT_bf[0:DH, reg:reg + 128], ho_t[:, :], identb[:, :])
            V.copy(attn_fm[(h % 2) * DH:(h % 2) * DH + DH, (h // 2) * T + ti * 128:(h // 2) * T + (ti + 1) * 128], psT_bf[0:DH, reg:reg + 128])
            if ti % 4 == 3:
                yield

    for s in range(SEQ_PER_CORE):
        c0s = s * T
        for j in range(NJ):
            L.dma(xn1b[:, j * T:(j + 1) * T], XN1[j * 128:(j + 1) * 128, c0s:c0s + T])
        for _ in emit_qkv(s, 0):
            pass
        for h in range(NHEADS):
            gens = []
            if h + 1 < NHEADS:
                gens.append(emit_qkv(s, h + 1))
            gens.append(emit_post(s, h))
            live = list(gens)
            while live:
                for g in list(live):
                    try:
                        next(g)
                    except StopIteration:
                        live.remove(g)
        for cc in range(T // CHUNK):
            for k in range(NJ):
                ps = psA if k % 2 == 0 else psB
                for j in range(NJ):
                    Te.mm(ps[:, :], wo(j, k), attn_fm[:, j * T + cc * CHUNK: j * T + (cc + 1) * CHUNK], start=(j == 0), stop=(j == NJ - 1))
                A_.act(ao[k % 2][:, :], ps[:, :], AF.Identity, bias=bo[:, k:k + 1])
                G.dma(ATT[k * 128:(k + 1) * 128, c0s + cc * CHUNK: c0s + (cc + 1) * CHUNK], ao[k % 2][:, :], accum=True)


def emit_final(P, engs, st, terms, OUT):
    nc = P.nc
    V, A_, Te, L, G = engs["V"], engs["A"], engs["T"], engs["L"], engs["G"]
    C = CHUNK
    nterm = len(terms)
    tin_w = [st.enter_context(nc.sbuf_tensor(f"o_tin{r}", [128, 3 * C], F32)) for r in range(2)]
    tin = [[tin_w[r][:, i * C:(i + 1) * C] for i in range(nterm)] for r in range(2)]
    po = [st.enter_context(nc.sbuf_tensor(f"o_po{r}", [128, C], F32)) for r in range(2)]
    for ch in range(NCH):
        c0 = ch * C
        for j in range(NJ):
            r = j % 2
            for i, t in enumerate(terms):
                L.dma(tin[r][i], t[j * 128:(j + 1) * 128, c0:c0 + C])
            V.tt(po[r][:], tin[r][0], tin[r][1], ALU.add)
            for i in range(2, nterm):
                V.tt(po[r][:], po[r][:], tin[r][i], ALU.add)
            G.dma(OUT[j * 128:(j + 1) * 128, c0:c0 + C], po[r][:])



def build_program(debug=False):
    nc = bass.Bass("TRN2", target_bir_lowering=False)
    stack = ExitStack()
    P = Prog(nc, stack)
    engs = make_engs(P)

    x0f = nc.dram_tensor("x0f", [D, NTOK], F32, kind="ExternalInput").ap()
    hin = {}
    for name, shape in [
        ("m_w1t", [D, D]), ("m_w2t", [D, D]), ("m_wlt", [D, D]), ("m_dmt", [D, D]),
        ("m_dwt", [D, H]), ("m_bwt", [D, H]), ("m_cwt", [H, D]),
        ("m_normw", [128, NJ]), ("m_convw", [128, NJ * 3]), ("m_convb", [128, NJ]),
        ("m_A", [H, 1]), ("m_db", [H, 1]), ("m_Bb", [H, 1]),
        ("m_Cb", [128, NJ]), ("m_Db", [128, NJ]),
    ]:
        hin[name] = nc.dram_tensor(name, shape, F32, kind="ExternalInput").ap()
    for name, shape in [
        ("f_gwt", [D, E]), ("f_w1t", [E, D, D]), ("f_w3t", [E, D, D]), ("f_w2t", [E, D, D]),
        ("f_b1", [128, E * NJ]), ("f_b2", [128, E * NJ]), ("f_b3", [128, E * NJ]),
        ("f_sw1t", [D, I2]), ("f_sw2t", [D, I2]), ("f_sw3t", [I2, D]),
        ("a_wq", [D, D]), ("a_wk", [D, D]), ("a_wv", [D, D]), ("a_wo", [D, D]),
        ("a_bo", [128, NJ]), ("a_projt", [DH, NBF]),
        ("r_wrt", [D, D]), ("r_wkt", [D, D]), ("r_wvt", [D, D]),
        ("r_mur", [128, NJ]), ("r_muk", [128, NJ]),
    ]:
        hin[name] = nc.dram_tensor(name, shape, F32, kind="ExternalInput").ap()
    out_fm = nc.dram_tensor("out_fm", [D, NTOK], F32, kind="ExternalOutput").ap()
    XN0 = dram_io(nc, "XN0", [D, NTOK], F32, debug)
    XN0B = nc.dram_tensor("XN0B", [D, NTOK], BF16, kind="Internal").ap()
    ACC0 = dram_io(nc, "ACC0", [D, NTOK], F32, debug)
    XN1 = dram_io(nc, "XN1", [D, NTOK], F32, debug)
    XN1B = nc.dram_tensor("XN1B", [D, NTOK], BF16, kind="Internal").ap()

    with stack:
        with ExitStack() as st_g:
            sm_all = st_g.enter_context(nc.sbuf_tensor("sm_all", [8, NTOK], F32))
            with ExitStack() as st:
                emit_ln(P, engs, st, x0f, XN0, dst_b=XN0B, gw=hin["f_gwt"], sm_all=sm_all)
            P.stage_barrier()
            with ExitStack() as st:
                emit_mamba(P, engs, st, hin, XN0B, ACC0, dbg={} if debug else None)
            P.stage_barrier()
            with ExitStack() as st:
                emit_moe(P, engs, st, hin, sm_all, XN0B, ACC0, dbg={} if debug else None)
        P.stage_barrier()
        with ExitStack() as st:
            emit_shared_ln(P, engs, st, hin, XN0B, [x0f, XN0, ACC0], out_fm, XN1, XN1B)
        P.stage_barrier()
        with ExitStack() as st:
            emit_rwkv(P, engs, st, hin, XN1, out_fm)
        P.stage_barrier()
        with ExitStack() as st:
            emit_performer(P, engs, st, hin, XN1B, out_fm)
        P.stage_barrier()

        P.stage_barrier()
    return nc


def kernel(**inputs):
    maps = stage_inputs(inputs)
    nc = build_program()
    res = run_bass_kernel_spmd(nc, maps, core_ids=list(range(NCORES)))
    return assemble_output(res.results)


# version 55
# speedup vs baseline: 1.0234x; 1.0072x over previous
import numpy as np
import ml_dtypes
from contextlib import ExitStack

import concourse.bass as bass
import concourse.mybir as mybir
from concourse.bass_utils import run_bass_kernel_spmd

F32 = mybir.dt.float32
BF16 = mybir.dt.bfloat16
AF = mybir.ActivationFunctionType
ALU = mybir.AluOpType
AX = mybir.AxisListType

NCORES = 8
B, T, D = 16, 2048, 1024
SEQ_PER_CORE = B // NCORES
NTOK = SEQ_PER_CORE * T
NJ = D // 128
CHUNK = 512
NCH = NTOK // CHUNK
H = 64
E = 8
I2 = 2048
NI = I2 // 128
NBF = 266
NHEADS = 16
DH = 64
LN_EPS = 1e-5
RMS_EPS = float(np.finfo(np.float32).eps)
PERF_EPS = 1e-4
PERF_RATIO = float(NBF) ** -0.5
PERF_DN = float(DH) ** -0.25

BF = ml_dtypes.bfloat16



def _pcol(v):
    return np.ascontiguousarray(v.reshape(NJ, 128).T)


def stage_inputs(inputs):
    x = inputs["L0_attn_x"]
    g = {}
    g["m_w1t"] = np.ascontiguousarray(inputs["L0_attn_W1"].T)
    g["m_w2t"] = np.ascontiguousarray(inputs["L0_attn_W2"].T)
    g["m_wlt"] = np.ascontiguousarray(inputs["L0_attn_Wlast"].T)
    g["m_dmt"] = np.ascontiguousarray(inputs["L0_attn_Dw"].T)
    g["m_dwt"] = np.ascontiguousarray(inputs["L0_attn_dw"].T)
    g["m_bwt"] = np.ascontiguousarray(inputs["L0_attn_Bw"].T)
    g["m_cwt"] = np.ascontiguousarray(inputs["L0_attn_Cw"].T)
    g["m_normw"] = _pcol(inputs["L0_attn_norm_w"])
    g["m_convw"] = np.ascontiguousarray(
        inputs["L0_attn_conv_w"].reshape(NJ, 128, 3).transpose(1, 0, 2).reshape(128, NJ * 3))
    g["m_convb"] = _pcol(inputs["L0_attn_conv_b"])
    g["m_A"] = np.ascontiguousarray(inputs["L0_attn_A"].reshape(H, 1))
    g["m_db"] = np.ascontiguousarray(inputs["L0_attn_db"].reshape(H, 1))
    g["m_Bb"] = np.ascontiguousarray(inputs["L0_attn_Bb"].reshape(H, 1))
    g["m_Cb"] = _pcol(inputs["L0_attn_Cb"])
    g["m_Db"] = _pcol(inputs["L0_attn_Db"])
    g["f_gwt"] = np.ascontiguousarray(inputs["L0_ffn_gate_w"].T)
    g["f_w1t"] = np.ascontiguousarray(inputs["L0_ffn_w1"].transpose(0, 2, 1))
    g["f_w3t"] = np.ascontiguousarray(inputs["L0_ffn_w3"].transpose(0, 2, 1))
    g["f_w2t"] = np.ascontiguousarray(inputs["L0_ffn_w2"].transpose(0, 2, 1))
    g["f_b1"] = np.ascontiguousarray(inputs["L0_ffn_b1"].reshape(E, NJ, 128).transpose(2, 0, 1).reshape(128, E * NJ))
    g["f_b2"] = np.ascontiguousarray(inputs["L0_ffn_b2"].reshape(E, NJ, 128).transpose(2, 0, 1).reshape(128, E * NJ))
    g["f_b3"] = np.ascontiguousarray(inputs["L0_ffn_b3"].reshape(E, NJ, 128).transpose(2, 0, 1).reshape(128, E * NJ))
    g["f_sw1t"] = np.ascontiguousarray(inputs["L0_ffn_sw1"].T)
    g["f_sw2t"] = np.ascontiguousarray(inputs["L0_ffn_sw2"].T)
    g["f_sw3t"] = np.ascontiguousarray(inputs["L0_ffn_sw3"].T)
    g["a_wq"] = np.ascontiguousarray(inputs["L1_attn_Wq"])
    g["a_wk"] = np.ascontiguousarray(inputs["L1_attn_Wk"])
    g["a_wv"] = np.ascontiguousarray(inputs["L1_attn_Wv"])
    g["a_wo"] = np.ascontiguousarray(inputs["L1_attn_Wo"])
    g["a_bo"] = _pcol(inputs["L1_attn_bo"])
    g["a_projt"] = np.ascontiguousarray(inputs["L1_attn_proj"].T)
    g["r_wrt"] = np.ascontiguousarray(inputs["L1_ffn_W_r"].T)
    g["r_wkt"] = np.ascontiguousarray(inputs["L1_ffn_W_k"].T)
    g["r_wvt"] = np.ascontiguousarray(inputs["L1_ffn_W_v"].T)
    g["r_mur"] = _pcol(inputs["L1_ffn_mu_r"])
    g["r_muk"] = _pcol(inputs["L1_ffn_mu_k"])

    maps = []
    for c in range(NCORES):
        m = dict(g)
        shard = x[c * SEQ_PER_CORE:(c + 1) * SEQ_PER_CORE].reshape(NTOK, D)
        m["x0f"] = np.ascontiguousarray(shard.T)
        maps.append(m)
    return maps


def assemble_output(results):
    outs = [r["out_fm"] for r in results]
    full = np.concatenate([o.T.reshape(SEQ_PER_CORE, T, D) for o in outs], axis=0)
    return np.ascontiguousarray(full.astype(np.float32))



def _key(ap):
    try:
        return ap.tensor.name
    except AttributeError:
        return None


class Prog:
    K_FENCE = 4

    def __init__(self, nc, stack):
        self.nc = nc
        self.stack = stack
        self._sems = {}
        self.writer = {}
        self.readers = {}
        self.waited = {}
        self.engines = {}

    def s(self, name):
        h = self._sems.get(name)
        if h is None:
            assert len(self._sems) < 100, "semaphore budget exceeded"
            h = [self.stack.enter_context(self.nc.semaphore(name)), 0]
            self._sems[name] = h
        return h

    def ensure_tok(self, rec):
        if rec["tok"] is not None:
            return rec["sem"], rec["tok"]
        eng = self.engines.get(rec["eng"])
        if eng is not None and eng.use_milestones:
            import bisect
            seqs = [m[0] for m in eng.milestones]
            i = bisect.bisect_left(seqs, rec["seq"])
            if i < len(eng.milestones):
                rec["tok"] = eng.milestones[i][1]
                return rec["sem"], rec["tok"]
            h = self.s(rec["sem"])
            eng.last_instr.then_inc(h[0], rec["step"])
            h[1] += rec["step"]
            eng.milestones.append((eng.seq, h[1]))
            rec["tok"] = h[1]
            return rec["sem"], rec["tok"]
        h = self.s(rec["sem"])
        rec["instr"].then_inc(h[0], rec["step"])
        h[1] += rec["step"]
        rec["tok"] = h[1]
        return rec["sem"], rec["tok"]

    def raw_wait(self, tag, raw_eng, sem, tok):
        w = self.waited.setdefault(tag, {})
        if w.get(sem, 0) < tok:
            raw_eng.wait_ge(self.s(sem)[0], tok)
            w[sem] = tok

    def stage_barrier(self):
        nc = self.nc
        for name, h in sorted(self._sems.items()):
            if name.startswith("D:"):
                self.raw_wait("G", nc.gpsimd, name, h[1])
        nc.all_engine_barrier()
        self.writer.clear()
        self.readers.clear()
        for tag in list(self.engines):
            w = self.waited.setdefault(tag, {})
            for name, h in self._sems.items():
                w[name] = h[1]


class Eng:
    def __init__(self, P, raw, tag, sem, step=1, use_milestones=False):
        self.P = P
        self.raw = raw
        self.tag = tag
        self.sem = sem
        self.step = step
        self.seq = 0
        self.use_milestones = use_milestones
        self.milestones = []
        self.last_instr = None
        P.engines[tag] = self

    def _edge(self, rec):
        if rec.get("dma"):
            sem = rec["sem"]
            self.P.raw_wait(self.tag, self.raw, sem, self.P.s(sem)[1])
        else:
            sem, tok = self.P.ensure_tok(rec)
            self.P.raw_wait(self.tag, self.raw, sem, tok)

    def _need(self, rec):
        if rec["eng"] != self.tag:
            return True
        if rec.get("dma") or self.step == 16:
            return True
        if self.tag == "T":
            return False
        return self.seq - rec["seq"] <= self.P.K_FENCE

    def _pre(self, reads, writes):
        P = self.P
        for k in reads:
            if k is None:
                continue
            rec = P.writer.get(k)
            if rec is not None and self._need(rec):
                self._edge(rec)
        for k in writes:
            if k is None:
                continue
            rec = P.writer.get(k)
            if rec is not None and self._need(rec):
                self._edge(rec)
            for rdr in P.readers.get(k, ()):
                if self._need(rdr):
                    self._edge(rdr)

    def _post(self, ins, reads, writes, sem=None, dma=False):
        P = self.P
        self.seq += 1
        rec = {"instr": ins, "eng": self.tag, "sem": sem or self.sem, "step": self.step,
               "tok": None, "seq": self.seq, "dma": dma}
        self.last_instr = ins
        if not self.use_milestones:
            P.ensure_tok(rec)
        for k in writes:
            if k is None:
                continue
            P.writer[k] = rec
            P.readers[k] = []
        for k in reads:
            if k is None or k in writes:
                continue
            P.readers.setdefault(k, []).append(rec)
        return rec

    def emit(self, fn, reads, writes, sem=None, dma=False):
        rkeys = [_key(a) for a in reads]
        wkeys = [_key(a) for a in writes]
        self._pre(rkeys, wkeys)
        ins = fn()
        return self._post(ins, rkeys, wkeys, sem=sem, dma=dma)

    def tokenize(self, rec):
        return self.P.ensure_tok(rec)

    def tt(self, out, in0, in1, op):
        return self.emit(lambda: self.raw.tensor_tensor(out=out, in0=in0, in1=in1, op=op), [in0, in1], [out])

    def ts(self, out, in0, s1, s2=None, op0=ALU.mult, op1=ALU.bypass, accum=None):
        outs = [out] + ([accum] if accum is not None else [])
        return self.emit(lambda: self.raw.tensor_scalar(
            out=out, in0=in0, scalar1=s1, scalar2=s2, op0=op0, op1=op1,
            accum_out=accum), [in0, s1, s2], outs)

    def stt(self, out, in0, scalar, in1, op0, op1, accum=None):
        outs = [out] + ([accum] if accum is not None else [])
        return self.emit(lambda: self.raw.scalar_tensor_tensor(
            out=out, in0=in0, scalar=scalar, in1=in1, op0=op0, op1=op1, accum_out=accum),
            [in0, in1, scalar], outs)

    def copy(self, out, in_):
        return self.emit(lambda: self.raw.tensor_copy(out=out, in_=in_), [in_], [out])

    def memset(self, ap, v):
        return self.emit(lambda: self.raw.memset(ap, v), [], [ap])

    def scan(self, out, d0, d1, init, op0, op1):
        return self.emit(lambda: self.raw.tensor_tensor_scan(
            out=out, data0=d0, data1=d1, initial=init, op0=op0, op1=op1),
            [d0, d1, init if not isinstance(init, float) else None], [out])

    def reduce(self, out, in_, op, negate=False):
        return self.emit(lambda: self.raw.tensor_reduce(
            out=out, in_=in_, axis=AX.X, op=op, negate=negate), [in_], [out])

    def recip(self, out, in_):
        return self.emit(lambda: self.raw.reciprocal(out=out, in_=in_), [in_], [out])

    def select(self, out, mask, a, b):
        return self.emit(lambda: self.raw.select(out=out, mask=mask, on_true=a, on_false=b), [mask, a, b], [out])

    def act(self, out, in_, func, bias=0.0, scale=1.0, accum=None):
        outs = [out] + ([accum] if accum is not None else [])
        return self.emit(lambda: self.raw.activation(
            out=out, in_=in_, func=func, bias=bias, scale=scale, accum_out=accum),
            [in_, bias, scale], outs)

    def mm(self, out, lhsT, rhs, start, stop):
        return self.emit(lambda: self.raw.matmul(out, lhsT, rhs, start=start, stop=stop),
                         [lhsT, rhs], [out])

    def transpose(self, out, in_, ident):
        return self.emit(lambda: self.raw.transpose(out, in_, ident), [in_, ident], [out])

    def dma(self, out, in_, semname=None, accum=False, disjoint=False, unordered=False):
        sem = semname or ("D:" + _key(out))
        if accum:
            fn = lambda: self.raw.dma_start(out=out, in_=in_, accum_op=ALU.add)
        else:
            fn = lambda: self.raw.dma_start(out=out, in_=in_)
        reads = [in_] if (accum and unordered) else ([in_, out] if accum else [in_])
        if disjoint or unordered:
            rkeys = [_key(a) for a in reads]
            self._pre(rkeys, [])
            ins = fn()
            rec = self._post(ins, rkeys, [_key(out)], sem=sem, dma=True)
        else:
            rec = self.emit(fn, reads, [out], sem=sem, dma=True)
        self.P.ensure_tok(rec)
        return rec

    def nop_rec(self):
        return self.emit(lambda: self.raw.nop(), [], [])


def make_engs(P):
    nc = P.nc
    return {
        "V": Eng(P, nc.vector, "V", "V"),
        "A": Eng(P, nc.scalar, "A", "A"),
        "T": Eng(P, nc.tensor, "T", "P", use_milestones=True),
        "L": Eng(P, nc.sync, "L", "L", step=16),
        "G": Eng(P, nc.gpsimd, "G", "G", step=16),
        "Q": Eng(P, nc.gpsimd, "Q", "Q"),
    }


def dram_io(nc, name, shape, dtype, debug):
    kind = "ExternalOutput" if debug else "Internal"
    return nc.dram_tensor(name, shape, dtype, kind=kind).ap()



def load_lhsT_weight(P, G, st, name, hbm, KK, MM):
    nc = P.nc
    njk = KK // 128
    sb = st.enter_context(nc.sbuf_tensor(name, [128, njk * MM], BF16))
    for j in range(njk):
        G.dma(sb[:, j * MM:(j + 1) * MM], hbm[j * 128:(j + 1) * 128, :])

    def sl(j, k, w=128):
        return sb[:, j * MM + k * w: j * MM + k * w + w]
    return sb, sl


def load_f32_tile(P, L, st, name, hbm, shape):
    sb = st.enter_context(P.nc.sbuf_tensor(name, list(shape), F32))
    L.dma(sb[:, :], hbm[:, :])
    return sb



def emit_ln(P, engs, st, src_hbm, dst_hbm, dst_b=None, gw=None, sm_all=None):
    nc = P.nc
    V, A_, Te, L, G = engs["V"], engs["A"], engs["T"], engs["L"], engs["G"]
    if gw is not None:
        gw_f = st.enter_context(nc.sbuf_tensor("ln_gw_f", [128, NJ * E], F32))
        for j in range(NJ):
            L.dma(gw_f[:, j * E:(j + 1) * E], gw[j * 128:(j + 1) * 128, :])
        psG = st.enter_context(nc.psum_tensor("ln_psG", [8, CHUNK], F32))
        psT = st.enter_context(nc.psum_tensor("ln_psT", [128, 8], F32))
        psTb = st.enter_context(nc.psum_tensor("ln_psTb", [8, 128], F32))
        identg = st.enter_context(nc.sbuf_tensor("ln_identg", [128, 128], F32))
        L.dma(identg[:, :], nc.inline_tensor(np.eye(128, dtype=np.float32), "ln_identg_d").ap()[:, :])
        lg_loc = [st.enter_context(nc.sbuf_tensor(f"ln_lg{r}", [8, CHUNK], F32)) for r in range(2)]
        ge_f = st.enter_context(nc.sbuf_tensor("ln_ge", [128, 8], F32))
        gsc = st.enter_context(nc.sbuf_tensor("ln_gsc", [128, 8], F32))
        gscm = st.enter_context(nc.sbuf_tensor("ln_gscm", [128, 8], F32))
        gmask = st.enter_context(nc.sbuf_tensor("ln_gmask", [128, 8], F32))
        gsm = st.enter_context(nc.sbuf_tensor("ln_gsm", [128, 8], F32))
        gca = st.enter_context(nc.sbuf_tensor("ln_gca", [128, 1], F32))
        gcb = st.enter_context(nc.sbuf_tensor("ln_gcb", [128, 1], F32))
        gcc = st.enter_context(nc.sbuf_tensor("ln_gcc", [128, 1], F32))
    xin_w = [st.enter_context(nc.sbuf_tensor(f"ln_xin{r}", [128, NJ * CHUNK], F32)) for r in range(2)]
    xnf_w = [st.enter_context(nc.sbuf_tensor(f"ln_xnf{r}", [128, NJ * CHUNK], F32)) for r in range(2)]
    xin = [[xin_w[r][:, j * CHUNK:(j + 1) * CHUNK] for j in range(NJ)] for r in range(2)]
    xnf = [[xnf_w[r][:, j * CHUNK:(j + 1) * CHUNK] for j in range(NJ)] for r in range(2)]
    xb16 = [st.enter_context(nc.sbuf_tensor(f"ln_xb16_{j}", [128, CHUNK], BF16)) for j in range(NJ)]
    xsq = [st.enter_context(nc.sbuf_tensor(f"ln_xsq_{j}", [128, CHUNK], BF16)) for j in range(NJ)]
    ones_bf = st.enter_context(nc.sbuf_tensor("ln_ones", [128, 1], BF16))
    ones_row = st.enter_context(nc.sbuf_tensor("ln_ones_row", [1, 128], F32))
    ps_sum = st.enter_context(nc.psum_tensor("ln_ps_sum", [1, CHUNK], F32))
    ps_sq = st.enter_context(nc.psum_tensor("ln_ps_sq", [1, CHUNK], F32))
    ps_r = st.enter_context(nc.psum_tensor("ln_ps_r", [128, CHUNK], F32))
    ps_mr = st.enter_context(nc.psum_tensor("ln_ps_mr", [128, CHUNK], F32))
    rows = [st.enter_context(nc.sbuf_tensor(f"ln_row{i}", [1, CHUNK], F32)) for i in range(4)]
    m_row, v_row, r_row, mr_row = rows
    V.memset(ones_bf[:, :], 1.0)
    V.memset(ones_row[:, :], 1.0)
    for ch in range(NCH):
        sl = ch % 2
        c0 = ch * CHUNK
        for j in range(NJ):
            L.dma(xin[sl][j], src_hbm[j * 128:(j + 1) * 128, c0:c0 + CHUNK])
        for j in range(NJ):
            A_.act(xb16[j][:], xin[sl][j], AF.Copy)
            A_.act(xsq[j][:], xb16[j][:], AF.Square)
        for j in range(NJ):
            Te.mm(ps_sum[0:1, :], ones_bf[:, 0:1], xb16[j][:], start=(j == 0), stop=(j == NJ - 1))
        for j in range(NJ):
            Te.mm(ps_sq[0:1, :], ones_bf[:, 0:1], xsq[j][:], start=(j == 0), stop=(j == NJ - 1))
        V.ts(m_row[:], ps_sum[0:1, :], 1.0 / D)
        V.ts(v_row[:], ps_sq[0:1, :], 1.0 / D, LN_EPS, ALU.mult, ALU.add)
        V.tt(mr_row[:], m_row[:], m_row[:], ALU.mult)
        V.tt(v_row[:], v_row[:], mr_row[:], ALU.subtract)
        A_.act(r_row[:], v_row[:], AF.Ln)
        A_.act(r_row[:], r_row[:], AF.Exp, scale=-0.5)
        V.tt(mr_row[:], m_row[:], r_row[:], ALU.mult)
        Te.mm(ps_r[:, :], ones_row[:, :], r_row[:], start=True, stop=True)
        Te.mm(ps_mr[:, :], ones_row[:, :], mr_row[:], start=True, stop=True)
        for j in range(NJ):
            V.tt(xnf[sl][j], xin[sl][j], ps_r[:, :], ALU.mult)
            V.tt(xnf[sl][j], xnf[sl][j], ps_mr[:, :], ALU.subtract)
        for j in range(NJ):
            G.dma(dst_hbm[j * 128:(j + 1) * 128, c0:c0 + CHUNK], xnf[sl][j], disjoint=True)
            if dst_b is not None:
                G.dma(dst_b[j * 128:(j + 1) * 128, c0:c0 + CHUNK], xnf[sl][j], disjoint=True)
        if gw is not None:
            for j in range(NJ):
                Te.mm(psG[0:8, :], gw_f[:, j * E:(j + 1) * E], xnf[sl][j], start=(j == 0), stop=(j == NJ - 1))
            A_.act(lg_loc[sl][:, :], psG[0:8, :], AF.Copy)
            for i in range(CHUNK // 128):
                Te.transpose(psT[:, 0:8], lg_loc[sl][:, i * 128:(i + 1) * 128], identg[0:8, 0:8])
                V.reduce(gca[:, 0:1], psT[:, 0:8], ALU.max, negate=True)
                A_.act(ge_f[:, :], psT[:, 0:8], AF.Exp, bias=gca[:, 0:1], accum=gcb[:, 0:1])
                V.recip(gcc[:, 0:1], gcb[:, 0:1])
                V.ts(gsc[:, :], ge_f[:, :], gcc[:, 0:1], None, ALU.mult)
                V.reduce(gca[:, 0:1], gsc[:, :], ALU.max)
                V.ts(gmask[:, :], gsc[:, :], gca[:, 0:1], None, ALU.is_ge)
                V.stt(gscm[:, :], gmask[:, :], -1e30, gsc[:, :], ALU.mult, ALU.add)
                V.reduce(gcb[:, 0:1], gscm[:, :], ALU.max)
                V.ts(gmask[:, :], gsc[:, :], gcb[:, 0:1], None, ALU.is_ge)
                V.tt(gsm[:, :], gsc[:, :], gmask[:, :], ALU.mult)
                Te.transpose(psTb[0:8, :], gsm[:, :], identg[:, :])
                V.copy(sm_all[:, c0 + i * 128:c0 + (i + 1) * 128], psTb[0:8, :])



def emit_mamba(P, engs, st, hin, XN0, MAM, dbg=None):
    nc = P.nc
    V, A_, Te, L, G = engs["V"], engs["A"], engs["T"], engs["L"], engs["G"]
    C = CHUNK

    def dump(gc, tile, name, rows):
        if dbg is None or gc != 0:
            return
        if name not in dbg:
            dbg[name] = nc.dram_tensor("DBG_" + name, [rows, C], F32, kind="ExternalOutput").ap()
        G.dma(dbg[name][:, :], tile, semname="D:DBG")

    _, w1 = load_lhsT_weight(P, G, st, "m_w1", hin["m_w1t"], D, D)
    _, w2 = load_lhsT_weight(P, G, st, "m_w2", hin["m_w2t"], D, D)
    _, dm = load_lhsT_weight(P, G, st, "m_dm", hin["m_dmt"], D, D)
    _, wl = load_lhsT_weight(P, G, st, "m_wl", hin["m_wlt"], D, D)
    _, dw = load_lhsT_weight(P, G, st, "m_dw", hin["m_dwt"], D, H)
    _, bw = load_lhsT_weight(P, G, st, "m_bw", hin["m_bwt"], D, H)
    cw_sb = st.enter_context(nc.sbuf_tensor("m_cw", [H, D], BF16))
    G.dma(cw_sb[:, :], hin["m_cwt"][:, :])
    normw = load_f32_tile(P, L, st, "m_normw_sb", hin["m_normw"], (128, NJ))
    convw = load_f32_tile(P, L, st, "m_convw_sb", hin["m_convw"], (128, NJ * 3))
    convb = load_f32_tile(P, L, st, "m_convb_sb", hin["m_convb"], (128, NJ))
    A_sb = load_f32_tile(P, L, st, "m_A_sb", hin["m_A"], (H, 1))
    db_sb = load_f32_tile(P, L, st, "m_db_sb", hin["m_db"], (H, 1))
    Bb_sb = load_f32_tile(P, L, st, "m_Bb_sb", hin["m_Bb"], (H, 1))
    Cb_sb = load_f32_tile(P, L, st, "m_Cb_sb", hin["m_Cb"], (128, NJ))
    Db_sb = load_f32_tile(P, L, st, "m_Db_sb", hin["m_Db"], (128, NJ))
    ones_row = st.enter_context(nc.sbuf_tensor("m_ones_row", [1, 128], F32))
    ones_bf = st.enter_context(nc.sbuf_tensor("m_ones_bf", [128, 1], BF16))

    xn0b_w = [st.enter_context(nc.sbuf_tensor(f"m_xn0b{r}", [128, NJ * C], BF16)) for r in range(2)]
    xn0b = [[xn0b_w[r][:, j * C:(j + 1) * C] for j in range(NJ)] for r in range(2)]
    xn2b = [st.enter_context(nc.sbuf_tensor(f"m_xn2b_{j}", [128, C], BF16)) for j in range(NJ)]
    xbf = [st.enter_context(nc.sbuf_tensor(f"m_xbf_{j}", [128, C + 2], F32)) for j in range(NJ)]
    xcb = [st.enter_context(nc.sbuf_tensor(f"m_xcb_{j}", [128, C], BF16)) for j in range(NJ)]
    zsf = [st.enter_context(nc.sbuf_tensor(f"m_zsf_{j}", [128, C], F32)) for j in range(NJ)]
    gb = [st.enter_context(nc.sbuf_tensor(f"m_gb_{j}", [128, C], BF16)) for j in range(NJ)]
    yf = st.enter_context(nc.sbuf_tensor("m_yf", [128, C], F32))
    cs = [st.enter_context(nc.sbuf_tensor(f"m_cs_{r}", [128, C], F32)) for r in range(2)]
    mo = [st.enter_context(nc.sbuf_tensor(f"m_mo_{r}", [128, C], F32)) for r in range(2)]
    A_log = st.enter_context(nc.sbuf_tensor("m_A_log", [H, 1], F32))
    relu_A = st.enter_context(nc.sbuf_tensor("m_relu_A", [H, 1], F32))
    sp_t = st.enter_context(nc.sbuf_tensor("m_sp_t", [H, C], F32))
    u_d = st.enter_context(nc.sbuf_tensor("m_u_d", [H, C], F32))
    dlt = st.enter_context(nc.sbuf_tensor("m_dlt", [H, C], F32))
    abl = st.enter_context(nc.sbuf_tensor("m_abl", [H, C], F32))
    bbar = st.enter_context(nc.sbuf_tensor("m_bbar", [H, C], F32))
    zero_h = st.enter_context(nc.sbuf_tensor("m_zero_h", [H, C], F32))
    acum_raw = st.enter_context(nc.sbuf_tensor("m_acum_raw", [H, C], F32))
    acum_ext = st.enter_context(nc.sbuf_tensor("m_acum_ext", [H, C + 1], F32))
    d_t = st.enter_context(nc.sbuf_tensor("m_d_t", [H, C], F32))
    g_t = st.enter_context(nc.sbuf_tensor("m_g_t", [H, C], F32))
    h_t = st.enter_context(nc.sbuf_tensor("m_h_t", [H, C], F32))
    hb = st.enter_context(nc.sbuf_tensor("m_hb", [H, C], BF16))
    carryA = st.enter_context(nc.sbuf_tensor("m_carryA", [H, 1], F32))
    carryH = st.enter_context(nc.sbuf_tensor("m_carryH", [H, 1], F32))
    rms_row = st.enter_context(nc.sbuf_tensor("m_rms_row", [1, C], F32))
    xsqb = [st.enter_context(nc.sbuf_tensor(f"m_xsqb_{j}", [128, C], BF16)) for j in range(NJ)]

    psS = st.enter_context(nc.psum_tensor("m_psS", [1, C], F32))
    psB = st.enter_context(nc.psum_tensor("m_psB", [128, C], F32))
    ps = [st.enter_context(nc.psum_tensor(f"m_ps{i}", [128, C], F32)) for i in range(4)]

    V.memset(ones_row[:, :], 1.0)
    V.memset(ones_bf[:, :], 1.0)
    V.memset(zero_h[:, :], 0.0)
    A_.act(A_log[:, :], A_sb[:, :], AF.Abs)
    A_.act(A_log[:, :], A_log[:, :], AF.Exp, scale=-1.0)
    A_.act(A_log[:, :], A_log[:, :], AF.Ln, bias=1.0)
    V.ts(relu_A[:, :], A_sb[:, :], 0.0, None, ALU.max)
    V.tt(A_log[:, :], A_log[:, :], relu_A[:, :], ALU.add)
    V.ts(A_log[:, :], A_log[:, :], -1.0, None, ALU.mult)
    V.tt(Cb_sb[:, :], Cb_sb[:, :], Db_sb[:, :], ALU.add)

    for s in range(SEQ_PER_CORE):
        V.memset(carryA[:, :], 0.0)
        V.memset(carryH[:, :], 0.0)
        V.memset(acum_ext[:, 0:1], 0.0)
        for j in range(NJ):
            V.memset(xbf[j][:, 0:2], 0.0)
        for ch in range(T // C):
            gc = s * (T // C) + ch
            sl = gc % 2
            c0 = s * T + ch * C
            for j in range(NJ):
                L.dma(xn0b[sl][j], XN0[j * 128:(j + 1) * 128, c0:c0 + C])
            for j in range(NJ):
                A_.act(xsqb[j][:], xn0b[sl][j], AF.Square)
            for j in range(NJ):
                Te.mm(psS[0:1, :], ones_bf[:, 0:1], xsqb[j][:], start=(j == 0), stop=(j == NJ - 1))
            V.ts(rms_row[:], psS[0:1, :], 1.0 / D, RMS_EPS, ALU.mult, ALU.add)
            A_.act(rms_row[:], rms_row[:], AF.Ln)
            A_.act(rms_row[:], rms_row[:], AF.Exp, scale=-0.5)
            Te.mm(psB[:, :], ones_row[:, :], rms_row[:], start=True, stop=True)
            for j in range(NJ):
                V.stt(xn2b[j][:], xn0b[sl][j], normw[:, j:j + 1], psB[:, :], ALU.mult, ALU.mult)
            if dbg is not None and gc == 0:
                for j in range(NJ):
                    dump(gc, xn2b[j][:, :], f"xn2b_{j}", 128)
            for k in range(NJ):
                for j in range(NJ):
                    Te.mm(ps[k % 2][:, :], w1(j, k), xn2b[j][:], start=(j == 0), stop=(j == NJ - 1))
                if ch > 0:
                    V.copy(xbf[k][:, 0:2], xbf[k][:, C:C + 2])
                V.copy(xbf[k][:, 2:C + 2], ps[k % 2][:, :])
            for k in range(NJ):
                for j in range(NJ):
                    Te.mm(ps[2 + k % 2][:, :], w2(j, k), xn2b[j][:], start=(j == 0), stop=(j == NJ - 1))
                A_.act(zsf[k][:], ps[2 + k % 2][:, :], AF.Silu)
            for k in range(NJ):
                cslot = k % 2
                V.ts(cs[cslot][:, :], xbf[k][:, 0:C], convw[:, 3 * k:3 * k + 1])
                V.stt(cs[cslot][:, :], xbf[k][:, 1:C + 1], convw[:, 3 * k + 1:3 * k + 2], cs[cslot][:, :], ALU.mult, ALU.add)
                V.stt(cs[cslot][:, :], xbf[k][:, 2:C + 2], convw[:, 3 * k + 2:3 * k + 3], cs[cslot][:, :], ALU.mult, ALU.add)
                A_.act(xcb[k][:], cs[cslot][:, :], AF.Silu, bias=convb[:, k:k + 1])
                dump(gc, xcb[k][:, :], f"xcb_{k}", 128)
            for j in range(NJ):
                Te.mm(ps[0][0:H, :], dw(j, 0, H), xcb[j][:], start=(j == 0), stop=(j == NJ - 1))
            for j in range(NJ):
                Te.mm(ps[1][0:H, :], bw(j, 0, H), xcb[j][:], start=(j == 0), stop=(j == NJ - 1))
            V.ts(u_d[:, :], ps[0][0:H, :], db_sb[:, 0:1], None, ALU.add)
            A_.act(sp_t[:, :], u_d[:, :], AF.Abs)
            A_.act(sp_t[:, :], sp_t[:, :], AF.Exp, scale=-1.0)
            A_.act(sp_t[:, :], sp_t[:, :], AF.Ln, bias=1.0)
            V.ts(dlt[:, :], u_d[:, :], 0.0, None, ALU.max)
            V.tt(dlt[:, :], dlt[:, :], sp_t[:, :], ALU.add)
            V.ts(abl[:, :], dlt[:, :], A_log[:, 0:1])
            V.stt(bbar[:, :], ps[1][0:H, :], Bb_sb[:, 0:1], dlt[:, :], ALU.add, ALU.mult)
            dump(gc, dlt[:, :], "dlt", H)
            dump(gc, abl[:, :], "abl", H)
            dump(gc, bbar[:, :], "bbar", H)
            V.scan(acum_raw[:, :], abl[:, :], zero_h[:, :], carryA[:, 0:1], ALU.add, ALU.add)
            V.copy(carryA[:, :], acum_raw[:, C - 1:C])
            V.ts(acum_ext[:, 1:C + 1], acum_raw[:, :], -20.0, 20.0, ALU.max, ALU.min)
            V.tt(d_t[:, :], acum_ext[:, 1:C + 1], acum_ext[:, 0:C], ALU.subtract)
            V.copy(acum_ext[:, 0:1], acum_ext[:, C:C + 1])
            dump(gc, acum_raw[:, :], "acum_raw", H)
            dump(gc, d_t[:, :], "d_t", H)
            A_.act(g_t[:, :], d_t[:, :], AF.Exp)
            V.scan(h_t[:, :], bbar[:, :], g_t[:, :], carryH[:, 0:1], ALU.add, ALU.mult)
            V.copy(carryH[:, :], h_t[:, C - 1:C])
            dump(gc, g_t[:, :], "g_t", H)
            dump(gc, h_t[:, :], "h_t", H)
            A_.act(hb[:, :], h_t[:, :], AF.Copy)
            for k in range(NJ):
                Te.mm(ps[2 + k % 2][:, :], cw_sb[:, k * 128:(k + 1) * 128], hb[:, :], start=True, stop=False)
                for j in range(NJ):
                    Te.mm(ps[2 + k % 2][:, :], dm(j, k), xcb[j][:], start=False, stop=(j == NJ - 1))
                V.ts(yf[:, :], ps[2 + k % 2][:, :], Cb_sb[:, k:k + 1], None, ALU.add)
                V.tt(gb[k][:, :], yf[:, :], zsf[k][:, :], ALU.mult)
            for k in range(NJ):
                for j in range(NJ):
                    Te.mm(ps[k % 2][:, :], wl(j, k), gb[j][:, :], start=(j == 0), stop=(j == NJ - 1))
                mslot = k % 2
                A_.act(mo[mslot][:, :], ps[k % 2][:, :], AF.Copy)
                G.dma(MAM[k * 128:(k + 1) * 128, c0:c0 + C], mo[mslot][:, :], disjoint=True)



def emit_moe(P, engs, st, hin, sm_all, XN0B, MOE, dbg=None):
    nc = P.nc
    V, A_, Te, L, G = engs["V"], engs["A"], engs["T"], engs["L"], engs["G"]
    C = CHUNK

    def dump(cond, tile, name, rows, cols=None):
        if dbg is None or not cond:
            return
        if name not in dbg:
            dbg[name] = nc.dram_tensor("DBG_" + name, [rows, cols or C], F32, kind="ExternalOutput").ap()
        G.dma(dbg[name][:, :], tile, semname="D:DBG")

    b1 = load_f32_tile(P, L, st, "f_b1_sb", hin["f_b1"], (128, E * NJ))
    b2 = load_f32_tile(P, L, st, "f_b2_sb", hin["f_b2"], (128, E * NJ))
    b3 = load_f32_tile(P, L, st, "f_b3_sb", hin["f_b3"], (128, E * NJ))
    sel_np = np.zeros((8, E * 128), dtype=np.float32)
    for e in range(E):
        sel_np[e, e * 128:(e + 1) * 128] = 1.0
    sel_dram = nc.inline_tensor(sel_np, "f_sel").ap()
    sel = st.enter_context(nc.sbuf_tensor("f_sel_sb", [8, E * 128], F32))
    L.dma(sel[:, :], sel_dram[:, :])

    xn0b_w = [st.enter_context(nc.sbuf_tensor(f"f_xn0b{r}", [128, NJ * C], BF16)) for r in range(2)]
    NWB = 3
    wbuf = [st.enter_context(nc.sbuf_tensor(f"f_wbuf{i}", [128, NJ * D], BF16)) for i in range(NWB)]
    xs_w = [st.enter_context(nc.sbuf_tensor(f"f_xs{r}", [128, NJ * C], BF16)) for r in range(2)]
    hb_w = st.enter_context(nc.sbuf_tensor("f_hb", [128, NJ * C], BF16))
    x3_w = st.enter_context(nc.sbuf_tensor("f_x3", [128, NJ * C], BF16))
    acc_w = st.enter_context(nc.sbuf_tensor("f_acc", [128, NJ * NTOK], BF16))
    x2t = [st.enter_context(nc.sbuf_tensor(f"f_x2t{r}", [128, C], F32)) for r in range(2)]
    qt = [st.enter_context(nc.sbuf_tensor(f"f_qt{r}", [128, C], F32)) for r in range(2)]
    mo = [st.enter_context(nc.sbuf_tensor(f"f_mo{r}", [128, C], F32)) for r in range(4)]
    bc_s = [st.enter_context(nc.sbuf_tensor(f"f_bc_s{r}", [128, C], F32)) for r in range(2)]
    bc_m = [st.enter_context(nc.sbuf_tensor(f"f_bc_m{r}", [128, C], F32)) for r in range(2)]

    psS = st.enter_context(nc.psum_tensor("f_psS", [128, C], F32))
    psU = [st.enter_context(nc.psum_tensor(f"f_psU{i}", [128, C], F32)) for i in range(4)]

    wsl = {}

    def load_cls(cls, src3d, e, q=None):
        wb = wbuf[{"w1": 0, "w3": 1, "w2": 2}[cls]]
        for j in range(NJ):
            G.dma(wb[:, j * D:(j + 1) * D], src3d[e, j * 128:(j + 1) * 128, :])

        def sl(j, k):
            return wb[:, j * D + k * 128: j * D + (k + 1) * 128]
        wsl[(cls, e)] = sl

    def emit_bc_xs(e, ch):
        sl_ = ch % 2
        c0 = ch * C
        Te.mm(psS[:, :], sel[:, e * 128:(e + 1) * 128], sm_all[:, c0:c0 + C], start=True, stop=True)
        V.copy(bc_s[sl_][:, :], psS[:, :])
        V.ts(bc_m[sl_][:, :], bc_s[sl_][:, :], 0.0, None, ALU.is_gt)
        for j in range(NJ):
            V.tt(xs_w[sl_][:, j * C:(j + 1) * C], xn0b_w[sl_][:, j * C:(j + 1) * C], bc_s[sl_][:, :], ALU.mult)

    def load_xn0b(ch):
        sl_ = ch % 2
        c0 = ch * C
        for j in range(NJ):
            L.dma(xn0b_w[sl_][:, j * C:(j + 1) * C], XN0B[j * 128:(j + 1) * 128, c0:c0 + C])

    load_cls("w1", hin["f_w1t"], 0, G)
    load_cls("w3", hin["f_w3t"], 0, L)
    load_cls("w2", hin["f_w2t"], 0, G)
    load_xn0b(0)
    load_xn0b(1)
    dump(True, sm_all[:, 0:C], "sm_fm", 8)
    emit_bc_xs(0, 0)
    for e in range(E):
        w1s = wsl.pop(("w1", e))
        w3s = wsl.pop(("w3", e))
        w2s = wsl.pop(("w2", e))
        for ch in range(NCH):
            sl_ = ch % 2
            c0 = ch * C
            last = ch == NCH - 1
            xs = xs_w[sl_]
            for k in range(NJ):
                for j in range(NJ):
                    Te.mm(psU[k % 4][:, :], w1s(j, k), xs[:, j * C:(j + 1) * C], start=(j == 0), stop=(j == NJ - 1))
                A_.act(hb_w[:, k * C:(k + 1) * C], psU[k % 4][:, :], AF.Silu, bias=b1[:, e * NJ + k:e * NJ + k + 1])
            if last and e + 1 < E:
                load_cls("w1", hin["f_w1t"], e + 1)
            for k in range(NJ):
                for j in range(NJ):
                    Te.mm(psU[k % 4][:, :], w3s(j, k), xs[:, j * C:(j + 1) * C], start=(j == 0), stop=(j == NJ - 1))
                V.copy(x3_w[:, k * C:(k + 1) * C], psU[k % 4][:, :])
            if last and e + 1 < E:
                load_cls("w3", hin["f_w3t"], e + 1)
            if ch + 1 < NCH:
                emit_bc_xs(e, ch + 1)
                if ch + 2 < NCH:
                    load_xn0b(ch + 2)
            elif e + 1 < E:
                load_xn0b(0)
                emit_bc_xs(e + 1, 0)
                load_xn0b(1)
            for k in range(NJ):
                r = k % 2
                rq = k % 2
                for j in range(NJ):
                    Te.mm(psU[k % 4][:, :], w2s(j, k), hb_w[:, j * C:(j + 1) * C], start=(j == 0), stop=(j == NJ - 1))
                A_.act(x2t[r][:, :], psU[k % 4][:, :], AF.Identity, bias=b2[:, e * NJ + k:e * NJ + k + 1])
                V.stt(qt[rq][:, :], x3_w[:, k * C:(k + 1) * C], b3[:, e * NJ + k:e * NJ + k + 1], x2t[r][:, :], ALU.add, ALU.mult)
                V.tt(qt[rq][:, :], qt[rq][:, :], bc_m[sl_][:, :], ALU.mult)
                asl = acc_w[:, k * NTOK + c0: k * NTOK + c0 + C]
                if e == 0:
                    V.copy(asl, qt[rq][:, :])
                elif e < E - 1:
                    V.tt(asl, asl, qt[rq][:, :], ALU.add)
                else:
                    V.tt(mo[k % 4][:, :], asl, qt[rq][:, :], ALU.add)
                    G.dma(MOE[k * 128:(k + 1) * 128, c0:c0 + C], mo[k % 4][:, :], accum=True, unordered=True)
        if e + 1 < E:
            load_cls("w2", hin["f_w2t"], e + 1)


def emit_shared(P, engs, st, hin, XN0, SHR):
    nc = P.nc
    V, A_, Te, L, G = engs["V"], engs["A"], engs["T"], engs["L"], engs["G"]
    C = CHUNK
    _, s1 = load_lhsT_weight(P, G, st, "s_sw1", hin["f_sw1t"], D, I2)
    _, s2 = load_lhsT_weight(P, G, st, "s_sw2", hin["f_sw2t"], D, I2)
    _, s3 = load_lhsT_weight(P, G, st, "s_sw3", hin["f_sw3t"], I2, D)
    xn0b_w = [st.enter_context(nc.sbuf_tensor(f"s_xn0b{r}", [128, NJ * C], BF16)) for r in range(2)]
    g_w = st.enter_context(nc.sbuf_tensor("s_g", [128, NI * C], BF16))
    tmp = st.enter_context(nc.sbuf_tensor("s_tmp", [128, C], F32))
    zo = [st.enter_context(nc.sbuf_tensor(f"s_zo{r}", [128, C], F32)) for r in range(2)]
    psA = [st.enter_context(nc.psum_tensor(f"s_psA{i}", [128, C], F32)) for i in range(2)]
    psB = [st.enter_context(nc.psum_tensor(f"s_psB{i}", [128, C], F32)) for i in range(2)]

    for ch in range(NCH):
        sl_ = ch % 2
        c0 = ch * C
        xb = xn0b_w[sl_]
        for j in range(NJ):
            L.dma(xb[:, j * C:(j + 1) * C], XN0[j * 128:(j + 1) * 128, c0:c0 + C])
        for i in range(NI):
            for j in range(NJ):
                Te.mm(psA[i % 2][:, :], s1(j, i), xb[:, j * C:(j + 1) * C], start=(j == 0), stop=(j == NJ - 1))
            for j in range(NJ):
                Te.mm(psB[i % 2][:, :], s2(j, i), xb[:, j * C:(j + 1) * C], start=(j == 0), stop=(j == NJ - 1))
            A_.act(tmp[:, :], psA[i % 2][:, :], AF.Silu)
            V.tt(g_w[:, i * C:(i + 1) * C], tmp[:, :], psB[i % 2][:, :], ALU.mult)
        for k in range(NJ):
            for i in range(NI):
                Te.mm(psA[k % 2][:, :], s3(i, k), g_w[:, i * C:(i + 1) * C], start=(i == 0), stop=(i == NI - 1))
            V.copy(zo[k % 2][:, :], psA[k % 2][:, :])
            G.dma(SHR[k * 128:(k + 1) * 128, c0:c0 + C], zo[k % 2][:, :], accum=True)



def emit_shared_ln(P, engs, st, hin, XN0B, terms, OUT, XN1, XN1B):
    nc = P.nc
    V, A_, Te, L, G = engs["V"], engs["A"], engs["T"], engs["L"], engs["G"]
    Q = engs["Q"]
    C = CHUNK
    _, s1 = load_lhsT_weight(P, G, st, "s_sw1", hin["f_sw1t"], D, I2)
    _, s2 = load_lhsT_weight(P, G, st, "s_sw2", hin["f_sw2t"], D, I2)
    _, s3 = load_lhsT_weight(P, G, st, "s_sw3", hin["f_sw3t"], I2, D)
    xn0b_w = [st.enter_context(nc.sbuf_tensor(f"s_xn0b{r}", [128, NJ * C], BF16)) for r in range(2)]
    g_w = st.enter_context(nc.sbuf_tensor("s_g", [128, NI * C], BF16))
    tmp = st.enter_context(nc.sbuf_tensor("s_tmp", [128, C], F32))
    nterm = len(terms)
    tin_w = [st.enter_context(nc.sbuf_tensor(f"c_tin{r}", [128, nterm * C], F32)) for r in range(2)]
    tin = [[tin_w[r][:, i * C:(i + 1) * C] for i in range(nterm)] for r in range(2)]
    x1_w2 = [st.enter_context(nc.sbuf_tensor(f"c_x1{r}", [128, NJ * C], F32)) for r in range(2)]
    nk_t = [st.enter_context(nc.sbuf_tensor(f"c_nk{r}", [128, C], F32)) for r in range(2)]
    xb16 = st.enter_context(nc.sbuf_tensor("c_xb16", [128, NJ * C], BF16))
    xsq = st.enter_context(nc.sbuf_tensor("c_xsq", [128, NJ * C], BF16))
    ones_bf = st.enter_context(nc.sbuf_tensor("c_ones", [128, 1], BF16))
    ones_row = st.enter_context(nc.sbuf_tensor("c_ones_row", [1, 128], F32))
    rows = [st.enter_context(nc.sbuf_tensor(f"c_row{i}", [1, C], F32)) for i in range(4)]
    m_row, v_row, r_row, mr_row = rows
    V.memset(ones_bf[:, :], 1.0)
    V.memset(ones_row[:, :], 1.0)
    psA = [st.enter_context(nc.psum_tensor(f"s_psA{i}", [128, C], F32)) for i in range(2)]
    psB = [st.enter_context(nc.psum_tensor(f"s_psB{i}", [128, C], F32)) for i in range(2)]
    ps_sum = st.enter_context(nc.psum_tensor("c_ps_sum", [1, C], F32))
    ps_sq = st.enter_context(nc.psum_tensor("c_ps_sq", [1, C], F32))
    ps_r = st.enter_context(nc.psum_tensor("c_ps_r", [128, C], F32))
    ps_mr = st.enter_context(nc.psum_tensor("c_ps_mr", [128, C], F32))

    def compute_block(ch):
        sl_ = ch % 2
        c0 = ch * C
        xb = xn0b_w[sl_]
        x1_w = x1_w2[sl_]
        for j in range(NJ):
            L.dma(xb[:, j * C:(j + 1) * C], XN0B[j * 128:(j + 1) * 128, c0:c0 + C])
        for i in range(NI):
            for j in range(NJ):
                Te.mm(psA[i % 2][:, :], s1(j, i), xb[:, j * C:(j + 1) * C], start=(j == 0), stop=(j == NJ - 1))
            for j in range(NJ):
                Te.mm(psB[i % 2][:, :], s2(j, i), xb[:, j * C:(j + 1) * C], start=(j == 0), stop=(j == NJ - 1))
            A_.act(tmp[:, :], psA[i % 2][:, :], AF.Silu)
            V.tt(g_w[:, i * C:(i + 1) * C], tmp[:, :], psB[i % 2][:, :], ALU.mult)
        for k in range(2):
            for i, t in enumerate(terms):
                L.dma(tin[k][i], t[k * 128:(k + 1) * 128, c0:c0 + C])
        for k in range(NJ):
            r = k % 2
            for i in range(NI):
                Te.mm(psA[k % 2][:, :], s3(i, k), g_w[:, i * C:(i + 1) * C], start=(i == 0), stop=(i == NI - 1))
            xk = x1_w[:, k * C:(k + 1) * C]
            V.tt(xk, tin[r][0], tin[r][1], ALU.add)
            for i in range(2, nterm):
                V.tt(xk, xk, tin[r][i], ALU.add)
            V.tt(xk, xk, psA[k % 2][:, :], ALU.add)
            if k + 2 < NJ:
                for i, t in enumerate(terms):
                    L.dma(tin[(k + 2) % 2][i], t[(k + 2) * 128:(k + 3) * 128, c0:c0 + C])

    def ln_block(ch):
        sl_ = ch % 2
        c0 = ch * C
        x1_w = x1_w2[sl_]
        for k in range(NJ):
            xk = x1_w[:, k * C:(k + 1) * C]
            A_.act(xb16[:, k * C:(k + 1) * C], xk, AF.Copy)
            Q.tt(xsq[:, k * C:(k + 1) * C], xb16[:, k * C:(k + 1) * C], xb16[:, k * C:(k + 1) * C], ALU.mult)
        for k in range(NJ):
            Te.mm(ps_sum[0:1, :], ones_bf[:, 0:1], xb16[:, k * C:(k + 1) * C], start=(k == 0), stop=(k == NJ - 1))
        for k in range(NJ):
            Te.mm(ps_sq[0:1, :], ones_bf[:, 0:1], xsq[:, k * C:(k + 1) * C], start=(k == 0), stop=(k == NJ - 1))
        V.ts(m_row[:], ps_sum[0:1, :], 1.0 / D)
        V.ts(v_row[:], ps_sq[0:1, :], 1.0 / D, LN_EPS, ALU.mult, ALU.add)
        V.tt(mr_row[:], m_row[:], m_row[:], ALU.mult)
        V.tt(v_row[:], v_row[:], mr_row[:], ALU.subtract)
        A_.act(r_row[:], v_row[:], AF.Ln)
        A_.act(r_row[:], r_row[:], AF.Exp, scale=-0.5)
        V.tt(mr_row[:], m_row[:], r_row[:], ALU.mult)
        Te.mm(ps_r[:, :], ones_row[:, :], r_row[:], start=True, stop=True)
        Te.mm(ps_mr[:, :], ones_row[:, :], mr_row[:], start=True, stop=True)
        for k in range(NJ):
            xk = x1_w[:, k * C:(k + 1) * C]
            nk = nk_t[k % 2][:, :]
            V.tt(nk, xk, ps_r[:, :], ALU.mult)
            V.tt(nk, nk, ps_mr[:, :], ALU.subtract)
            G.dma(OUT[k * 128:(k + 1) * 128, c0:c0 + C], xk)
            L.dma(XN1[k * 128:(k + 1) * 128, c0:c0 + C], nk, disjoint=True)
            G.dma(XN1B[k * 128:(k + 1) * 128, c0:c0 + C], nk, disjoint=True)

    for ch in range(NCH + 1):
        if ch < NCH:
            compute_block(ch)
        if ch >= 1:
            ln_block(ch - 1)



def emit_combine_ln(P, engs, st, terms, X1, XN1, XN1B=None):
    nc = P.nc
    V, A_, Te, L, G = engs["V"], engs["A"], engs["T"], engs["L"], engs["G"]
    C = CHUNK
    nterm = len(terms)
    Q = engs["Q"]
    tin_w = [st.enter_context(nc.sbuf_tensor(f"c_tin{r}", [128, nterm * C], F32)) for r in range(2)]
    tin = [[tin_w[r][:, i * C:(i + 1) * C] for i in range(nterm)] for r in range(2)]
    x1_w2 = [st.enter_context(nc.sbuf_tensor(f"c_x1{r}", [128, NJ * C], F32)) for r in range(2)]
    nk_t = [st.enter_context(nc.sbuf_tensor(f"c_nk{r}", [128, C], F32)) for r in range(2)]
    xb16 = st.enter_context(nc.sbuf_tensor("c_xb16", [128, NJ * C], BF16))
    xsq = st.enter_context(nc.sbuf_tensor("c_xsq", [128, NJ * C], BF16))
    ones_bf = st.enter_context(nc.sbuf_tensor("c_ones", [128, 1], BF16))
    ones_row = st.enter_context(nc.sbuf_tensor("c_ones_row", [1, 128], F32))
    ps_sum = st.enter_context(nc.psum_tensor("c_ps_sum", [1, C], F32))
    ps_sq = st.enter_context(nc.psum_tensor("c_ps_sq", [1, C], F32))
    ps_r = st.enter_context(nc.psum_tensor("c_ps_r", [128, C], F32))
    ps_mr = st.enter_context(nc.psum_tensor("c_ps_mr", [128, C], F32))
    rows = [st.enter_context(nc.sbuf_tensor(f"c_row{i}", [1, C], F32)) for i in range(4)]
    m_row, v_row, r_row, mr_row = rows
    V.memset(ones_bf[:, :], 1.0)
    V.memset(ones_row[:, :], 1.0)
    for ch in range(NCH):
        c0 = ch * C
        x1_w, xn1_w = x1_w2[ch % 2], xn1_w2[ch % 2]
        xb16, xsq = xb16_2[ch % 2], xsq_2[ch % 2]
        for j in range(NJ):
            r = j % 4
            for i, t in enumerate(terms):
                L.dma(tin[r][i], t[j * 128:(j + 1) * 128, c0:c0 + C])
            xj = x1_w[:, j * C:(j + 1) * C]
            V.tt(xj, tin[r][0], tin[r][1], ALU.add)
            for i in range(2, nterm):
                V.tt(xj, xj, tin[r][i], ALU.add)
            A_.act(xb16[:, j * C:(j + 1) * C], xj, AF.Copy)
            Q.tt(xsq[:, j * C:(j + 1) * C], xb16[:, j * C:(j + 1) * C], xb16[:, j * C:(j + 1) * C], ALU.mult)
        for j in range(NJ):
            Te.mm(ps_sum[0:1, :], ones_bf[:, 0:1], xb16[:, j * C:(j + 1) * C], start=(j == 0), stop=(j == NJ - 1))
        for j in range(NJ):
            Te.mm(ps_sq[0:1, :], ones_bf[:, 0:1], xsq[:, j * C:(j + 1) * C], start=(j == 0), stop=(j == NJ - 1))
        V.ts(m_row[:], ps_sum[0:1, :], 1.0 / D)
        V.ts(v_row[:], ps_sq[0:1, :], 1.0 / D, LN_EPS, ALU.mult, ALU.add)
        V.tt(mr_row[:], m_row[:], m_row[:], ALU.mult)
        V.tt(v_row[:], v_row[:], mr_row[:], ALU.subtract)
        A_.act(r_row[:], v_row[:], AF.Ln)
        A_.act(r_row[:], r_row[:], AF.Exp, scale=-0.5)
        V.tt(mr_row[:], m_row[:], r_row[:], ALU.mult)
        Te.mm(ps_r[:, :], ones_row[:, :], r_row[:], start=True, stop=True)
        Te.mm(ps_mr[:, :], ones_row[:, :], mr_row[:], start=True, stop=True)
        for j in range(NJ):
            xj = x1_w[:, j * C:(j + 1) * C]
            nj = xn1_w[:, j * C:(j + 1) * C]
            V.tt(nj, xj, ps_r[:, :], ALU.mult)
            V.tt(nj, nj, ps_mr[:, :], ALU.subtract)
            G.dma(X1[j * 128:(j + 1) * 128, c0:c0 + C], xj, disjoint=True)
            L.dma(XN1[j * 128:(j + 1) * 128, c0:c0 + C], nj, disjoint=True)
            if XN1B is not None:
                G.dma(XN1B[j * 128:(j + 1) * 128, c0:c0 + C], nj, disjoint=True)



def emit_rwkv(P, engs, st, hin, XN1, FFN1):
    nc = P.nc
    V, A_, Te, L, G = engs["V"], engs["A"], engs["T"], engs["L"], engs["G"]
    C = CHUNK
    _, wr = load_lhsT_weight(P, G, st, "r_wr", hin["r_wrt"], D, D)
    _, wk = load_lhsT_weight(P, G, st, "r_wk", hin["r_wkt"], D, D)
    _, wv = load_lhsT_weight(P, G, st, "r_wv", hin["r_wvt"], D, D)
    mur = load_f32_tile(P, L, st, "r_mur_sb", hin["r_mur"], (128, NJ))
    muk = load_f32_tile(P, L, st, "r_muk_sb", hin["r_muk"], (128, NJ))
    omr = st.enter_context(nc.sbuf_tensor("r_omr", [128, NJ], F32))
    omk = st.enter_context(nc.sbuf_tensor("r_omk", [128, NJ], F32))
    V.ts(omr[:, :], mur[:, :], -1.0, 1.0, ALU.mult, ALU.add)
    V.ts(omk[:, :], muk[:, :], -1.0, 1.0, ALU.mult, ALU.add)
    CE = C + 1
    Q = engs["Q"]
    xe_w = [st.enter_context(nc.sbuf_tensor(f"r_xe{r}", [128, NJ * CE], F32)) for r in range(2)]
    xr_w2 = [st.enter_context(nc.sbuf_tensor(f"r_xr{r}", [128, NJ * C], BF16)) for r in range(2)]
    xk_w2 = [st.enter_context(nc.sbuf_tensor(f"r_xk{r}", [128, NJ * C], BF16)) for r in range(2)]
    ka_w = st.enter_context(nc.sbuf_tensor("r_ka", [128, NJ * C], BF16))
    r_w = st.enter_context(nc.sbuf_tensor("r_r", [128, NJ * C], F32))
    tmp = st.enter_context(nc.sbuf_tensor("r_tmp", [128, C], F32))
    tmq = st.enter_context(nc.sbuf_tensor("r_tmq", [128, C], F32))
    tmv = st.enter_context(nc.sbuf_tensor("r_tmv", [128, C], F32))
    fo = [st.enter_context(nc.sbuf_tensor(f"r_fo{i}", [128, C], F32)) for i in range(2)]
    psR = [st.enter_context(nc.psum_tensor(f"r_psR{i}", [128, C], F32)) for i in range(3)]
    psK = [st.enter_context(nc.psum_tensor(f"r_psK{i}", [128, C], F32)) for i in range(3)]

    def load_blend(s, ch):
        sl_ = ch % 2
        c0 = s * T + ch * C
        xe = xe_w[sl_]
        for j in range(NJ):
            xj = xe[:, j * CE:(j + 1) * CE]
            if ch == 0:
                V.memset(xj[:, 0:1], 0.0)
                L.dma(xj[:, 1:CE], XN1[j * 128:(j + 1) * 128, c0:c0 + C])
            else:
                L.dma(xj, XN1[j * 128:(j + 1) * 128, c0 - 1:c0 + C])
        for j in range(NJ):
            xj = xe[:, j * CE:(j + 1) * CE]
            V.ts(tmp[:, :], xj[:, 0:C], omr[:, j:j + 1])
            V.stt(xr_w2[sl_][:, j * C:(j + 1) * C], xj[:, 1:CE], mur[:, j:j + 1], tmp[:, :], ALU.mult, ALU.add)
            V.ts(tmq[:, :], xj[:, 0:C], omk[:, j:j + 1])
            V.stt(xk_w2[sl_][:, j * C:(j + 1) * C], xj[:, 1:CE], muk[:, j:j + 1], tmq[:, :], ALU.mult, ALU.add)

    load_blend(0, 0)
    for s in range(SEQ_PER_CORE):
        for ch in range(T // C):
            sl_ = ch % 2
            c0 = s * T + ch * C
            xr_w, xk_w = xr_w2[sl_], xk_w2[sl_]
            for k in range(NJ):
                for j in range(NJ):
                    Te.mm(psR[k % 3][:, :], wr(j, k), xr_w[:, j * C:(j + 1) * C], start=(j == 0), stop=(j == NJ - 1))
                A_.act(r_w[:, k * C:(k + 1) * C], psR[k % 3][:, :], AF.Sigmoid)
                for j in range(NJ):
                    Te.mm(psK[k % 3][:, :], wk(j, k), xk_w[:, j * C:(j + 1) * C], start=(j == 0), stop=(j == NJ - 1))
                V.ts(tmv[:, :], psK[k % 3][:, :], 0.0, None, ALU.max)
                A_.act(ka_w[:, k * C:(k + 1) * C], tmv[:, :], AF.Square)
            if ch + 1 < T // C:
                load_blend(s, ch + 1)
            elif s + 1 < SEQ_PER_CORE:
                load_blend(s + 1, 0)
            for k in range(NJ):
                for j in range(NJ):
                    Te.mm(psR[k % 3][:, :], wv(j, k), ka_w[:, j * C:(j + 1) * C], start=(j == 0), stop=(j == NJ - 1))
                V.tt(fo[k % 2][:, :], psR[k % 3][:, :], r_w[:, k * C:(k + 1) * C], ALU.mult)
                G.dma(FFN1[k * 128:(k + 1) * 128, c0:c0 + C], fo[k % 2][:, :], accum=True)



def emit_performer(P, engs, st, hin, XN1, ATT):
    nc = P.nc
    V, A_, Te, L, G = engs["V"], engs["A"], engs["T"], engs["L"], engs["G"]
    Q = engs["Q"]
    NT = T // 128
    _, wq = load_lhsT_weight(P, G, st, "a_wq_sb", hin["a_wq"], D, D)
    _, wk = load_lhsT_weight(P, G, st, "a_wk_sb", hin["a_wk"], D, D)
    _, wv = load_lhsT_weight(P, G, st, "a_wv_sb", hin["a_wv"], D, D)
    _, wo = load_lhsT_weight(P, G, st, "a_wo_sb", hin["a_wo"], D, D)
    bo = load_f32_tile(P, L, st, "a_bo_sb", hin["a_bo"], (128, NJ))
    projf = load_f32_tile(P, L, st, "a_projf", hin["a_projt"], (DH, NBF))
    projb = st.enter_context(nc.sbuf_tensor("a_projb", [DH, NBF], BF16))
    V.ts(projb[:, :], projf[:, :], PERF_DN)
    identb_np = np.eye(128, dtype=ml_dtypes.bfloat16)
    identb = st.enter_context(nc.sbuf_tensor("a_identb_sb", [128, 128], BF16))
    L.dma(identb[:, :], nc.inline_tensor(identb_np, "a_identb").ap()[:, :])
    identf_np = np.eye(128, dtype=np.float32)
    identf = st.enter_context(nc.sbuf_tensor("a_identf_sb", [128, 128], F32))
    L.dma(identf[:, :], nc.inline_tensor(identf_np, "a_identf").ap()[:, :])
    ones_row = st.enter_context(nc.sbuf_tensor("a_ones_row", [1, 128], F32))
    V.memset(ones_row[:, :], 1.0)
    ones_colb = st.enter_context(nc.sbuf_tensor("a_ones_colb", [128, 1], BF16))
    V.memset(ones_colb[:, :], 1.0)
    ones_rowb = st.enter_context(nc.sbuf_tensor("a_ones_rowb", [1, 128], BF16))
    V.memset(ones_rowb[:, :], 1.0)
    vsum_sb = [st.enter_context(nc.sbuf_tensor(f"a_vsum{r}", [1, DH + 1], BF16)) for r in range(2)]
    csum_sb = st.enter_context(nc.sbuf_tensor("a_csum", [1, DH + 1], BF16))
    colk_all = [st.enter_context(nc.sbuf_tensor(f"a_colk{r}", [128, NT], F32)) for r in range(2)]

    xn1b = st.enter_context(nc.sbuf_tensor("a_xn1b", [128, NJ * T], BF16))
    attn_fm = st.enter_context(nc.sbuf_tensor("a_attn_fm", [128, NJ * T], BF16))
    q_b = [st.enter_context(nc.sbuf_tensor(f"a_q_b{r}", [DH, T], BF16)) for r in range(2)]
    k_b = [st.enter_context(nc.sbuf_tensor(f"a_k_b{r}", [DH, T], BF16)) for r in range(2)]
    v_f = [st.enter_context(nc.sbuf_tensor(f"a_v_f{r}", [DH, T], BF16)) for r in range(2)]
    qsq = [st.enter_context(nc.sbuf_tensor(f"a_qsq{r}", [DH, T], BF16)) for r in range(2)]
    ksq = [st.enter_context(nc.sbuf_tensor(f"a_ksq{r}", [DH, T], BF16)) for r in range(2)]
    v_w = [st.enter_context(nc.sbuf_tensor(f"a_v_w{r}", [128, NT * (DH + 1)], BF16)) for r in range(2)]
    V.memset(v_w[0][:, :], 1.0)
    V.memset(v_w[1][:, :], 1.0)
    qp_fm = st.enter_context(nc.sbuf_tensor("a_qp_fm", [128, 3 * T], BF16))
    diag_q = [st.enter_context(nc.sbuf_tensor(f"a_diag_q{r}", [128, NT], F32)) for r in range(2)]
    diag_k = [st.enter_context(nc.sbuf_tensor(f"a_diag_k{r}", [128, NT], F32)) for r in range(2)]
    kmax_c = st.enter_context(nc.sbuf_tensor("a_kmax", [128, NT], F32))
    col_a = st.enter_context(nc.sbuf_tensor("a_col_a", [128, 1], F32))
    col_b = [st.enter_context(nc.sbuf_tensor(f"a_col_b{r}", [128, 1], F32)) for r in range(2)]
    stabk = st.enter_context(nc.sbuf_tensor("a_stabk", [128, 1], F32))
    row1 = st.enter_context(nc.sbuf_tensor("a_row1", [1, 128], F32))
    sc11 = st.enter_context(nc.sbuf_tensor("a_sc11", [1, 1], F32))
    e_t = [st.enter_context(nc.sbuf_tensor(f"a_e_t{r}", [128, NBF], F32)) for r in range(2)]
    qkp_b = [st.enter_context(nc.sbuf_tensor(f"a_qkp{r}", [128, NBF], BF16)) for r in range(2)]
    ctx_b = st.enter_context(nc.sbuf_tensor("a_ctx", [128, 3 * (DH + 1)], BF16))
    kp_all = st.enter_context(nc.sbuf_tensor("a_kp_all", [128, NT * NBF], BF16))
    ho_t = st.enter_context(nc.sbuf_tensor("a_ho", [128, DH], BF16))
    ao = [st.enter_context(nc.sbuf_tensor(f"a_ao{i}", [128, CHUNK], F32)) for i in range(2)]

    psA = st.enter_context(nc.psum_tensor("a_psA", [128, CHUNK], F32))
    psB = st.enter_context(nc.psum_tensor("a_psB", [128, CHUNK], F32))
    psT_bf = st.enter_context(nc.psum_tensor("a_psT_bf", [128, 3 * 128], BF16))
    psD = [st.enter_context(nc.psum_tensor(f"a_psD{i}", [128, NBF], F32)) for i in range(2)]
    psCtx = st.enter_context(nc.psum_tensor("a_psCtx", [128, 3 * (DH + 1)], F32))
    psO = [st.enter_context(nc.psum_tensor(f"a_psO{i}", [128, DH + 1], F32)) for i in range(2)]
    MBW = [128, 128, NBF - 256]

    def emit_qkv(s, h):
        r = h % 2
        for cc in range(T // CHUNK):
            ps = psA if cc % 2 == 0 else psB
            csl = slice(cc * CHUNK, (cc + 1) * CHUNK)
            for j in range(NJ):
                Te.mm(ps[0:DH, :], wq(j, h, DH), xn1b[:, j * T + cc * CHUNK: j * T + (cc + 1) * CHUNK], start=(j == 0), stop=(j == NJ - 1))
            A_.act(q_b[r][:, csl], ps[0:DH, :], AF.Copy)
            Q.tt(qsq[r][:, csl], q_b[r][:, csl], q_b[r][:, csl], ALU.mult)
            for j in range(NJ):
                Te.mm(ps[0:DH, :], wk(j, h, DH), xn1b[:, j * T + cc * CHUNK: j * T + (cc + 1) * CHUNK], start=(j == 0), stop=(j == NJ - 1))
            A_.act(k_b[r][:, csl], ps[0:DH, :], AF.Copy)
            Q.tt(ksq[r][:, csl], k_b[r][:, csl], k_b[r][:, csl], ALU.mult)
            for j in range(NJ):
                Te.mm(ps[0:DH, :], wv(j, h, DH), xn1b[:, j * T + cc * CHUNK: j * T + (cc + 1) * CHUNK], start=(j == 0), stop=(j == NJ - 1))
            A_.act(v_f[r][:, csl], ps[0:DH, :], AF.Copy)
            yield
        for ti in range(NT):
            reg = (ti % 3) * 128
            Te.transpose(psT_bf[:, reg:reg + DH], v_f[r][:, ti * 128:(ti + 1) * 128], identb[0:DH, 0:DH])
            V.copy(v_w[r][:, ti * (DH + 1): ti * (DH + 1) + DH], psT_bf[:, reg:reg + DH])
            if ti % 4 == 3:
                yield
        for ti in range(NT):
            Te.mm(psA[:, ti:ti + 1], qsq[r][:, ti * 128:(ti + 1) * 128], ones_colb[0:DH, 0:1], start=True, stop=True)
        V.ts(diag_q[r][:, :], psA[:, 0:NT], 0.5 * PERF_DN * PERF_DN)
        yield
        for ti in range(NT):
            Te.mm(psB[:, ti:ti + 1], ksq[r][:, ti * 128:(ti + 1) * 128], ones_colb[0:DH, 0:1], start=True, stop=True)
        V.ts(diag_k[r][:, :], psB[:, 0:NT], 0.5 * PERF_DN * PERF_DN)
        yield
        for ti in range(NT):
            Te.mm(psO[0][0:1, :], ones_colb[:, 0:1], v_w[r][:, ti * (DH + 1):(ti + 1) * (DH + 1)], start=(ti == 0), stop=(ti == NT - 1))
        V.ts(vsum_sb[r][:, :], psO[0][0:1, :], PERF_EPS)
        yield

    def emit_post(s, h):
        r = h % 2
        for ti in range(NT):
            Te.mm(psD[ti % 2][:, :], k_b[r][:, ti * 128:(ti + 1) * 128], projb[:, :], start=True, stop=True)
            V.reduce(kmax_c[:, ti:ti + 1], psD[ti % 2][:, :], ALU.max)
            if ti % 4 == 3:
                yield
        V.reduce(col_a[:, 0:1], kmax_c[:, :], ALU.max)
        Te.transpose(psD[0][0:1, 0:128], col_a[:, 0:1], identf[:, :])
        V.copy(row1[:, :], psD[0][0:1, 0:128])
        V.reduce(sc11[0:1, 0:1], row1[:, :], ALU.max)
        Te.mm(psD[0][:, 0:1], ones_row[:, :], sc11[0:1, 0:1], start=True, stop=True)
        V.ts(stabk[:, 0:1], psD[0][:, 0:1], -1.0)
        V.ts(colk_all[r][:, :], diag_k[r][:, :], stabk[:, 0:1], -1.0, ALU.subtract, ALU.mult)
        yield
        for ti in range(NT):
            rr = ti % 2
            Te.mm(psD[rr][:, :], k_b[r][:, ti * 128:(ti + 1) * 128], projb[:, :], start=True, stop=True)
            A_.act(kp_all[:, ti * NBF:(ti + 1) * NBF], psD[rr][:, :], AF.Exp, bias=colk_all[r][:, ti:ti + 1])
            if ti % 4 == 3:
                yield
        for mb in range(3):
            for ti in range(NT):
                Te.mm(psCtx[0:MBW[mb], mb * (DH + 1):(mb + 1) * (DH + 1)],
                      kp_all[:, ti * NBF + mb * 128: ti * NBF + mb * 128 + MBW[mb]],
                      v_w[r][:, ti * (DH + 1): (ti + 1) * (DH + 1)],
                      start=(ti == 0), stop=False)
            Te.mm(psCtx[0:MBW[mb], mb * (DH + 1):(mb + 1) * (DH + 1)],
                  ones_rowb[0:1, 0:MBW[mb]], vsum_sb[r][:, :], start=False, stop=True)
        for mb in range(3):
            A_.act(ctx_b[0:MBW[mb], mb * (DH + 1):(mb + 1) * (DH + 1)], psCtx[0:MBW[mb], mb * (DH + 1):(mb + 1) * (DH + 1)], AF.Copy)
        for mb in range(3):
            Te.mm(psD[0][0:1, 0:DH + 1], ones_colb[0:MBW[mb], 0:1], ctx_b[0:MBW[mb], mb * (DH + 1):(mb + 1) * (DH + 1)], start=(mb == 0), stop=(mb == 2))
        V.ts(csum_sb[:, :], psD[0][0:1, 0:DH + 1], PERF_EPS)
        yield
        Te.mm(psD[0][:, :], q_b[r][:, 0:128], projb[:, :], start=True, stop=True)
        for ti in range(NT):
            rr = ti % 2
            V.reduce(col_a[:, 0:1], psD[rr][:, :], ALU.max, negate=True)
            V.tt(col_b[rr][:, 0:1], col_a[:, 0:1], diag_q[r][:, ti:ti + 1], ALU.subtract)
            A_.act(qkp_b[rr][:, :], psD[rr][:, :], AF.Exp, bias=col_b[rr][:, 0:1])
            if ti + 1 < NT:
                Te.mm(psD[(ti + 1) % 2][:, :], q_b[r][:, (ti + 1) * 128:(ti + 2) * 128], projb[:, :], start=True, stop=True)
            yield
            for mb in range(3):
                reg = mb * 128
                Te.transpose(psT_bf[0:MBW[mb], reg:reg + 128], qkp_b[rr][:, mb * 128: mb * 128 + MBW[mb]], identb[:, :])
                A_.act(qp_fm[0:MBW[mb], mb * T + ti * 128: mb * T + (ti + 1) * 128], psT_bf[0:MBW[mb], reg:reg + 128], AF.Copy)
        def psO_group(ti):
            rr = ti % 2
            for mb in range(3):
                Te.mm(psO[rr][:, :], qp_fm[0:MBW[mb], mb * T + ti * 128: mb * T + (ti + 1) * 128],
                      ctx_b[0:MBW[mb], mb * (DH + 1):(mb + 1) * (DH + 1)], start=(mb == 0), stop=False)
            Te.mm(psO[rr][:, :], ones_rowb[:, :], csum_sb[:, :], start=False, stop=True)
        psO_group(0)
        ho2 = [ho_t[:, :], ho_t[:, :]]
        for ti in range(NT):
            rr = ti % 2
            if ti + 1 < NT:
                psO_group(ti + 1)
            V.recip(col_a[:, 0:1], psO[rr][:, DH:DH + 1])
            V.ts(ho_t[:, :], psO[rr][:, 0:DH], col_a[:, 0:1])
            reg = (ti % 3) * 128
            Te.transpose(psT_bf[0:DH, reg:reg + 128], ho_t[:, :], identb[:, :])
            V.copy(attn_fm[(h % 2) * DH:(h % 2) * DH + DH, (h // 2) * T + ti * 128:(h // 2) * T + (ti + 1) * 128], psT_bf[0:DH, reg:reg + 128])
            if ti % 4 == 3:
                yield

    for s in range(SEQ_PER_CORE):
        c0s = s * T
        for j in range(NJ):
            L.dma(xn1b[:, j * T:(j + 1) * T], XN1[j * 128:(j + 1) * 128, c0s:c0s + T])
        for _ in emit_qkv(s, 0):
            pass
        for h in range(NHEADS):
            gens = []
            if h + 1 < NHEADS:
                gens.append(emit_qkv(s, h + 1))
            gens.append(emit_post(s, h))
            live = list(gens)
            while live:
                for g in list(live):
                    try:
                        next(g)
                    except StopIteration:
                        live.remove(g)
        for cc in range(T // CHUNK):
            for k in range(NJ):
                ps = psA if k % 2 == 0 else psB
                for j in range(NJ):
                    Te.mm(ps[:, :], wo(j, k), attn_fm[:, j * T + cc * CHUNK: j * T + (cc + 1) * CHUNK], start=(j == 0), stop=(j == NJ - 1))
                A_.act(ao[k % 2][:, :], ps[:, :], AF.Identity, bias=bo[:, k:k + 1])
                G.dma(ATT[k * 128:(k + 1) * 128, c0s + cc * CHUNK: c0s + (cc + 1) * CHUNK], ao[k % 2][:, :], accum=True)


def emit_final(P, engs, st, terms, OUT):
    nc = P.nc
    V, A_, Te, L, G = engs["V"], engs["A"], engs["T"], engs["L"], engs["G"]
    C = CHUNK
    nterm = len(terms)
    tin_w = [st.enter_context(nc.sbuf_tensor(f"o_tin{r}", [128, 3 * C], F32)) for r in range(2)]
    tin = [[tin_w[r][:, i * C:(i + 1) * C] for i in range(nterm)] for r in range(2)]
    po = [st.enter_context(nc.sbuf_tensor(f"o_po{r}", [128, C], F32)) for r in range(2)]
    for ch in range(NCH):
        c0 = ch * C
        for j in range(NJ):
            r = j % 2
            for i, t in enumerate(terms):
                L.dma(tin[r][i], t[j * 128:(j + 1) * 128, c0:c0 + C])
            V.tt(po[r][:], tin[r][0], tin[r][1], ALU.add)
            for i in range(2, nterm):
                V.tt(po[r][:], po[r][:], tin[r][i], ALU.add)
            G.dma(OUT[j * 128:(j + 1) * 128, c0:c0 + C], po[r][:])



def build_program(debug=False):
    nc = bass.Bass("TRN2", target_bir_lowering=False)
    stack = ExitStack()
    P = Prog(nc, stack)
    engs = make_engs(P)

    x0f = nc.dram_tensor("x0f", [D, NTOK], F32, kind="ExternalInput").ap()
    hin = {}
    for name, shape in [
        ("m_w1t", [D, D]), ("m_w2t", [D, D]), ("m_wlt", [D, D]), ("m_dmt", [D, D]),
        ("m_dwt", [D, H]), ("m_bwt", [D, H]), ("m_cwt", [H, D]),
        ("m_normw", [128, NJ]), ("m_convw", [128, NJ * 3]), ("m_convb", [128, NJ]),
        ("m_A", [H, 1]), ("m_db", [H, 1]), ("m_Bb", [H, 1]),
        ("m_Cb", [128, NJ]), ("m_Db", [128, NJ]),
    ]:
        hin[name] = nc.dram_tensor(name, shape, F32, kind="ExternalInput").ap()
    for name, shape in [
        ("f_gwt", [D, E]), ("f_w1t", [E, D, D]), ("f_w3t", [E, D, D]), ("f_w2t", [E, D, D]),
        ("f_b1", [128, E * NJ]), ("f_b2", [128, E * NJ]), ("f_b3", [128, E * NJ]),
        ("f_sw1t", [D, I2]), ("f_sw2t", [D, I2]), ("f_sw3t", [I2, D]),
        ("a_wq", [D, D]), ("a_wk", [D, D]), ("a_wv", [D, D]), ("a_wo", [D, D]),
        ("a_bo", [128, NJ]), ("a_projt", [DH, NBF]),
        ("r_wrt", [D, D]), ("r_wkt", [D, D]), ("r_wvt", [D, D]),
        ("r_mur", [128, NJ]), ("r_muk", [128, NJ]),
    ]:
        hin[name] = nc.dram_tensor(name, shape, F32, kind="ExternalInput").ap()
    out_fm = nc.dram_tensor("out_fm", [D, NTOK], F32, kind="ExternalOutput").ap()
    XN0 = dram_io(nc, "XN0", [D, NTOK], F32, debug)
    XN0B = nc.dram_tensor("XN0B", [D, NTOK], BF16, kind="Internal").ap()
    ACC0 = dram_io(nc, "ACC0", [D, NTOK], F32, debug)
    XN1 = dram_io(nc, "XN1", [D, NTOK], F32, debug)
    XN1B = nc.dram_tensor("XN1B", [D, NTOK], BF16, kind="Internal").ap()

    with stack:
        with ExitStack() as st_g:
            sm_all = st_g.enter_context(nc.sbuf_tensor("sm_all", [8, NTOK], F32))
            with ExitStack() as st:
                emit_ln(P, engs, st, x0f, XN0, dst_b=XN0B, gw=hin["f_gwt"], sm_all=sm_all)
            P.stage_barrier()
            with ExitStack() as st:
                emit_mamba(P, engs, st, hin, XN0B, ACC0, dbg={} if debug else None)
            P.stage_barrier()
            with ExitStack() as st:
                emit_moe(P, engs, st, hin, sm_all, XN0B, ACC0, dbg={} if debug else None)
        P.stage_barrier()
        with ExitStack() as st:
            emit_shared_ln(P, engs, st, hin, XN0B, [x0f, XN0, ACC0], out_fm, XN1, XN1B)
        P.stage_barrier()
        with ExitStack() as st:
            emit_rwkv(P, engs, st, hin, XN1, out_fm)
        P.stage_barrier()
        with ExitStack() as st:
            emit_performer(P, engs, st, hin, XN1B, out_fm)
        P.stage_barrier()

        P.stage_barrier()
    return nc


def kernel(**inputs):
    maps = stage_inputs(inputs)
    nc = build_program()
    res = run_bass_kernel_spmd(nc, maps, core_ids=list(range(NCORES)))
    return assemble_output(res.results)
